# version 14
# baseline (speedup 1.0000x reference)
import contextlib
import numpy as np
import concourse.bass as bass
import concourse.mybir as mybir
from concourse.bass_utils import run_bass_kernel_spmd

F32 = mybir.dt.float32
BF16 = mybir.dt.bfloat16
I32 = mybir.dt.int32
AF = mybir.ActivationFunctionType
ALU = mybir.AluOpType

T = 4096
D = 1024
NH = 16
HD = 64
FF = 4096
NE = 8
NC_ = 8
NCH = T // 128
EPS1 = 1e-5
EPS2 = 1e-6

ENGINES = ("sync", "scalar", "vector", "tensor", "gpsimd")
DEBUG_C = False
EXPOSE = []


class Tick:
    __slots__ = ("sem", "val")

    def __init__(self, sem, val):
        self.sem = sem
        self.val = val


class Rec:

    def __init__(self, nc):
        self.nc = nc
        self.es = contextlib.ExitStack()
        self.ops = None
        self.sem = {}
        self.semval = {}
        for eng in ENGINES:
            h = self.es.enter_context(nc.semaphore(f"prog_{eng}"))
            self.sem[eng] = h
            self.semval[eng] = 0
        self.dsem = {}
        self.dsemval = {}
        for q in ("sync", "scalar", "gpsimd", "vector"):
            h = self.es.enter_context(nc.semaphore(f"dma_{q}"))
            self.dsem[q] = h
            self.dsemval[q] = 0
        self.waited = {eng: {} for eng in ENGINES}
        self.named_dsems = {}

    def dma_sem(self, name):
        if name not in self.named_dsems:
            h = self.es.enter_context(self.nc.semaphore(f"dsem_{name}"))
            self.named_dsems[name] = [h, 0]
        return name

    def phase(self, name):
        return _Phase(self, name)

    def _add_waits(self, eng, deps):
        fns = []
        seen = self.waited[eng]
        best = {}
        for t in deps:
            if t is None:
                continue
            key = id(t.sem)
            if t.val > best.get(key, (0, None))[0]:
                best[key] = (t.val, t.sem)
        for key, (val, sem) in best.items():
            if seen.get(key, 0) >= val:
                continue
            seen[key] = val
            fns.append((sem, val))
        return fns

    def op(self, eng, fn, deps=(), inc=True):
        waits = self._add_waits(eng, deps)
        tick = None
        if inc:
            self.semval[eng] += 1
            tick = Tick(self.sem[eng], self.semval[eng])

        sem = self.sem[eng]

        def run(e, waits=waits, fn=fn, tick=tick, sem=sem):
            for (s, v) in waits:
                e.wait_ge(s, v)
            inst = fn(e)
            if tick is not None:
                inst.then_inc(sem, 1)

        self.ops[eng].append(run)
        return tick

    def raw(self, eng, fn, deps=()):
        return self.op(eng, fn, deps, inc=False)

    def dma(self, q, out, in_, deps=(), sem=None, **kw):
        waits = self._add_waits(q, deps)
        if sem is None:
            self.dsemval[q] += 16
            tick = Tick(self.dsem[q], self.dsemval[q])
            sem = self.dsem[q]
        else:
            rec = self.named_dsems[sem]
            rec[1] += 16
            tick = Tick(rec[0], rec[1])
            sem = rec[0]

        def run(e, waits=waits, out=out, in_=in_, kw=kw, sem=sem):
            for (s, v) in waits:
                e.wait_ge(s, v)
            e.dma_start(out, in_, **kw).then_inc(sem, 16)

        self.ops[q].append(run)
        return tick

    def drain_ticks(self):
        ticks = [Tick(self.dsem[q], self.dsemval[q]) for q in self.dsem]
        ticks += [Tick(h, v) for (h, v) in self.named_dsems.values()]
        return ticks

    def wait(self, eng, deps):
        waits = self._add_waits(eng, deps)
        if not waits:
            return

        def run(e, waits=waits):
            for (s, v) in waits:
                e.wait_ge(s, v)

        self.ops[eng].append(run)


class _Phase:
    def __init__(self, rec, name):
        self.rec = rec
        self.name = name

    def __enter__(self):
        self.rec.ops = {eng: [] for eng in ENGINES}
        return self.rec

    def __exit__(self, et, ev, tb):
        if et is not None:
            return False
        rec = self.rec
        nc = rec.nc
        with nc.Block(self.name) as blk:
            for eng in ENGINES:
                lst = rec.ops[eng]
                if not lst:
                    continue

                def body(e, lst=lst):
                    for fn in lst:
                        fn(e)

                getattr(blk, eng)(body)
        rec.ops = None
        return False



def bcast_row(R, sb_tile, dram_ap, n, deps=(), sem=None):
    src = bass.AP(dram_ap.tensor, dram_ap.offset, [[0, 128], [1, n]])
    return R.dma("sync", sb_tile[:, :n], src, deps=deps, sem=sem)


def col_tile_load(R, sb_tile, dram_vec, nchunk, deps=(), sem=None):
    ticks = []
    for c in range(nchunk):
        ticks.append(R.dma("sync", sb_tile[:, c:c + 1],
                           dram_vec[c * 128:(c + 1) * 128].rearrange("(p o) -> p o", o=1),
                           deps=deps, sem=sem))
    return ticks


def build_identity(R, ident_f32, ident_bf16, io_f, io_p):
    t1 = R.op("gpsimd", lambda e: e.iota(io_f[:, :], [[1, 128]], channel_multiplier=0,
                                         allow_small_or_imprecise_dtypes=True))
    t2 = R.op("gpsimd", lambda e: e.iota(io_p[:, :], [[0, 128]], channel_multiplier=1,
                                         allow_small_or_imprecise_dtypes=True))
    t3 = R.op("vector", lambda e: e.tensor_tensor(ident_f32[:, :], io_f[:, :], io_p[:, :],
                                                  ALU.is_equal), deps=[t1, t2])
    t4 = R.op("vector", lambda e: e.tensor_copy(ident_bf16[:, :], ident_f32[:, :]), deps=[t3])
    return t3, t4



SCRATCH = {
    "xn0_s": ([T, D], F32, "A"),
    "qT_s": ([D, T], BF16, "A"),
    "kT_s": ([D, T], BF16, "A"),
    "v2_s": ([NH, 128, NCH * 65], BF16, "A"),
    "woT_s": ([128, 8 * 1024], BF16, "A"),
    "w1g_s": ([8, 128, 32 * 128], BF16, "A"),
    "w2g_s": ([8, 128, 32 * 128], BF16, "A"),
    "wzbf_s": ([D, D], BF16, "A"),
    "whbf_s": ([D, D], BF16, "A"),
    "m1g_s": ([NE, 8, 128, 4 * 8 * 128], BF16, "A"),
    "m2g_s": ([NE, 8, 128, 32 * 128], BF16, "A"),
    "ln3_s": ([128, 64], F32, "A"),
    "xln3T_s": ([D, T], BF16, "A"),
    "z05_s": ([8, 128, 8 * 512], BF16, "B"),
    "rrow_s": ([2, 512], F32, "B"),
    "attnT_s": ([D, T], BF16, "B"),
    "x1_s": ([T, D], F32, "C"),
    "xn1T_s": ([D, T], BF16, "D"),
    "s1_s": ([T, D], BF16, "D"),
    "w_rows_s": ([NE, T], BF16, "D"),
    "y_s": ([T, D], F32, "D"),
    "dbg1_s": ([T, D], F32, "C"),
    "dbg2_s": ([T, D], F32, "C"),
    "dbgy_s": ([128, 32 * 512], BF16, "C"),
    "dbgz_s": ([128, 8 * 512], BF16, "C"),
    "dbgxl_s": ([128, 8 * 512], BF16, "C"),
    "dbgz2_s": ([128, 8 * 512], F32, "C"),
}

INPUTS = {
    "x": ([T, D], F32),
    "ln0_g": ([D], F32), "ln0_b": ([D], F32),
    "wq": ([D, D], F32), "bq": ([D], F32),
    "wk": ([D, D], F32), "bk": ([D], F32),
    "wv": ([D, D], F32), "bv": ([D], F32),
    "wo": ([D, D], F32), "bo": ([D], F32),
    "mu": ([1], F32),
    "lnf_g": ([D], F32), "lnf_b": ([D], F32),
    "fc1_q": ([D, FF], I32), "fc1_s": ([8, FF], F32), "fc1_b": ([FF], F32),
    "fc2_q": ([FF, D], I32), "fc2_s": ([32, D], F32), "fc2_b": ([D], F32),
    "wz": ([D, D], F32), "bz": ([D], F32),
    "wh": ([D, D], F32), "bh": ([D], F32),
    "sw_beta": ([1], F32),
    "lng_g": ([D], F32), "lng_b": ([D], F32),
    "mw1": ([NE, D, FF], F32), "mb1": ([NE, FF], F32),
    "mw2": ([NE, FF, D], F32), "mb2": ([NE, D], F32),
}

PHASE_INPUTS = {
    "A": ["x", "ln0_g", "ln0_b", "wq", "bq", "wk", "bk", "wv", "bv", "wo",
          "fc1_q", "fc1_s", "fc2_q", "fc2_s", "wz", "wh", "mw1", "mw2", "lnf_g", "lnf_b"],
    "B": ["mu", "fc1_b", "fc2_b"],
    "C": ["x", "bo"],
    "D": ["bz", "bh", "sw_beta", "lng_g", "lng_b"],
    "E": ["mb1", "mb2"],
}
PHASE_SCRATCH_R = {
    "A": [],
    "B": ["qT_s", "kT_s", "v2_s", "xln3T_s", "w1g_s", "w2g_s"],
    "C": ["xn0_s", "attnT_s", "woT_s", "z05_s"],
    "D": ["x1_s", "wzbf_s", "whbf_s"],
    "E": ["x1_s", "xn1T_s", "s1_s", "w_rows_s", "m1g_s", "m2g_s"],
}
OUTPUT = {"out": ([T, D], F32)}


def declare_io(nc, phases, fused):
    io = {}
    ins = set()
    for p in phases:
        ins.update(PHASE_INPUTS[p])
    for name in sorted(ins):
        shape, dt = INPUTS[name]
        io[name] = nc.dram_tensor(name, shape, dt, kind="ExternalInput").ap()
    for name, (shape, dt, wr) in SCRATCH.items():
        if name.startswith("dbg") and not DEBUG_C:
            continue
        if fused:
            kind = "ExternalOutput" if name in EXPOSE else "Internal"
            used = wr in phases or any(name in PHASE_SCRATCH_R[p] for p in phases)
            if not used:
                continue
        else:
            r = any(name in PHASE_SCRATCH_R[p] for p in phases)
            w = wr in phases
            if w:
                kind = "ExternalOutput"
            elif r:
                kind = "ExternalInput"
            else:
                continue
        io[name] = nc.dram_tensor(name, shape, dt, kind=kind).ap()
    if "E" in phases:
        io["out"] = nc.dram_tensor("out", [T, D], F32, kind="ExternalOutput").ap()
    return io



def const_col(R, g, val):
    key = ("ccol", float(val))
    if key in g:
        return g[key]
    tile = g["es"].enter_context(R.nc.sbuf_tensor(f"ccol_{len(g)}", [128, 1], F32, side="right"))
    t = R.op("vector", lambda e: e.memset(tile[:, :], float(val)))
    g[key] = (tile, t)
    return g[key]


def ln_stats(R, x_ap, x_tick, stat, eps, act_sq_scratch, n=D, deps=()):
    deps = list(deps)
    tsum = R.op("vector", lambda e: e.reduce_sum(stat[:, 0:1], x_ap, axis=mybir.AxisListType.X),
                deps=[x_tick] + deps)
    tssq = R.op("scalar", lambda e: e.activation(act_sq_scratch[:, :n], x_ap, AF.Square,
                                                 accum_out=stat[:, 1:2]), deps=[x_tick] + deps)
    tm = R.op("vector", lambda e: e.tensor_single_scalar(stat[:, 2:3], stat[:, 0:1], 1.0 / n, ALU.mult),
              deps=[tsum])
    te1 = R.op("vector", lambda e: e.tensor_single_scalar(stat[:, 3:4], stat[:, 1:2], 1.0 / n, ALU.mult),
               deps=[tssq])
    tmm = R.op("vector", lambda e: e.tensor_tensor(stat[:, 4:5], stat[:, 2:3], stat[:, 2:3], ALU.mult),
               deps=[tm])
    tvar = R.op("vector", lambda e: e.tensor_tensor(stat[:, 5:6], stat[:, 3:4], stat[:, 4:5], ALU.subtract),
                deps=[te1, tmm])
    eps_tile, eps_tick = eps
    tsd = R.op("scalar", lambda e: e.activation(stat[:, 6:7], stat[:, 5:6], AF.Sqrt,
                                                bias=eps_tile[:, 0:1]),
               deps=[tvar, eps_tick])
    tr = R.op("vector", lambda e: e.reciprocal(stat[:, 6:7], stat[:, 6:7]), deps=[tsd])
    tnmr = R.op("vector", lambda e: e.scalar_tensor_tensor(stat[:, 7:8], stat[:, 2:3], -1.0,
                                                           stat[:, 6:7], ALU.mult, ALU.mult),
                deps=[tm, tr])
    return tr, tnmr



def ensure_identity(R, g):
    if "ident_bf16" in g:
        return
    nc = R.nc
    ges = g["es"]
    g["ident_f32"] = ges.enter_context(nc.sbuf_tensor("ident_f32", [128, 128], F32, side="right"))
    g["ident_bf16"] = ges.enter_context(nc.sbuf_tensor("ident_bf16", [128, 128], BF16, side="right"))
    io_f = ges.enter_context(nc.sbuf_tensor("io_f", [128, 128], F32, side="right"))
    io_p = ges.enter_context(nc.sbuf_tensor("io_p", [128, 128], F32, side="right"))
    build_identity(R, g["ident_f32"], g["ident_bf16"], io_f, io_p)
    g["t_ident"] = R.op("vector", lambda e: e.engine_nop())


def emit_phaseA(R, io, g):
    nc = R.nc
    esA = contextlib.ExitStack()

    with esA:
        es0 = contextlib.ExitStack()
        wqT = esA.enter_context(nc.sbuf_tensor("wqT", [128, 8 * 1024], BF16))
        wkT = esA.enter_context(nc.sbuf_tensor("wkT", [128, 8 * 1024], BF16))
        wvT = esA.enter_context(nc.sbuf_tensor("wvT", [128, 8 * 1024], BF16))
        bqc = esA.enter_context(nc.sbuf_tensor("bqc", [128, 8], F32))
        bk8 = esA.enter_context(nc.sbuf_tensor("bk8", [128, 8], F32))
        bvb = esA.enter_context(nc.sbuf_tensor("bvb", [128, D], F32))
        g0b = esA.enter_context(nc.sbuf_tensor("g0b", [128, D], F32))
        b0b = esA.enter_context(nc.sbuf_tensor("b0b", [128, D], F32))
        g2b = esA.enter_context(nc.sbuf_tensor("g2bA", [128, D], F32))
        b2b = esA.enter_context(nc.sbuf_tensor("b2bA", [128, D], F32))

        with R.phase("A0"), es0:
            ensure_identity(R, g)

            if not g.get("defer_prep"):
                g["t_wzbf"] = R.dma("gpsimd", io["wzbf_s"], io["wz"])
                g["t_whbf"] = R.dma("gpsimd", io["whbf_s"], io["wh"])
                mg = es0.enter_context(nc.sbuf_tensor("m1cA0", [128, FF], BF16))
                pm = None
                for e_ in range(NE):
                    for kc in range(8):
                        tld = R.dma("gpsimd", mg[:, :], io["mw1"][e_, kc * 128:(kc + 1) * 128, :],
                                    deps=[pm] if pm else [], sem=R.dma_sem("Am1g"))
                        for g8 in range(8):
                            for fi in range(4):
                                pm = R.dma("gpsimd",
                                           io["m1g_s"][e_, g8][:, (fi * 8 + kc) * 128:(fi * 8 + kc + 1) * 128],
                                           mg[:, (4 * g8 + fi) * 128:(4 * g8 + fi + 1) * 128],
                                           deps=[tld], sem=R.dma_sem("Am1gst"))

            sset = R.dma_sem("Aset")
            t_g0 = bcast_row(R, g0b, io["ln0_g"], D, sem=sset)
            t_b0 = bcast_row(R, b0b, io["ln0_b"], D, sem=sset)
            t_bv = bcast_row(R, bvb, io["bv"], D, sem=sset)
            t_g2 = bcast_row(R, g2b, io["lnf_g"], D, sem=sset)
            t_b2f = bcast_row(R, b2b, io["lnf_b"], D, sem=sset)
            t_bq = col_tile_load(R, bqc, io["bq"], 8, sem=sset)
            t_bkc = col_tile_load(R, bk8, io["bk"], 8, sem=sset)
            t_bk8 = R.op("vector", lambda e: e.tensor_single_scalar(bk8[:, :], bk8[:, :], 0.125, ALU.mult),
                         deps=t_bkc)
            g["t_ln0"] = [t_g0, t_b0]
            g["t_ln3gb"] = [t_g2, t_b2f]
            g["t_bv"] = t_bv
            g["t_bq"] = t_bq
            g["t_bk8"] = t_bk8

            wstage = [es0.enter_context(nc.sbuf_tensor(f"wstage{i}", [128, 1024], BF16)) for i in range(2)]
            wo_stage = es0.enter_context(nc.sbuf_tensor("wo_stage", [128, 1024], BF16))
            ps_tr = es0.enter_context(nc.psum_tensor("ps_tr", [128, 1024], BF16))
            last_evac = [None, None]
            ps_free = None
            wo_dma = None
            for wi, (wname, dest) in enumerate([("wq", wqT), ("wk", wkT), ("wv", wvT), ("wo", None)]):
                for ic in range(8):
                    buf = (wi * 8 + ic) % 2
                    tld = R.dma("gpsimd", wstage[buf][:, :], io[wname][ic * 128:(ic + 1) * 128, :],
                                deps=[last_evac[buf]] if last_evac[buf] else [],
                                sem=R.dma_sem(f"Aws{buf}"))
                    trs = []
                    for kc in range(8):
                        trs.append(R.op("tensor", lambda e, buf=buf, kc=kc: e.transpose(
                            ps_tr[:, kc * 128:(kc + 1) * 128],
                            wstage[buf][:, kc * 128:(kc + 1) * 128], g["ident_bf16"][:, :]),
                            deps=[tld, g["t_ident"], ps_free] if ps_free else [tld, g["t_ident"]]))
                    last_evac[buf] = trs[-1]
                    if dest is not None:
                        tev = R.op("vector", lambda e, dest=dest, ic=ic: e.tensor_copy(
                            dest.ap().rearrange("p (kc i) -> p kc i", kc=8)[:, :, ic * 128:(ic + 1) * 128],
                            ps_tr.ap().rearrange("p (kc i) -> p kc i", kc=8)),
                            deps=trs)
                        ps_free = tev
                    else:
                        dwo = trs + ([wo_dma] if wo_dma else [])
                        tev = R.op("vector", lambda e, ic=ic: e.tensor_copy(
                            wo_stage[:, :], ps_tr[:, :]), deps=dwo)
                        ps_free = tev
                        wo_dma = R.dma("sync", io["woT_s"].rearrange("p (kc i) -> p kc i", kc=8)[:, :, ic * 128:(ic + 1) * 128],
                              wo_stage.ap().rearrange("p (kc i) -> p kc i", kc=8), deps=[tev],
                              sem=R.dma_sem("Awo"))
            g["t_wqkv"] = [last_evac[0], last_evac[1], ps_free]

            if not g.get("defer_prep"):
                qi1 = [es0.enter_context(nc.sbuf_tensor(f"qi1_{i}", [128, FF], I32)) for i in range(2)]
                sc1 = [es0.enter_context(nc.sbuf_tensor(f"sc1_{i}", [128, FF], F32)) for i in range(2)]
                wout1 = [es0.enter_context(nc.sbuf_tensor(f"wout1_{i}", [128, FF], BF16)) for i in range(2)]
                prev = [None, None]
                for kc in range(8):
                    buf = kc % 2
                    deps = [prev[buf]] if prev[buf] else []
                    tq = R.dma("sync", qi1[buf][:, :], io["fc1_q"][kc * 128:(kc + 1) * 128, :], deps=deps,
                               sem=R.dma_sem(f"Adq{buf}"))
                    tsc = bcast_row(R, sc1[buf], io["fc1_s"][kc], FF, deps=deps, sem=R.dma_sem(f"Adq{buf}"))
                    tw = R.op("vector", lambda e, buf=buf: e.scalar_tensor_tensor(
                        wout1[buf][:, :], qi1[buf][:, :], -128.0, sc1[buf][:, :], ALU.add, ALU.mult),
                        deps=[tq, tsc])
                    last = None
                    for g8 in range(8):
                        for fi in range(4):
                            last = R.dma("sync",
                                         io["w1g_s"][g8][:, (fi * 8 + kc) * 128:(fi * 8 + kc + 1) * 128],
                                         wout1[buf][:, (4 * g8 + fi) * 128:(4 * g8 + fi + 1) * 128],
                                         deps=[tw], sem=R.dma_sem(f"Aw1st{buf}"))
                    prev[buf] = last
                qi2 = [es0.enter_context(nc.sbuf_tensor(f"qi2_{i}", [128, D], I32)) for i in range(2)]
                sc2 = [es0.enter_context(nc.sbuf_tensor(f"sc2_{i}", [128, D], F32)) for i in range(2)]
                wout2 = [es0.enter_context(nc.sbuf_tensor(f"wout2_{i}", [128, D], BF16)) for i in range(2)]
                prev = [None, None]
                for fc in range(32):
                    buf = fc % 2
                    deps = [prev[buf]] if prev[buf] else []
                    tq = R.dma("sync", qi2[buf][:, :], io["fc2_q"][fc * 128:(fc + 1) * 128, :], deps=deps,
                               sem=R.dma_sem(f"Adq2{buf}"))
                    tsc = bcast_row(R, sc2[buf], io["fc2_s"][fc], D, deps=deps, sem=R.dma_sem(f"Adq2{buf}"))
                    tw = R.op("vector", lambda e, buf=buf: e.scalar_tensor_tensor(
                        wout2[buf][:, :], qi2[buf][:, :], -128.0, sc2[buf][:, :], ALU.add, ALU.mult),
                        deps=[tq, tsc])
                    last = None
                    for dc in range(8):
                        last = R.dma("sync", io["w2g_s"][dc][:, fc * 128:(fc + 1) * 128],
                                     wout2[buf][:, dc * 128:(dc + 1) * 128], deps=[tw],
                                     sem=R.dma_sem(f"Aw2st{buf}"))
                    prev[buf] = last

                m2st = [es0.enter_context(nc.sbuf_tensor(f"m2st{i}", [128, D], F32)) for i in range(2)]
                m2bf = [es0.enter_context(nc.sbuf_tensor(f"m2bf{i}", [128, D], BF16)) for i in range(2)]
                prev = [None, None]
                for e_ in range(NE):
                    for fc in range(32):
                        buf = (e_ * 32 + fc) % 2
                        deps = [prev[buf]] if prev[buf] else []
                        tld = R.dma("sync", m2st[buf][:, :], io["mw2"][e_, fc * 128:(fc + 1) * 128, :],
                                    deps=deps, sem=R.dma_sem(f"Am2{buf}"))
                        tcv = R.op("gpsimd", lambda e, buf=buf: e.tensor_copy(m2bf[buf][:, :], m2st[buf][:, :]),
                                   deps=[tld])
                        last = None
                        for dc in range(8):
                            last = R.dma("scalar", io["m2g_s"][e_, dc][:, fc * 128:(fc + 1) * 128],
                                         m2bf[buf][:, dc * 128:(dc + 1) * 128], deps=[tcv],
                                         sem=R.dma_sem(f"Am2st{buf}"))
                        prev[buf] = last


        with R.phase("A"):
            HQ = FF // 4
            qi1d = [esA.enter_context(nc.sbuf_tensor(f"qi1d{j}", [128, HQ], I32)) for j in range(2)]
            sc1d = [esA.enter_context(nc.sbuf_tensor(f"sc1d{j}", [128, HQ], F32)) for j in range(2)]
            wo1d = [esA.enter_context(nc.sbuf_tensor(f"wo1d{j}", [128, HQ], BF16)) for j in range(2)]
            qi2d = [esA.enter_context(nc.sbuf_tensor(f"qi2d{j}", [128, D], I32)) for j in range(2)]
            sc2d = [esA.enter_context(nc.sbuf_tensor(f"sc2d{j}", [128, D], F32)) for j in range(2)]
            wo2d = [esA.enter_context(nc.sbuf_tensor(f"wo2d{j}", [128, D], BF16)) for j in range(2)]
            dq_prev = [[None, None], [None, None]]

            def dequant1(idx):
                kc, hh = idx // 4, idx % 4
                j = idx % 2
                deps = [dq_prev[0][j]] if dq_prev[0][j] else []
                tq = R.dma("gpsimd", qi1d[j][:, :],
                           io["fc1_q"][kc * 128:(kc + 1) * 128, hh * HQ:(hh + 1) * HQ],
                           deps=deps, sem=R.dma_sem(f"Adq1_{j}"))
                sr = io["fc1_s"][kc][hh * HQ:(hh + 1) * HQ]
                tsc = R.dma("gpsimd", sc1d[j][:, :], bass.AP(sr.tensor, sr.offset, [[0, 128], [1, HQ]]),
                            deps=deps, sem=R.dma_sem(f"Adq1_{j}"))
                tw = R.op("vector", lambda e, j=j: e.scalar_tensor_tensor(
                    wo1d[j][:, :], qi1d[j][:, :], -128.0, sc1d[j][:, :], ALU.add, ALU.mult),
                    deps=[tq, tsc])
                last = None
                for g8 in range(2 * hh, 2 * hh + 2):
                    fl0 = 4 * g8 - 8 * hh
                    last = R.dma("scalar",
                                 io["w1g_s"][g8].rearrange("p (fi kq) -> p fi kq", fi=4)[:, :, kc * 128:(kc + 1) * 128],
                                 wo1d[j][:, fl0 * 128:(fl0 + 4) * 128].rearrange("p (fi q) -> p fi q", fi=4),
                                 deps=[tw], sem=R.dma_sem(f"Adq1st{j}"))
                dq_prev[0][j] = last

            def dequant2(fc):
                j = fc % 2
                deps = [dq_prev[1][j]] if dq_prev[1][j] else []
                tq = R.dma("gpsimd", qi2d[j][:, :], io["fc2_q"][fc * 128:(fc + 1) * 128, :], deps=deps,
                           sem=R.dma_sem(f"Adq2_{j}"))
                sr = io["fc2_s"][fc]
                tsc = R.dma("gpsimd", sc2d[j][:, :], bass.AP(sr.tensor, sr.offset, [[0, 128], [1, D]]),
                            deps=deps, sem=R.dma_sem(f"Adq2_{j}"))
                tw = R.op("vector", lambda e, j=j: e.scalar_tensor_tensor(
                    wo2d[j][:, :], qi2d[j][:, :], -128.0, sc2d[j][:, :], ALU.add, ALU.mult),
                    deps=[tq, tsc])
                last = R.dma("scalar",
                             io["w2g_s"].rearrange("dc p n -> p dc n")[:, :, fc * 128:(fc + 1) * 128],
                             wo2d[j][:, :].rearrange("p (dc q) -> p dc q", dc=8),
                             deps=[tw], sem=R.dma_sem(f"Adq2st{j}"))
                dq_prev[1][j] = last

            def dq_hook(i):
                for k in range(4):
                    idx = i * 4 + k
                    if idx < 32:
                        dequant1(idx)
                for k in range(4):
                    fc = i * 4 + k
                    if fc < 32:
                        dequant2(fc)

            ln3p = esA.enter_context(nc.sbuf_tensor("ln3p", [128, 64], F32))
            xb = [esA.enter_context(nc.sbuf_tensor(f"xA{i}", [128, D], F32)) for i in range(2)]
            sqs = esA.enter_context(nc.sbuf_tensor("sqsA", [128, D], F32))
            st1 = [esA.enter_context(nc.sbuf_tensor(f"st1A{i}", [128, 8], F32)) for i in range(2)]
            st2 = [esA.enter_context(nc.sbuf_tensor(f"st2A{i}", [128, 8], F32)) for i in range(2)]
            xn0b = [esA.enter_context(nc.sbuf_tensor(f"xn0A{i}", [128, D], F32)) for i in range(2)]
            tmp1 = [esA.enter_context(nc.sbuf_tensor(f"tmp1A{i}", [128, D], F32)) for i in range(2)]
            xlnb = [esA.enter_context(nc.sbuf_tensor(f"xlnA{i}", [128, D], BF16)) for i in range(2)]
            xlnT = [esA.enter_context(nc.sbuf_tensor(f"xlnTA{i}", [128, 8 * 512], BF16)) for i in range(2)]
            ps_tA = esA.enter_context(nc.psum_tensor("ps_tA", [128, 1024], BF16))
            ps_qk = [esA.enter_context(nc.psum_tensor(f"ps_qk{i}", [128, 512], F32)) for i in range(2)]
            ps_v = [esA.enter_context(nc.psum_tensor(f"ps_v{i}", [128, 1024], F32)) for i in range(2)]
            qs = [esA.enter_context(nc.sbuf_tensor(f"qsA{i}", [128, 512], BF16)) for i in range(2)]
            tmp2 = [esA.enter_context(nc.sbuf_tensor(f"tmp2A{i}", [128, D], F32)) for i in range(2)]
            xl3b = [esA.enter_context(nc.sbuf_tensor(f"xl3bA{i}", [128, D], BF16)) for i in range(2)]
            xl3st = [esA.enter_context(nc.sbuf_tensor(f"xl3stA{i}", [128, 8 * 512], BF16)) for i in range(2)]
            vst = [esA.enter_context(nc.sbuf_tensor(f"vstA{i}", [128, NH * 65], BF16)) for i in range(2)]

            t_vones = [R.op("vector", lambda e, b=b: e.memset(
                vst[b].ap().rearrange("p (h s) -> p h s", h=NH)[:, :, 64:65], 1.0)) for b in range(2)]

            t_x_free = [None, None]
            t_xn0_free = [[], []]
            t_st1_free = [[], []]
            t_st2_free = [[], []]
            t_tmp_free = [[], []]
            t_tmp2_free = [[], []]
            t_xl3b_free = [None, None]
            t_xl3st_free = [None, None]
            t_xln_free = [None, None]
            t_xlnT_free = [None, None]
            t_pstA_free = None
            t_psqk_free = [None, None]
            t_psv_free = [None, None]
            t_qs_free = [None, None]
            t_vst_free = [t_vones[0], t_vones[1]]

            ln0deps = g["t_ln0"]
            for i in range(8):
                dq_hook(i)
                tb = i % 2
                t_chunks = []
                for c in range(4):
                    gch = 4 * i + c
                    buf = gch % 2
                    dl = [t_x_free[buf]] if t_x_free[buf] else []
                    tx = R.dma("sync", xb[buf][:, :], io["x"][gch * 128:(gch + 1) * 128, :], deps=dl,
                               sem=R.dma_sem(f"Axb{buf}"))
                    r1, n1 = ln_stats(R, xb[buf][:, :], tx, st1[buf], const_col(R, g, EPS1), sqs,
                                      deps=t_st1_free[buf])
                    txn0 = R.op("scalar", lambda e, buf=buf: e.activation(
                        xn0b[buf][:, :], xb[buf][:, :], AF.Identity,
                        bias=st1[buf][:, 7:8], scale=st1[buf][:, 6:7]),
                        deps=[r1, n1, tx] + t_xn0_free[buf])
                    t_x_free[buf] = txn0
                    t_st1_free[buf] = [txn0]
                    txn0_st = R.dma("sync", io["xn0_s"][gch * 128:(gch + 1) * 128, :],
                                    xn0b[buf][:, :], deps=[txn0], sem=R.dma_sem(f"Axn0st{buf}"))
                    r2, n2 = ln_stats(R, xn0b[buf][:, :], txn0, st2[buf], const_col(R, g, EPS2), sqs,
                                      deps=t_st2_free[buf])
                    tt1 = R.op("scalar", lambda e, buf=buf: e.activation(
                        tmp1[buf][:, :], xn0b[buf][:, :], AF.Identity,
                        bias=st2[buf][:, 7:8], scale=st2[buf][:, 6:7]),
                        deps=[r2, n2] + t_tmp_free[buf])
                    eps1c = const_col(R, g, EPS1)
                    tsd3 = R.op("scalar", lambda e, buf=buf, gch=gch: e.activation(
                        ln3p[:, gch:gch + 1], st2[buf][:, 5:6], AF.Sqrt, bias=eps1c[0][:, 0:1]),
                        deps=[r2, eps1c[1]])
                    tr3 = R.op("vector", lambda e, gch=gch: e.reciprocal(
                        ln3p[:, gch:gch + 1], ln3p[:, gch:gch + 1]), deps=[tsd3])
                    tn3 = R.op("vector", lambda e, buf=buf, gch=gch: e.scalar_tensor_tensor(
                        ln3p[:, 32 + gch:33 + gch], st2[buf][:, 2:3], -1.0,
                        ln3p[:, gch:gch + 1], ALU.mult, ALU.mult), deps=[tr3])
                    t_st2_free[buf] = [tt1, tn3]
                    t_xn0_free[buf] = [tt1, txn0_st]
                    tt2 = R.op("vector", lambda e, buf=buf: e.tensor_tensor(
                        tmp1[buf][:, :], tmp1[buf][:, :], g0b[:, :], ALU.mult),
                        deps=[tt1] + ln0deps)
                    dxl = [t_xln_free[buf]] if t_xln_free[buf] else []
                    tt3 = R.op("vector", lambda e, buf=buf: e.tensor_tensor(
                        xlnb[buf][:, :], tmp1[buf][:, :], b0b[:, :], ALU.add),
                        deps=[tt2] + dxl)
                    t_tmp_free[buf] = [tt3]
                    trs = []
                    for kc in range(8):
                        dtr = [tt3, g["t_ident"]]
                        if t_pstA_free:
                            dtr.append(t_pstA_free)
                        trs.append(R.op("tensor", lambda e, buf=buf, kc=kc: e.transpose(
                            ps_tA[:, kc * 128:(kc + 1) * 128],
                            xlnb[buf][:, kc * 128:(kc + 1) * 128], g["ident_bf16"][:, :]),
                            deps=dtr))
                    t_xln_free[buf] = trs[-1]
                    dxT = [t_xlnT_free[tb]] if (c == 0 and t_xlnT_free[tb]) else []
                    tev = R.op("vector", lambda e, tb=tb, c=c: e.tensor_copy(
                        xlnT[tb].ap().rearrange("p (kc t) -> p kc t", kc=8)[:, :, c * 128:(c + 1) * 128],
                        ps_tA.ap().rearrange("p (kc t) -> p kc t", kc=8)),
                        deps=trs + dxT)
                    t_pstA_free = tev
                    t_chunks.append(tev)

                    tb1 = R.op("scalar", lambda e, buf=buf, gch=gch: e.activation(
                        tmp2[buf][:, :], xn0b[buf][:, :], AF.Identity,
                        bias=ln3p[:, 32 + gch:33 + gch], scale=ln3p[:, gch:gch + 1]),
                        deps=[tn3, txn0] + t_tmp2_free[buf])
                    t_xn0_free[buf] = t_xn0_free[buf] + [tb1]
                    tb2 = R.op("vector", lambda e, buf=buf: e.tensor_tensor(
                        tmp2[buf][:, :], tmp2[buf][:, :], g2b[:, :], ALU.mult),
                        deps=[tb1] + g["t_ln3gb"])
                    dx3 = [t_xl3b_free[buf]] if t_xl3b_free[buf] else []
                    tb3 = R.op("vector", lambda e, buf=buf: e.tensor_tensor(
                        xl3b[buf][:, :], tmp2[buf][:, :], b2b[:, :], ALU.add),
                        deps=[tb2] + dx3)
                    t_tmp2_free[buf] = [tb3]
                    trs3 = []
                    for kc in range(8):
                        dtr = [tb3, g["t_ident"], t_pstA_free]
                        trs3.append(R.op("tensor", lambda e, buf=buf, kc=kc: e.transpose(
                            ps_tA[:, kc * 128:(kc + 1) * 128],
                            xl3b[buf][:, kc * 128:(kc + 1) * 128], g["ident_bf16"][:, :]),
                            deps=dtr))
                    t_xl3b_free[buf] = trs3[-1]
                    sb2 = i % 2
                    d3s = [t_xl3st_free[sb2]] if (c == 0 and t_xl3st_free[sb2]) else []
                    tev3 = R.op("vector", lambda e, sb2=sb2, c=c: e.tensor_copy(
                        xl3st[sb2].ap().rearrange("p (kc t) -> p kc t", kc=8)[:, :, c * 128:(c + 1) * 128],
                        ps_tA.ap().rearrange("p (kc t) -> p kc t", kc=8)), deps=trs3 + d3s)
                    t_pstA_free = tev3
                    if c == 3:
                        t_xl3st_free[sb2] = R.dma(
                            "sync",
                            io["xln3T_s"].rearrange("(kc p) t -> p kc t", kc=8)[:, :, i * 512:(i + 1) * 512],
                            xl3st[sb2].ap().rearrange("p (kc t) -> p kc t", kc=8),
                            deps=[tev3], sem=R.dma_sem(f"Axl3{sb2}"))

                for (wT, bias_col, scale, dst) in [(wqT, bqc, 1.0, "qT_s"), (wkT, bk8, 0.125, "kT_s")]:
                    for ic in range(8):
                        pb = ic % 2
                        mm_last = None
                        for kc in range(8):
                            dmm = t_chunks + g["t_wqkv"] if kc == 0 else []
                            if kc == 0 and t_psqk_free[pb]:
                                dmm = dmm + [t_psqk_free[pb]]
                            mm_last = R.op("tensor", lambda e, wT=wT, ic=ic, kc=kc, pb=pb, tb=tb: e.matmul(
                                ps_qk[pb][:, :],
                                wT.ap().rearrange("p (kc i) -> p kc i", kc=8)[:, kc, ic * 128:(ic + 1) * 128],
                                xlnT[tb][:, kc * 512:(kc + 1) * 512],
                                start=(kc == 0), stop=(kc == 7)),
                                deps=dmm, inc=(kc == 7))
                        dq = [mm_last] + (g["t_bq"] if dst == "qT_s" else [g["t_bk8"]])
                        if t_qs_free[pb]:
                            dq.append(t_qs_free[pb])
                        tev = R.op("scalar", lambda e, pb=pb, ic=ic, bias_col=bias_col, scale=scale: e.activation(
                            qs[pb][:, :], ps_qk[pb][:, :], AF.Identity,
                            bias=bias_col[:, ic:ic + 1], scale=scale), deps=dq)
                        t_psqk_free[pb] = tev
                        t_qs_free[pb] = R.dma("sync", io[dst][ic * 128:(ic + 1) * 128, i * 512:(i + 1) * 512],
                                              qs[pb][:, :], deps=[tev], sem=R.dma_sem(f"Aqs{pb}"))

                for c in range(4):
                    gch = 4 * i + c
                    pb = c % 2
                    mm_last = None
                    for kc in range(8):
                        dmm = [t_chunks[c]] + g["t_wqkv"] if kc == 0 else []
                        if kc == 0 and t_psv_free[pb]:
                            dmm = dmm + [t_psv_free[pb]]
                        for nh_ in range(2):
                            mm_last = R.op("tensor", lambda e, kc=kc, c=c, pb=pb, nh_=nh_, tb=tb: e.matmul(
                                ps_v[pb][:, nh_ * 512:(nh_ + 1) * 512],
                                xlnT[tb].ap().rearrange("p (kc t) -> p kc t", kc=8)[:, kc, c * 128:(c + 1) * 128],
                                wvT.ap().rearrange("p (kc i) -> p kc i", kc=8)[:, kc, nh_ * 512:(nh_ + 1) * 512],
                                start=(kc == 0), stop=(kc == 7)),
                                deps=dmm, inc=(kc == 7 and nh_ == 1))
                    if c == 3:
                        t_xlnT_free[tb] = mm_last
                    dv = [mm_last, g["t_bv"], t_vst_free[pb]]
                    tev = R.op("vector", lambda e, pb=pb: e.scalar_tensor_tensor(
                        vst[pb].ap().rearrange("p (h s) -> p h s", h=NH)[:, :, 0:64],
                        ps_v[pb].ap().rearrange("p (h d) -> p h d", h=NH),
                        0.0, bvb.ap().rearrange("p (h d) -> p h d", h=NH),
                        ALU.bypass, ALU.add), deps=dv)
                    t_psv_free[pb] = tev
                    lastv = None
                    for hh in range(NH):
                        lastv = R.dma("sync", io["v2_s"][hh][:, gch * 65:(gch + 1) * 65],
                                      vst[pb][:, hh * 65:(hh + 1) * 65], deps=[tev],
                                      sem=R.dma_sem(f"Avst{pb}"))
                    t_vst_free[pb] = lastv

            R.dma("sync", io["ln3_s"][:, :], ln3p[:, :],
                  deps=[Tick(R.sem["vector"], R.semval["vector"]),
                        Tick(R.sem["scalar"], R.semval["scalar"])])
            R.wait("sync", R.drain_ticks())



EMITTERS = {}


def launch_io_names(phases):
    ins = set()
    outs = set()
    for p in phases:
        ins.update(PHASE_INPUTS[p])
    written = {n for n, (_, _, wr) in SCRATCH.items() if wr in phases}
    for p in phases:
        for n in PHASE_SCRATCH_R[p]:
            if n not in written:
                ins.add(n)
    outs.update(written)
    if "E" in phases:
        outs.add("out")
    return sorted(ins), sorted(outs)


def build_launch(phases, fused):
    nc = bass.Bass("TRN2", target_bir_lowering=False)
    io = declare_io(nc, phases, fused)
    g = {"es": contextlib.ExitStack(), "defer_prep": fused and "B" in phases}
    R = Rec(nc)
    with g["es"], R.es:
        for p in phases:
            EMITTERS[p](R, io, g)
    return nc


def map_inputs(inputs, core):
    d = {
        "x": np.ascontiguousarray(inputs["L0_attn_sequence"][core]),
        "ln0_g": inputs["L0_attn_ln_gamma"], "ln0_b": inputs["L0_attn_ln_beta"],
        "wq": inputs["L0_attn_Wq"], "bq": inputs["L0_attn_bq"],
        "wk": inputs["L0_attn_Wk"], "bk": inputs["L0_attn_bk"],
        "wv": inputs["L0_attn_Wv"], "bv": inputs["L0_attn_bv"],
        "wo": inputs["L0_attn_Wo"], "bo": inputs["L0_attn_bo"],
        "mu": np.asarray(inputs["L0_attn_mu"], dtype=np.float32).reshape(1),
        "lnf_g": inputs["L0_ffn_ln_gamma"], "lnf_b": inputs["L0_ffn_ln_beta"],
        "fc1_q": inputs["L0_ffn_fc1_q"], "fc1_s": inputs["L0_ffn_fc1_scales"],
        "fc1_b": inputs["L0_ffn_fc1_bias"],
        "fc2_q": inputs["L0_ffn_fc2_q"], "fc2_s": inputs["L0_ffn_fc2_scales"],
        "fc2_b": inputs["L0_ffn_fc2_bias"],
        "wz": inputs["L1_attn_Wz"], "bz": inputs["L1_attn_bz"],
        "wh": inputs["L1_attn_Wh"], "bh": inputs["L1_attn_bh"],
        "sw_beta": np.asarray(inputs["L1_attn_swish_beta"], dtype=np.float32).reshape(1),
        "lng_g": inputs["L1_attn_ln_gamma"], "lng_b": inputs["L1_attn_ln_beta"],
        "mw1": inputs["L1_ffn_W1"], "mb1": inputs["L1_ffn_b1"],
        "mw2": inputs["L1_ffn_W2"], "mb2": inputs["L1_ffn_b2"],
    }
    return {k: np.ascontiguousarray(np.asarray(v)) for k, v in d.items()}


LAUNCH_PLAN = [["A", "B", "C", "D", "E"]]


def kernel(**inputs):
    state = [map_inputs(inputs, c) for c in range(NC_)]
    for phases in LAUNCH_PLAN:
        fused = len(LAUNCH_PLAN) == 1
        nc = build_launch(phases, fused=fused)
        ins, outs = launch_io_names(phases)
        if fused:
            ins = sorted(set().union(*[PHASE_INPUTS[p] for p in phases]))
        in_maps = [{n: state[c][n] for n in ins} for c in range(NC_)]
        res = run_bass_kernel_spmd(nc, in_maps, core_ids=list(range(NC_)))
        for c in range(NC_):
            state[c].update(res.results[c])
    return np.stack([state[c]["out"] for c in range(NC_)], axis=0)



def make_prework(R, io, g, es):
    nc = R.nc
    m2bf = es.enter_context(nc.sbuf_tensor("m2bfP", [128, D], BF16))
    m1bf = es.enter_context(nc.sbuf_tensor("m1bfP", [128, FF // 2], BF16))
    prevm = [None]
    prevg = [None]

    def m2cast(e_, fc):
        deps = [prevm[0]] if prevm[0] else []
        tld = R.dma("gpsimd", m2bf[:, :], io["mw2"][e_, fc * 128:(fc + 1) * 128, :],
                    deps=deps, sem=R.dma_sem("Pm2"))
        last = None
        for dc in range(8):
            last = R.dma("gpsimd", io["m2g_s"][e_, dc][:, fc * 128:(fc + 1) * 128],
                         m2bf[:, dc * 128:(dc + 1) * 128], deps=[tld],
                         sem=R.dma_sem("Pm2st"))
        prevm[0] = last

    def m1cast(e_, kc, hh):
        deps = [prevg[0]] if prevg[0] else []
        tld = R.dma("gpsimd", m1bf[:, :],
                    io["mw1"][e_, kc * 128:(kc + 1) * 128, hh * (FF // 2):(hh + 1) * (FF // 2)],
                    deps=deps, sem=R.dma_sem("Pm1"))
        last = None
        for g8 in range(4 * hh, 4 * hh + 4):
            for fi in range(4):
                fl = (4 * g8 + fi) - 16 * hh
                last = R.dma("gpsimd",
                             io["m1g_s"][e_, g8][:, (fi * 8 + kc) * 128:(fi * 8 + kc + 1) * 128],
                             m1bf[:, fl * 128:(fl + 1) * 128],
                             deps=[tld], sem=R.dma_sem("Pm1st"))
        prevg[0] = last

    def slice_fn(si):
        def run():
            if si == 0:
                g["t_wzbf"] = R.dma("gpsimd", io["wzbf_s"], io["wz"])
                g["t_whbf"] = R.dma("gpsimd", io["whbf_s"], io["wh"])
            for k in range(8):
                idx = si * 8 + k
                m1cast(idx // 16, (idx // 2) % 8, idx % 2)
            for k in range(16):
                idx = si * 16 + k
                m2cast(idx // 32, idx % 32)
        return run

    return [slice_fn(si) for si in range(16)]


def emit_phaseB(R, io, g, prework=None):
    nc = R.nc
    QT = 512
    NQT = T // QT
    ND = 5
    with contextlib.ExitStack() as esB:
        with R.phase("B"):
            if g.get("defer_prep") and prework is None:
                prework = make_prework(R, io, g, esB)
            kTp = [esB.enter_context(nc.sbuf_tensor(f"kTpB{i}", [128, T], BF16)) for i in range(2)]
            vh = [esB.enter_context(nc.sbuf_tensor(f"vhB{i}", [128, NCH * 65], BF16)) for i in range(2)]
            qh = [esB.enter_context(nc.sbuf_tensor(f"qhB{i}", [128, T], BF16)) for i in range(2)]
            e_t = [esB.enter_context(nc.sbuf_tensor(f"e_tB{i}", [128, 2 * QT], BF16)) for i in range(3)]
            t_t = [esB.enter_context(nc.sbuf_tensor(f"t_tB{i}", [128, 2 * QT], BF16)) for i in range(3)]
            p_t = [esB.enter_context(nc.sbuf_tensor(f"p_tB{i}", [128, 2 * QT], BF16)) for i in range(3)]
            rrow = [esB.enter_context(nc.sbuf_tensor(f"rrowB{i}", [1, QT], F32)) for i in range(2)]
            o_un = [esB.enter_context(nc.sbuf_tensor(f"o_unB{i}", [64, QT], F32)) for i in range(2)]
            rb_bc = [esB.enter_context(nc.sbuf_tensor(f"rb_bcB{i}", [64, QT], F32)) for i in range(2)]
            att_st = [esB.enter_context(nc.sbuf_tensor(f"att_stB{i}", [64, QT], BF16)) for i in range(2)]

            t_th = esB.enter_context(nc.sbuf_tensor("t_thB", [128, 512], BF16))
            y_u = esB.enter_context(nc.sbuf_tensor("y_uB", [128, 512], BF16))
            hb1 = esB.enter_context(nc.sbuf_tensor("hb1B", [128, 32], F32))
            mu_c = esB.enter_context(nc.sbuf_tensor("mu_cB", [128, 1], F32))
            negmu = esB.enter_context(nc.sbuf_tensor("negmuB", [128, 1], F32))
            negmuh = esB.enter_context(nc.sbuf_tensor("negmuhB", [128, 1], F32))
            xl3 = esB.enter_context(nc.sbuf_tensor("xl3B", [128, 8 * 512], BF16))
            y_t = esB.enter_context(nc.sbuf_tensor("y_tB", [128, 32 * 512], BF16))
            wcf = [esB.enter_context(nc.sbuf_tensor(f"wcfB{i}", [128, 32 * 128], BF16)) for i in range(2)]
            z05s = [esB.enter_context(nc.sbuf_tensor(f"z05sB{i}", [128, 512], BF16)) for i in range(2)]
            fc1bc = esB.enter_context(nc.sbuf_tensor("fc1bcB", [128, 32], F32))
            h2bc = esB.enter_context(nc.sbuf_tensor("h2bcB", [128, 8], F32))
            ps_s = [esB.enter_context(nc.psum_tensor(f"ps_sB{i}", [128, 2 * QT], F32)) for i in range(3)]
            ps_o = esB.enter_context(nc.psum_tensor("ps_oB", [128, QT], F32))
            ps_c1 = [esB.enter_context(nc.psum_tensor("ps_c1B", [128, 512], F32))] * 2
            ps_c2 = ps_c1[0]

            skv = R.dma_sem("Bkv")
            t_mu = bcast_row(R, mu_c, io["mu"], 1, sem=skv)
            t_nm = R.op("vector", lambda e: e.tensor_single_scalar(negmu[:, :], mu_c[:, :], -1.0, ALU.mult),
                        deps=[t_mu])
            t_nmh = R.op("vector", lambda e: e.tensor_single_scalar(negmuh[:, :], mu_c[:, :], -0.5, ALU.mult),
                         deps=[t_mu])
            t_f1b = col_tile_load(R, fc1bc, io["fc1_b"], 32, sem=skv)
            t_f2b = col_tile_load(R, h2bc, io["fc2_b"], 8, sem=skv)
            t_h2b = R.op("vector", lambda e: e.tensor_single_scalar(h2bc[:, :], h2bc[:, :], 0.5, ALU.mult),
                         deps=t_f2b)
            t_hb1 = R.op("vector", lambda e: e.tensor_single_scalar(hb1[:, :], fc1bc[:, :], 0.5, ALU.mult),
                         deps=t_f1b)

            t_q = [None, None]
            t_q[0] = R.dma("sync", qh[0][0:64, :], io["qT_s"][0:64, :], sem=R.dma_sem("Bq0"))
            t_kTl = [None, None]
            t_kTl[0] = None
            tk0 = R.dma("sync", kTp[0][0:64, :], io["kT_s"][0:64, :], sem=R.dma_sem("BkT0"))
            tk1 = R.dma("sync", kTp[0][64:128, :], io["kT_s"][64:128, :], sem=R.dma_sem("BkT0"))
            t_kTl[0] = tk1
            t_vhl = [None, None]
            t_vhl[0] = R.dma("sync", vh[0][:, :], io["v2_s"][0], sem=R.dma_sem("Bv0"))
            q_free = [None, None]
            kT_free = [None, None]
            vh_free = [None, None]
            ps_s_free = [None] * 3
            ps_o_free = None
            ep_pend = [None]
            ep_rr_free = [None, None]
            ep_ou_free = [None, None]
            ep_rb_free = [None, None]
            ep_att_free = [None, None]
            et_free = [None] * 3
            tt_free = [None] * 3
            pt_free = [None] * 3
            att_free = None

            cf = {"xl3_free": None, "y_free": None, "wcf_free": [None, None],
                  "c1_free": [None, None], "c2_free": None, "z_free": [None, None],
                  "ylast": None, "xl3_ticks": [], "mm1_last": None,
                  "ev_free": None, "pend": []}

            def flush_ev():
                while cf["pend"]:
                    cf["pend"].pop(0)()

            def st_load_xl3(i):
                def run():
                    d = [cf["xl3_free"]] if cf["xl3_free"] else []
                    cf["xl3_ticks"] = [R.dma("sync", xl3[:, kc * 512:(kc + 1) * 512],
                                             io["xln3T_s"][kc * 128:(kc + 1) * 128, i * 512:(i + 1) * 512],
                                             deps=d, sem=R.dma_sem("BxlT")) for kc in range(8)]
                return run

            def st_mm1(i, g8, fi):
                def run():
                    flush_ev()
                    pg = g8 % 2
                    f = 4 * g8 + fi
                    if fi == 0:
                        dgw = [cf["wcf_free"][pg]] if cf["wcf_free"][pg] else []
                        cf["tw"] = R.dma("scalar", wcf[pg][:, :], io["w1g_s"][g8], deps=dgw,
                                         sem=R.dma_sem(f"BCw{pg}"))
                    tw = cf["tw"]

                    def make_ev(f, mmt):
                        def ev():
                            dv = [cf["ev_free"]] if cf["ev_free"] else []
                            tth = R.op("scalar", lambda e, f=f: e.activation(
                                t_th[:, :], ps_c1[0][:, :], AF.Tanh,
                                bias=hb1[:, f:f + 1], scale=0.5),
                                deps=[mmt, t_hb1] + dv)
                            tyu = R.op("scalar", lambda e, f=f: e.activation(
                                y_u[:, :], ps_c1[0][:, :], AF.Identity,
                                bias=fc1bc[:, f:f + 1]),
                                deps=[mmt] + dv)
                            cf["c1_free"][0] = tyu
                            s1 = R.op("vector", lambda e: e.scalar_tensor_tensor(
                                t_th[:, :], t_th[:, :], 0.5, y_u[:, :], ALU.mult, ALU.mult),
                                deps=[tth, tyu])
                            dyw = [cf["y_free"]] if (f < 2 and cf["y_free"]) else []
                            s2 = R.op("vector", lambda e, f=f: e.scalar_tensor_tensor(
                                y_t[:, f * 512:(f + 1) * 512], y_u[:, :], 0.5,
                                t_th[:, :], ALU.mult, ALU.add),
                                deps=[s1] + dyw)
                            cf["ev_free"] = s2
                            cf["ylast"] = s2
                        return ev

                    mm = None
                    for kc in range(8):
                        d0 = []
                        if kc == 0:
                            d0 = [tw, cf["c1_free"][0]] + cf["xl3_ticks"]
                            if fi == 0 and cf["c2_free"]:
                                d0.append(cf["c2_free"])
                        mm = R.op("tensor", lambda e, fi=fi, kc=kc, pg=pg: e.matmul(
                            ps_c1[0][:, :],
                            wcf[pg][:, (fi * 8 + kc) * 128:(fi * 8 + kc + 1) * 128],
                            xl3[:, kc * 512:(kc + 1) * 512],
                            start=(kc == 0), stop=(kc == 7)),
                            deps=d0, inc=(kc == 7))
                    cf["pend"].append(make_ev(f, mm))
                    if fi == 3:
                        cf["wcf_free"][pg] = mm
                        if g8 == 7:
                            cf["mm1_last"] = mm
                return run

            def st_mm2(i, dc, oct_):
                def run():
                    flush_ev()
                    pg = dc % 2
                    if oct_ == 0:
                        dgw = [cf["wcf_free"][pg]] if cf["wcf_free"][pg] else []
                        cf["tw"] = R.dma("scalar", wcf[pg][:, :], io["w2g_s"][dc], deps=dgw,
                                         sem=R.dma_sem(f"BCw{pg}"))
                    tw = cf["tw"]
                    mm = None
                    for fx in range(8):
                        f = oct_ * 8 + fx
                        d0 = []
                        if f == 0:
                            d0 = [cf["c2_free"], tw, cf["ylast"]]
                            if cf["c1_free"][0]:
                                d0.append(cf["c1_free"][0])
                        elif fx == 0:
                            d0 = [tw]
                        mm = R.op("tensor", lambda e, f=f, pg=pg: e.matmul(
                            ps_c2[:, :],
                            wcf[pg][:, f * 128:(f + 1) * 128],
                            y_t[:, f * 512:(f + 1) * 512],
                            start=(f == 0), stop=(f == 31)),
                            deps=d0, inc=(f == 31))
                    if oct_ < 3:
                        return
                    cf["wcf_free"][pg] = mm
                    if dc == 7:
                        cf["y_free"] = mm
                        cf["xl3_free"] = cf["mm1_last"]

                    def ev(mm=mm, dc=dc, i=i):
                        zb = dc % 2
                        dz = [cf["z_free"][zb]] if cf["z_free"][zb] else []
                        tev = R.op("scalar", lambda e, dc=dc, zb=zb: e.activation(
                            z05s[zb][:, :], ps_c2[:, :], AF.Identity,
                            bias=h2bc[:, dc:dc + 1], scale=0.5),
                            deps=[mm, t_h2b] + dz)
                        cf["c2_free"] = tev
                        cf["z_free"][zb] = R.dma("sync", io["z05_s"][i][:, dc * 512:(dc + 1) * 512],
                                                 z05s[zb][:, :], deps=[tev], sem=R.dma_sem(f"Bz{zb}"))
                    cf["pend"].append(ev)
                return run

            steps = []
            for i in range(8):
                steps.append(st_load_xl3(i))
                for g8 in range(8):
                    for fi in range(4):
                        steps.append(st_mm1(i, g8, fi))
                for dc in range(8):
                    for oc in range(4):
                        steps.append(st_mm2(i, dc, oc))
            step_i = [0]

            def run_steps(k):
                while k > 0 and step_i[0] < len(steps):
                    steps[step_i[0]]()
                    step_i[0] += 1
                    k -= 1

            NP = NCH // 2
            units = [(h, qt) for h in range(NH) for qt in range(NQT)]
            pend = {}

            def emit_S(u, kp):
                h, qt = units[u]
                qb = h % 2
                base = (h % 2) * 64
                pj = h // 2
                kb = pj % 2
                sb = (u * NP + kp) % 3
                if kp == 0 and qt == 0:
                    if prework is not None and h < len(prework):
                        prework[h]()
                    if h + 1 < NH:
                        nb = (h + 1) % 2
                        d = [q_free[nb]] if q_free[nb] else []
                        t_q[nb] = R.dma("sync", qh[nb][((h + 1) % 2) * 64:((h + 1) % 2) * 64 + 64, :],
                                        io["qT_s"][(h + 1) * 64:(h + 2) * 64, :], deps=d,
                                        sem=R.dma_sem(f"Bq{nb}"))
                        dv = [vh_free[nb]] if vh_free[nb] else []
                        t_vhl[nb] = R.dma("sync", vh[nb][:, :], io["v2_s"][h + 1], deps=dv,
                                          sem=R.dma_sem(f"Bv{nb}"))
                    if h % 2 == 0 and pj + 1 < 8:
                        nk = (pj + 1) % 2
                        dk = [kT_free[nk]] if kT_free[nk] else []
                        R.dma("sync", kTp[nk][0:64, :], io["kT_s"][(2 * pj + 2) * 64:(2 * pj + 3) * 64, :],
                              deps=dk, sem=R.dma_sem(f"BkT{nk}"))
                        t_kTl[nk] = R.dma("sync", kTp[nk][64:128, :],
                                          io["kT_s"][(2 * pj + 3) * 64:(2 * pj + 4) * 64, :],
                                          sem=R.dma_sem(f"BkT{nk}"))
                d0 = [t_q[qb], ps_s_free[sb], t_kTl[kb], t_vhl[h % 2]]
                tmm = None
                for half in range(2):
                    kc = 2 * kp + half
                    tmm = R.op("tensor", lambda e, base=base, kb=kb, kc=kc, qb=qb, qt=qt, sb=sb, half=half: e.matmul(
                        ps_s[sb][:, half * QT:(half + 1) * QT],
                        kTp[kb][base:base + 64, kc * 128:(kc + 1) * 128],
                        qh[qb][base:base + 64, qt * QT: (qt + 1) * QT],
                        start=True, stop=True),
                        deps=(d0 if half == 0 else []), inc=(half == 1))
                if qt == NQT - 1 and kp == NP - 1:
                    q_free[qb] = tmm
                    if h % 2 == 1:
                        kT_free[h // 2 % 2] = tmm
                pend[(u, kp)] = tmm

            emit_S(0, 0)
            for u in range(len(units)):
                h, qt = units[u]
                last_pv = None
                for kp in range(NP):
                    sb = (u * NP + kp) % 3
                    tb3 = (u * NP + kp) % 3
                    if kp + 1 < NP:
                        emit_S(u, kp + 1)
                    elif u + 1 < len(units):
                        emit_S(u + 1, 0)
                    tmm = pend.pop((u, kp))
                    t_tanh = R.op("scalar", lambda e, sb=sb, tb3=tb3: e.activation(
                        t_t[tb3][:, :], ps_s[sb][:, :], AF.Tanh, bias=negmuh[:, 0:1], scale=0.5),
                        deps=[tmm, t_nmh, tt_free[tb3]])
                    t_exp = R.op("scalar", lambda e, sb=sb, tb3=tb3: e.activation(
                        e_t[tb3][:, :], ps_s[sb][:, :], AF.Exp, bias=negmu[:, 0:1]),
                        deps=[tmm, t_nm, et_free[tb3]])
                    ps_s_free[sb] = t_exp
                    t_gg = R.op("vector", lambda e, tb3=tb3: e.tensor_scalar(
                        t_t[tb3][:, :], t_t[tb3][:, :], 0.5, 0.5, ALU.mult, ALU.add),
                        deps=[t_tanh])
                    t_p = R.op("vector", lambda e, tb3=tb3: e.tensor_tensor(
                        p_t[tb3][:, :], e_t[tb3][:, :], t_t[tb3][:, :], ALU.mult),
                        deps=[t_exp, t_gg, pt_free[tb3]])
                    et_free[tb3] = t_p
                    tt_free[tb3] = t_p
                    for half in range(2):
                        kc = 2 * kp + half
                        dpv = [t_p] if half == 0 else []
                        if kp == 0 and half == 0 and ps_o_free is not None:
                            dpv.append(ps_o_free)
                        last_pv = R.op("tensor", lambda e, kc=kc, h=h, tb3=tb3, half=half: e.matmul(
                            ps_o[0:65, :],
                            vh[h % 2][:, kc * 65:(kc + 1) * 65],
                            p_t[tb3][:, half * QT:(half + 1) * QT],
                            start=(kc == 0), stop=(kc == NCH - 1)),
                            deps=dpv, inc=(half == 1))
                    pt_free[tb3] = last_pv
                    if kp % 2 == 1:
                        run_steps(1)
                if qt == NQT - 1:
                    vh_free[h % 2] = last_pv
                if ep_pend[0] is not None:
                    ep_pend[0]()
                    ep_pend[0] = None
                eb = u % 2
                t_rec = R.op("vector", lambda e, eb=eb: e.reciprocal(rrow[eb][0:1, :], ps_o[64:65, :]),
                             deps=[last_pv] + ([ep_rr_free[eb]] if ep_rr_free[eb] else []))
                t_oev = R.op("vector", lambda e, eb=eb: e.tensor_copy(o_un[eb][:, :], ps_o[0:64, :]),
                             deps=[last_pv] + ([ep_ou_free[eb]] if ep_ou_free[eb] else []))
                ps_o_free = t_oev
                t_st1 = R.dma("sync", io["rrow_s"][eb], rrow[eb][0:1, :], deps=[t_rec],
                              sem=R.dma_sem(f"Brr{eb}"))
                rsl = io["rrow_s"][eb]
                t_bcd = R.dma("sync", rb_bc[eb][:, :],
                              bass.AP(rsl.tensor, rsl.offset, [[0, 64], [1, QT]]),
                              deps=[t_st1] + ([ep_rb_free[eb]] if ep_rb_free[eb] else []),
                              sem=R.dma_sem(f"Brb{eb}"))

                def stage2(eb=eb, h=h, qt=qt, t_bcd=t_bcd, t_oev=t_oev):
                    da = [ep_att_free[eb]] if ep_att_free[eb] else []
                    t_mul = R.op("vector", lambda e, eb=eb: e.tensor_tensor(
                        att_st[eb][:, :], o_un[eb][:, :], rb_bc[eb][:, :], ALU.mult),
                        deps=[t_oev, t_bcd] + da)
                    ep_ou_free[eb] = t_mul
                    ep_rb_free[eb] = t_mul
                    ep_rr_free[eb] = t_mul
                    ep_att_free[eb] = R.dma("sync",
                                            io["attnT_s"][h * 64:(h + 1) * 64, qt * QT:(qt + 1) * QT],
                                            att_st[eb][:, :], deps=[t_mul], sem=R.dma_sem(f"Batt{eb}"))
                ep_pend[0] = stage2
                run_steps(1)
            if ep_pend[0] is not None:
                ep_pend[0]()
                ep_pend[0] = None
            run_steps(len(steps))
            flush_ev()

            R.wait("sync", R.drain_ticks())


def emit_phaseC(R, io, g):
    nc = R.nc
    with contextlib.ExitStack() as esC:
        with R.phase("C"):
            ensure_identity(R, g)
            woT = esC.enter_context(nc.sbuf_tensor("woTC", [128, 8 * 1024], BF16))
            bob = esC.enter_context(nc.sbuf_tensor("bobC", [128, D], F32))
            xn0t = esC.enter_context(nc.sbuf_tensor("xn0tC", [128, 4 * D], F32))
            xt = [esC.enter_context(nc.sbuf_tensor(f"xtC{i}", [128, D], F32)) for i in range(2)]
            z05 = esC.enter_context(nc.sbuf_tensor("z05C", [128, 8 * 512], BF16))
            attnt = esC.enter_context(nc.sbuf_tensor("attntC", [128, 8 * 512], BF16))
            ztok = esC.enter_context(nc.sbuf_tensor("ztokC", [128, D], F32))
            fg = esC.enter_context(nc.sbuf_tensor("fgC", [128, D], F32))
            ug = esC.enter_context(nc.sbuf_tensor("ugC", [128, D], F32))
            sg = esC.enter_context(nc.sbuf_tensor("sgC", [128, D], F32))
            x1b = [esC.enter_context(nc.sbuf_tensor(f"x1bC{i}", [128, D], F32)) for i in range(2)]
            ps_tr = esC.enter_context(nc.psum_tensor("ps_trC", [128, 1024], BF16))
            ps_s0 = esC.enter_context(nc.psum_tensor("ps_s0C", [128, 1024], F32))

            scs = R.dma_sem("Cset")
            t_wo = R.dma("sync", woT[:, :], io["woT_s"][:, :], sem=scs)
            t_bo = bcast_row(R, bob, io["bo"], D, sem=scs)

            t_xn0_free = None
            t_xt_free = [None, None]
            t_pstr_free = None
            t_ps0_free = None
            t_z05_free = None
            t_att_free = None
            t_ztok_free = None
            t_fg_free = None
            t_ug_free = None
            t_sg_free = None
            t_x1_free = [None, None]

            for i in range(8):
                dz05 = [t_z05_free] if t_z05_free else []
                t_z = R.dma("sync", z05[:, :], io["z05_s"][i], deps=dz05, sem=R.dma_sem("Cz05"))
                txn_l = []
                for c in range(4):
                    gch = 4 * i + c
                    dl = [t_xn0_free] if (c == 0 and t_xn0_free) else []
                    txn_l.append(R.dma("sync", xn0t[:, c * D:(c + 1) * D],
                                       io["xn0_s"][gch * 128:(gch + 1) * 128, :], deps=dl,
                                       sem=R.dma_sem(f"Cxn{c}")))
                datt = [t_att_free] if t_att_free else []
                t_att = [R.dma("sync", attnt[:, jc * 512:(jc + 1) * 512],
                               io["attnT_s"][jc * 128:(jc + 1) * 128, i * 512:(i + 1) * 512],
                               deps=datt, sem=R.dma_sem("Catt")) for jc in range(8)]

                last_trs = None
                for c in range(4):
                    gch = 4 * i + c
                    xbuf = gch % 2
                    dxl2 = [t_xt_free[xbuf]] if t_xt_free[xbuf] else []
                    tx = R.dma("sync", xt[xbuf][:, :], io["x"][gch * 128:(gch + 1) * 128, :],
                               deps=dxl2, sem=R.dma_sem(f"Cx{xbuf}"))
                    trs = []
                    for dc in range(8):
                        dtr = [t_z, g["t_ident"]]
                        if t_pstr_free:
                            dtr.append(t_pstr_free)
                        trs.append(R.op("tensor", lambda e, dc=dc, c=c: e.transpose(
                            ps_tr[:, dc * 128:(dc + 1) * 128],
                            z05[:, dc * 512 + c * 128: dc * 512 + (c + 1) * 128],
                            g["ident_bf16"][:, :]), deps=dtr))
                    last_trs = trs[-1]
                    dzt = [t_ztok_free] if t_ztok_free else []
                    tzt = R.op("vector", lambda e: e.tensor_copy(ztok[:, :], ps_tr[:, :]),
                               deps=trs + dzt)
                    t_pstr_free = tzt
                    dfg = [t_fg_free] if t_fg_free else []
                    tf = R.op("vector", lambda e, c=c: e.tensor_tensor(
                        fg[:, :], ztok[:, :], xn0t[:, c * D:(c + 1) * D], ALU.add),
                        deps=[tzt, txn_l[c]] + dfg)
                    t_ztok_free = tf
                    tg0 = R.op("scalar", lambda e: e.activation(fg[:, :], fg[:, :], AF.Silu),
                               deps=[tf])
                    mm = None
                    for jc in range(8):
                        d0 = []
                        if jc == 0:
                            d0 = t_att + ([t_wo] if i == 0 else [])
                            if t_ps0_free:
                                d0.append(t_ps0_free)
                        for nh_ in range(2):
                            mm = R.op("tensor", lambda e, jc=jc, c=c, nh_=nh_: e.matmul(
                                ps_s0[:, nh_ * 512:(nh_ + 1) * 512],
                                attnt[:, jc * 512 + c * 128: jc * 512 + (c + 1) * 128],
                                woT[:, jc * 1024 + nh_ * 512: jc * 1024 + (nh_ + 1) * 512],
                                start=(jc == 0), stop=(jc == 7)),
                                deps=d0, inc=(jc == 7 and nh_ == 1))
                    dug = [t_ug_free] if t_ug_free else []
                    tu = R.op("vector", lambda e: e.scalar_tensor_tensor(
                        ug[:, :], ps_s0[:, :], 0.0, bob[:, :], ALU.bypass, ALU.add),
                        deps=[mm, t_bo] + dug)
                    t_ps0_free = tu
                    dsg = [t_sg_free] if t_sg_free else []
                    tsg = R.op("vector", lambda e: e.tensor_tensor(
                        sg[:, :], ug[:, :], fg[:, :], ALU.mult),
                        deps=[tu, tg0] + dsg)
                    t_ug_free = tsg
                    t_fg_free = tsg
                    dx1 = [t_x1_free[xbuf]] if t_x1_free[xbuf] else []
                    tx1 = R.op("vector", lambda e, xbuf=xbuf: e.tensor_tensor(
                        x1b[xbuf][:, :], sg[:, :], xt[xbuf][:, :], ALU.add),
                        deps=[tsg, tx] + dx1)
                    t_sg_free = tx1
                    t_xt_free[xbuf] = tx1
                    t_x1_free[xbuf] = R.dma("sync", io["x1_s"][gch * 128:(gch + 1) * 128, :],
                                            x1b[xbuf][:, :], deps=[tx1], sem=R.dma_sem(f"Cx1{xbuf}"))
                t_xn0_free = tf
                t_z05_free = last_trs
                t_att_free = mm

            R.wait("sync", R.drain_ticks())



def emit_phaseD(R, io, g):
    nc = R.nc
    with contextlib.ExitStack() as esD:
        es0 = contextlib.ExitStack()
        with R.phase("D0"), es0:
            ensure_identity(R, g)
            x1t = [es0.enter_context(nc.sbuf_tensor(f"x1tD{i}", [128, D], F32)) for i in range(2)]
            st = [es0.enter_context(nc.sbuf_tensor(f"stD{i}", [128, 8], F32)) for i in range(2)]
            xn1 = [es0.enter_context(nc.sbuf_tensor(f"xn1D{i}", [128, D], F32)) for i in range(2)]
            xnb = [es0.enter_context(nc.sbuf_tensor(f"xnbD{i}", [128, D], BF16)) for i in range(2)]
            stg = [es0.enter_context(nc.sbuf_tensor(f"stgD{i}", [128, 8 * 512], BF16)) for i in range(2)]
            scol = es0.enter_context(nc.sbuf_tensor("scolD", [128, 32], F32))
            sqs = es0.enter_context(nc.sbuf_tensor("sqsD", [128, D], F32))
            ps_tr = es0.enter_context(nc.psum_tensor("ps_trD0", [128, 1024], BF16))
            ps_rt = es0.enter_context(nc.psum_tensor("ps_rtD0", [128, 128], F32))
            srow = es0.enter_context(nc.sbuf_tensor("srowD", [32, 128], F32))
            r_sgn = es0.enter_context(nc.sbuf_tensor("r_sgnD", [32, 128], F32))
            r_adj = es0.enter_context(nc.sbuf_tensor("r_adjD", [32, 128], F32))
            r_i = es0.enter_context(nc.sbuf_tensor("r_iD", [32, 128], I32))
            r_f = es0.enter_context(nc.sbuf_tensor("r_fD", [32, 128], F32))
            r_u = es0.enter_context(nc.sbuf_tensor("r_uD", [32, 128], F32))
            r_fli = es0.enter_context(nc.sbuf_tensor("r_fliD", [32, 128], I32))
            r_flf = es0.enter_context(nc.sbuf_tensor("r_flfD", [32, 128], F32))
            r_m = es0.enter_context(nc.sbuf_tensor("r_mD", [32, 128], F32))
            r_e1 = es0.enter_context(nc.sbuf_tensor("r_e1D", [32, 128], F32))
            r_e2 = es0.enter_context(nc.sbuf_tensor("r_e2D", [32, 128], F32))
            r_w = es0.enter_context(nc.sbuf_tensor("r_wD", [32, 128], BF16))

            t_x_free = [None, None]
            t_st_free = [[], []]
            t_xn_free = [[], []]
            t_xnb_free = [None, None]
            t_stg_free = [None, None]
            t_pstr_free = None
            t_scol = []
            for gch in range(NCH):
                buf = gch % 2
                dl = [t_x_free[buf]] if t_x_free[buf] else []
                tx = R.dma("sync", x1t[buf][:, :], io["x1_s"][gch * 128:(gch + 1) * 128, :],
                           deps=dl, sem=R.dma_sem(f"Dx{buf}"))
                r1, n1 = ln_stats(R, x1t[buf][:, :], tx, st[buf], const_col(R, g, EPS1), sqs,
                                  deps=t_st_free[buf])
                txn = R.op("scalar", lambda e, buf=buf: e.activation(
                    xn1[buf][:, :], x1t[buf][:, :], AF.Identity,
                    bias=st[buf][:, 7:8], scale=st[buf][:, 6:7]),
                    deps=[r1, n1] + t_xn_free[buf])
                t_x_free[buf] = txn
                t_st_free[buf] = [txn]
                tsc = R.op("vector", lambda e, buf=buf, gch=gch: e.tensor_tensor(
                    scol[:, gch:gch + 1], xn1[buf][:, 0:1], xn1[buf][:, 1:2], ALU.add),
                    deps=[txn])
                t_scol.append(tsc)
                dxb = [t_xnb_free[buf]] if t_xnb_free[buf] else []
                txb = R.op("vector", lambda e, buf=buf: e.tensor_copy(xnb[buf][:, :], xn1[buf][:, :]),
                           deps=[txn] + dxb)
                t_xn_free[buf] = [txb, tsc]
                trs = []
                for kc in range(8):
                    dtr = [txb, g["t_ident"]]
                    if t_pstr_free:
                        dtr.append(t_pstr_free)
                    trs.append(R.op("tensor", lambda e, buf=buf, kc=kc: e.transpose(
                        ps_tr[:, kc * 128:(kc + 1) * 128],
                        xnb[buf][:, kc * 128:(kc + 1) * 128], g["ident_bf16"][:, :]),
                        deps=dtr))
                t_xnb_free[buf] = trs[-1]
                sbuf2 = (gch // 4) % 2
                cc = gch % 4
                dstg = [t_stg_free[sbuf2]] if (cc == 0 and t_stg_free[sbuf2]) else []
                tev = R.op("vector", lambda e, sbuf2=sbuf2, cc=cc: e.tensor_copy(
                    stg[sbuf2].ap().rearrange("p (kc t) -> p kc t", kc=8)[:, :, cc * 128:(cc + 1) * 128],
                    ps_tr.ap().rearrange("p (kc t) -> p kc t", kc=8)), deps=trs + dstg)
                t_pstr_free = tev
                if cc == 3:
                    ti0 = gch // 4
                    t_stg_free[sbuf2] = R.dma(
                        "sync",
                        io["xn1T_s"].rearrange("(kc p) t -> p kc t", kc=8)[:, :, ti0 * 512:(ti0 + 1) * 512],
                        stg[sbuf2].ap().rearrange("p (kc t) -> p kc t", kc=8),
                        deps=[tev], sem=R.dma_sem(f"Dxst{sbuf2}"))

            ttr = R.op("tensor", lambda e: e.transpose(ps_rt[0:32, :], scol[:, :], g["ident_f32"][:, :]),
                       deps=t_scol + [g["t_ident"]])
            tsr = R.op("vector", lambda e: e.tensor_copy(srow[:, :], ps_rt[0:32, :]), deps=[ttr])
            tg1 = R.op("scalar", lambda e: e.activation(r_sgn[:, :], srow[:, :], AF.Sign), deps=[tsr])
            tg2 = R.op("vector", lambda e: e.scalar_tensor_tensor(
                r_adj[:, :], r_sgn[:, :], -0.5, srow[:, :], ALU.mult, ALU.add), deps=[tg1, tsr])
            tg3 = R.op("vector", lambda e: e.tensor_copy(r_i[:, :], r_adj[:, :]), deps=[tg2])
            tg4 = R.op("vector", lambda e: e.tensor_copy(r_f[:, :], r_i[:, :]), deps=[tg3])
            tg5 = R.op("vector", lambda e: e.tensor_scalar(
                r_u[:, :], r_f[:, :], 0.125, -0.4375, ALU.mult, ALU.add), deps=[tg4])
            tg6 = R.op("vector", lambda e: e.tensor_copy(r_fli[:, :], r_u[:, :]), deps=[tg5])
            tg7 = R.op("vector", lambda e: e.tensor_copy(r_flf[:, :], r_fli[:, :]), deps=[tg6])
            tg8 = R.op("vector", lambda e: e.scalar_tensor_tensor(
                r_m[:, :], r_flf[:, :], -8.0, r_f[:, :], ALU.mult, ALU.add), deps=[tg7, tg4])
            prev_w = None
            for e_ in range(NE):
                dp = [tg8] + ([prev_w] if prev_w else [])
                te1 = R.op("vector", lambda e, e_=e_: e.tensor_single_scalar(
                    r_e1[:, :], r_m[:, :], float(e_), ALU.is_equal), deps=dp)
                te2 = R.op("vector", lambda e, e_=e_: e.tensor_single_scalar(
                    r_e2[:, :], r_m[:, :], float((e_ + 7) % 8), ALU.is_equal), deps=dp)
                te3 = R.op("vector", lambda e: e.tensor_tensor(
                    r_e1[:, :], r_e1[:, :], r_e2[:, :], ALU.add), deps=[te1, te2])
                te4 = R.op("vector", lambda e: e.tensor_single_scalar(
                    r_w[:, :], r_e1[:, :], 0.5, ALU.mult), deps=[te3])
                prev_w = R.dma("sync", io["w_rows_s"][e_].rearrange("(c j) -> c j", c=32), r_w[:, :],
                               deps=[te4], sem=R.dma_sem("Dwr"))
            R.wait("sync", R.drain_ticks())

        es1 = contextlib.ExitStack()
        with R.phase("D1"), es1:
            wz_t = es1.enter_context(nc.sbuf_tensor("wz_tD", [128, 8 * D], BF16))
            wh_t = es1.enter_context(nc.sbuf_tensor("wh_tD", [128, 8 * D], BF16))
            xnT = es1.enter_context(nc.sbuf_tensor("xnTD", [128, 8 * 1024], BF16))
            zf = es1.enter_context(nc.sbuf_tensor("zfD", [128, 8 * 1024], F32))
            hf = es1.enter_context(nc.sbuf_tensor("hfD", [128, 8 * 1024], F32))
            hout = es1.enter_context(nc.sbuf_tensor("houtD", [128, 8 * 1024], F32))
            sgt = es1.enter_context(nc.sbuf_tensor("sgtD", [128, 8 * 1024], F32))
            ytok = [es1.enter_context(nc.sbuf_tensor(f"ytokD{i}", [128, 1024], F32)) for i in range(2)]
            carry = es1.enter_context(nc.sbuf_tensor("carryD", [128, 8], F32))
            bzc = es1.enter_context(nc.sbuf_tensor("bzcD", [128, 8], F32))
            bhc = es1.enter_context(nc.sbuf_tensor("bhcD", [128, 8], F32))
            betac = es1.enter_context(nc.sbuf_tensor("betacD", [128, 1], F32))
            ps_pz = [es1.enter_context(nc.psum_tensor(f"ps_pzD{i}", [128, 1024], F32)) for i in range(2)]
            ps_trD = es1.enter_context(nc.psum_tensor("ps_trD1", [128, 1024], F32))

            sdw = R.dma_sem("Dw")
            t_wz = [R.dma("sync", wz_t[:, kc * D:(kc + 1) * D],
                          io["wzbf_s"][kc * 128:(kc + 1) * 128, :], sem=sdw) for kc in range(8)]
            t_wh = [R.dma("sync", wh_t[:, kc * D:(kc + 1) * D],
                          io["whbf_s"][kc * 128:(kc + 1) * 128, :], sem=sdw) for kc in range(8)]
            sds = R.dma_sem("Dset")
            t_bz = col_tile_load(R, bzc, io["bz"], 8, sem=sds)
            t_bh = col_tile_load(R, bhc, io["bh"], 8, sem=sds)
            t_beta = bcast_row(R, betac, io["sw_beta"], 1, sem=sds)
            t_carry0 = R.op("vector", lambda e: e.memset(carry[:, :], 0.0))

            t_xnT_free = None
            t_pz_free = [None, None]
            t_zf_free = None
            t_hf_free = None
            t_hout_free = None
            t_sgt_free = None
            t_ytok_free = [None, None]
            t_pstr_free = None
            t_carry = {dc: t_carry0 for dc in range(8)}
            for tc in range(4):
                dxn = [t_xnT_free] if t_xnT_free else []
                t_xn = [R.dma("sync", xnT[:, kc * 1024:(kc + 1) * 1024],
                              io["xn1T_s"][kc * 128:(kc + 1) * 128, tc * 1024:(tc + 1) * 1024],
                              deps=dxn, sem=R.dma_sem("DxnT")) for kc in range(8)]
                t_z_evac = []
                t_h_evac = []
                for (wt, tws, bias_t, tbias, func, dest, evlist) in (
                        (wz_t, t_wz, bzc, t_bz, AF.Sigmoid, zf, t_z_evac),
                        (wh_t, t_wh, bhc, t_bh, AF.Identity, hf, t_h_evac)):
                    for dc in range(8):
                        pb = dc % 2
                        mm = None
                        for kc in range(8):
                            d0 = []
                            if kc == 0:
                                d0 = t_xn + tws + [t_pz_free[pb]]
                                if dest is zf and t_zf_free:
                                    d0.append(t_zf_free)
                                if dest is hf and t_hf_free:
                                    d0.append(t_hf_free)
                            for n in range(2):
                                mm = R.op("tensor", lambda e, wt=wt, dc=dc, kc=kc, pb=pb, n=n: e.matmul(
                                    ps_pz[pb][:, n * 512:(n + 1) * 512],
                                    wt[:, kc * D + dc * 128: kc * D + (dc + 1) * 128],
                                    xnT[:, kc * 1024 + n * 512: kc * 1024 + (n + 1) * 512],
                                    start=(kc == 0), stop=(kc == 7)),
                                    deps=d0 if (kc == 0 and n == 0) else [], inc=(kc == 7 and n == 1))
                        tev = R.op("scalar", lambda e, wt=wt, dc=dc, pb=pb, bias_t=bias_t, func=func, dest=dest: e.activation(
                            dest[:, dc * 1024:(dc + 1) * 1024], ps_pz[pb][:, :], func,
                            bias=bias_t[:, dc:dc + 1]),
                            deps=[mm] + tbias)
                        t_pz_free[pb] = tev
                        evlist.append(tev)
                t_xnT_free = mm
                tb = R.op("vector", lambda e: e.tensor_tensor(hf[:, :], zf[:, :], hf[:, :], ALU.mult),
                          deps=t_z_evac + t_h_evac)
                ta = R.op("vector", lambda e: e.tensor_scalar(zf[:, :], zf[:, :], -1.0, 1.0,
                                                              ALU.mult, ALU.add), deps=[tb])
                t_scans = []
                for dc in range(8):
                    dsc = [ta, tb, t_carry[dc]]
                    if t_hout_free:
                        dsc.append(t_hout_free)
                    tsc = R.op("vector", lambda e, dc=dc: e.tensor_tensor_scan(
                        hout[:, dc * 1024:(dc + 1) * 1024],
                        zf[:, dc * 1024:(dc + 1) * 1024],
                        hf[:, dc * 1024:(dc + 1) * 1024],
                        carry[:, dc:dc + 1] if tc > 0 else 0.0,
                        ALU.mult, ALU.add), deps=dsc)
                    t_scans.append(tsc)
                    if tc < 3:
                        t_carry[dc] = R.op("vector", lambda e, dc=dc: e.tensor_copy(
                            carry[:, dc:dc + 1],
                            hout[:, dc * 1024 + 1023: dc * 1024 + 1024]), deps=[tsc])
                t_zf_free = t_scans[-1]
                t_hf_free = t_scans[-1]
                dsg = [t_sgt_free] if t_sgt_free else []
                tsig = R.op("scalar", lambda e: e.activation(
                    sgt[:, :], hout[:, :], AF.Sigmoid, scale=betac[:, 0:1]),
                    deps=t_scans + [t_beta] + dsg)
                ty = R.op("vector", lambda e: e.scalar_tensor_tensor(
                    hf[:, :], hout[:, :], betac[:, 0:1], sgt[:, :], ALU.mult, ALU.mult),
                    deps=[tsig])
                t_hout_free = ty
                t_sgt_free = ty
                t_hf_free = None
                for ts_ in range(8):
                    yb = ts_ % 2
                    trs = []
                    for dc in range(8):
                        dtr = [ty, g["t_ident"]]
                        if t_pstr_free:
                            dtr.append(t_pstr_free)
                        trs.append(R.op("tensor", lambda e, dc=dc, ts_=ts_: e.transpose(
                            ps_trD[:, dc * 128:(dc + 1) * 128],
                            hf[:, dc * 1024 + ts_ * 128: dc * 1024 + (ts_ + 1) * 128],
                            g["ident_f32"][:, :]), deps=dtr))
                    dyt = [t_ytok_free[yb]] if t_ytok_free[yb] else []
                    tev = R.op("vector", lambda e, yb=yb: e.tensor_copy(ytok[yb][:, :], ps_trD[:, :]),
                               deps=trs + dyt)
                    t_pstr_free = tev
                    gch = tc * 8 + ts_
                    t_ytok_free[yb] = R.dma("sync", io["y_s"][gch * 128:(gch + 1) * 128, :],
                                            ytok[yb][:, :], deps=[tev], sem=R.dma_sem(f"Dyst{yb}"))
                    if ts_ == 7:
                        t_hf_free = trs[-1]
            R.wait("sync", R.drain_ticks())

        es2 = contextlib.ExitStack()
        with R.phase("D2"), es2:
            yt = [es2.enter_context(nc.sbuf_tensor(f"ytD2{i}", [128, D], F32)) for i in range(2)]
            st = [es2.enter_context(nc.sbuf_tensor(f"stD2{i}", [128, 8], F32)) for i in range(2)]
            tmp = [es2.enter_context(nc.sbuf_tensor(f"tmpD2{i}", [128, D], F32)) for i in range(2)]
            s1b = [es2.enter_context(nc.sbuf_tensor(f"s1bD2{i}", [128, D], BF16)) for i in range(2)]
            sqs = es2.enter_context(nc.sbuf_tensor("sqsD2", [128, D], F32))
            ggb = es2.enter_context(nc.sbuf_tensor("ggbD2", [128, D], F32))
            gbb = es2.enter_context(nc.sbuf_tensor("gbbD2", [128, D], F32))
            sds2 = R.dma_sem("Dset2")
            t_gg = bcast_row(R, ggb, io["lng_g"], D, sem=sds2)
            t_gb = bcast_row(R, gbb, io["lng_b"], D, sem=sds2)
            t_y_free = [None, None]
            t_st_free = [[], []]
            t_tmp_free = [[], []]
            t_tmp2_free = [[], []]
            t_xl3b_free = [None, None]
            t_xl3st_free = [None, None]
            t_s1_free = [None, None]
            for gch in range(NCH):
                buf = gch % 2
                dl = [t_y_free[buf]] if t_y_free[buf] else []
                ty_ = R.dma("sync", yt[buf][:, :], io["y_s"][gch * 128:(gch + 1) * 128, :],
                            deps=dl, sem=R.dma_sem(f"D2y{buf}"))
                r1, n1 = ln_stats(R, yt[buf][:, :], ty_, st[buf], const_col(R, g, EPS1), sqs,
                                  deps=t_st_free[buf])
                tt1 = R.op("scalar", lambda e, buf=buf: e.activation(
                    tmp[buf][:, :], yt[buf][:, :], AF.Identity,
                    bias=st[buf][:, 7:8], scale=st[buf][:, 6:7]),
                    deps=[r1, n1] + t_tmp_free[buf])
                t_y_free[buf] = tt1
                t_st_free[buf] = [tt1]
                tt2 = R.op("vector", lambda e, buf=buf: e.tensor_tensor(
                    tmp[buf][:, :], tmp[buf][:, :], ggb[:, :], ALU.mult), deps=[tt1, t_gg])
                ds1 = [t_s1_free[buf]] if t_s1_free[buf] else []
                tt3 = R.op("vector", lambda e, buf=buf: e.tensor_tensor(
                    s1b[buf][:, :], tmp[buf][:, :], gbb[:, :], ALU.add), deps=[tt2, t_gb] + ds1)
                t_tmp_free[buf] = [tt3]
                t_s1_free[buf] = R.dma("sync", io["s1_s"][gch * 128:(gch + 1) * 128, :],
                                       s1b[buf][:, :], deps=[tt3], sem=R.dma_sem(f"D2s{buf}"))
            R.wait("sync", R.drain_ticks())






def emit_phaseE(R, io, g):
    nc = R.nc
    NRING = 9
    with contextlib.ExitStack() as esE:
        with R.phase("E"):
            ensure_identity(R, g)
            m1g = [esE.enter_context(nc.sbuf_tensor(f"m1gE{i}", [128, 4096], BF16)) for i in range(2)]
            w2g = [esE.enter_context(nc.sbuf_tensor(f"w2gE{i}", [128, 32 * 128], BF16)) for i in range(2)]
            h1 = esE.enter_context(nc.sbuf_tensor("h1E", [128, 32 * 512], BF16))
            accT = esE.enter_context(nc.sbuf_tensor("accTE", [128, 8 * 512], F32))
            xnT = esE.enter_context(nc.sbuf_tensor("xnTE", [128, 8 * 512], BF16))
            wbc = esE.enter_context(nc.sbuf_tensor("wbcE", [128, 8 * 512], BF16))
            g1f = esE.enter_context(nc.sbuf_tensor("g1fE", [128, 8 * 512], BF16))
            g1tok = esE.enter_context(nc.sbuf_tensor("g1tokE", [128, 4 * 1024], BF16))
            b1c = esE.enter_context(nc.sbuf_tensor("b1cE", [128, 8 * 32], F32))
            b2r = esE.enter_context(nc.sbuf_tensor("b2rE", [1, D], BF16))
            wrow = esE.enter_context(nc.sbuf_tensor("wrowE", [1, NE * 512], BF16))
            x1t = [esE.enter_context(nc.sbuf_tensor(f"x1tE{i}", [128, D], F32)) for i in range(2)]
            s1t = [esE.enter_context(nc.sbuf_tensor(f"s1tE{i}", [128, D], BF16)) for i in range(2)]
            outt = [esE.enter_context(nc.sbuf_tensor(f"outtE{i}", [128, D], F32)) for i in range(2)]
            ps_h = [esE.enter_context(nc.psum_tensor(f"ps_hE{i}", [128, 512], F32)) for i in range(2)]
            ps_z = [esE.enter_context(nc.psum_tensor(f"ps_zE{i}", [128, 512], F32)) for i in range(2)]
            ps_tr = esE.enter_context(nc.psum_tensor("ps_trE", [128, 1024], BF16))

            ses = R.dma_sem("Eset")
            t_b1 = []
            for e_ in range(NE):
                t_b1 += col_tile_load(R, b1c[:, e_ * 32:(e_ + 1) * 32], io["mb1"][e_], 32, sem=ses)
            t_b2_free = None

            t_m1g_free = [None, None]
            t_w2g_free = [None, None]
            t_xnT_free = None
            t_wbc_free = None
            t_h1_free = None
            t_ps_h_free = [None, None]
            t_ps_z_free = [None, None]
            t_acc_free = None
            t_g1f_free = None
            t_pstr_free = None
            t_g1tok_free = None
            t_x1_free = [None, None]
            t_s1_free = [None, None]
            t_out_free = [None, None]
            unit = 0
            for ti in range(8):
                dxn = [t_xnT_free] if t_xnT_free else []
                t_xn = [R.dma("sync", xnT[:, kc * 512:(kc + 1) * 512],
                              io["xn1T_s"][kc * 128:(kc + 1) * 128, ti * 512:(ti + 1) * 512],
                              deps=dxn, sem=R.dma_sem("ExnT")) for kc in range(8)]
                dwb = [t_wbc_free] if t_wbc_free else []
                t_wb = []
                t_wr = []
                for e_ in range(NE):
                    wslice = io["w_rows_s"][e_][ti * 512:(ti + 1) * 512]
                    srcap = bass.AP(wslice.tensor, wslice.offset, [[0, 128], [1, 512]])
                    t_wb.append(R.dma("sync", wbc[:, e_ * 512:(e_ + 1) * 512], srcap,
                                      deps=dwb, sem=R.dma_sem("Ewbc")))
                    t_wr.append(R.dma("sync", wrow[0:1, e_ * 512:(e_ + 1) * 512],
                                      wslice.rearrange("(o n) -> o n", o=1), deps=dwb,
                                      sem=R.dma_sem("Ewbc")))
                t_acc_parts = [None] * 8
                for e_ in range(NE):
                    db2 = [t_b2_free] if t_b2_free else []
                    t_b2 = R.dma("gpsimd", b2r[0:1, :], io["mb2"][e_].rearrange("(o n) -> o n", o=1),
                                 deps=db2, sem=R.dma_sem("Eb2"))
                    t_h_parts = []
                    tgl = [None] * 8
                    for g8 in range(2):
                        gb = g8 % 2
                        dg = [t_m1g_free[gb]] if t_m1g_free[gb] else []
                        tgl[g8] = R.dma("sync", m1g[gb][:, :], io["m1g_s"][e_, g8], deps=dg,
                                        sem=R.dma_sem(f"Em1g{gb}"))
                    for g8 in range(8):
                        gb = g8 % 2
                        for fi in range(4):
                            f = 4 * g8 + fi
                            pb = f % 2
                            mm = None
                            for kc in range(8):
                                d0 = []
                                if kc == 0:
                                    d0 = [tgl[g8]] + t_xn
                                    if t_ps_h_free[pb]:
                                        d0.append(t_ps_h_free[pb])
                                    if f < 2 and t_h1_free:
                                        d0.append(t_h1_free)
                                mm = R.op("tensor", lambda e, fi=fi, kc=kc, pb=pb, gb=gb: e.matmul(
                                    ps_h[pb][:, :],
                                    m1g[gb][:, (fi * 8 + kc) * 128:(fi * 8 + kc + 1) * 128],
                                    xnT[:, kc * 512:(kc + 1) * 512],
                                    start=(kc == 0), stop=(kc == 7)),
                                    deps=d0, inc=(kc == 7))
                            if fi == 3:
                                t_m1g_free[gb] = mm
                                if g8 + 2 < 8:
                                    dg = [mm]
                                    tgl[g8 + 2] = R.dma("sync", m1g[gb][:, :],
                                                        io["m1g_s"][e_, g8 + 2], deps=dg,
                                                        sem=R.dma_sem(f"Em1g{gb}"))
                            tev = R.op("scalar", lambda e, f=f, pb=pb, e_=e_: e.activation(
                                h1[:, f * 512:(f + 1) * 512], ps_h[pb][:, :], AF.Relu,
                                bias=b1c[:, e_ * 32 + f:e_ * 32 + f + 1]),
                                deps=[mm] + (t_b1 if unit == 0 else []))
                            t_ps_h_free[pb] = tev
                            tmk = R.op("vector", lambda e, f=f, e_=e_: e.tensor_tensor(
                                h1[:, f * 512:(f + 1) * 512], h1[:, f * 512:(f + 1) * 512],
                                wbc[:, e_ * 512:(e_ + 1) * 512], ALU.mult),
                                deps=[tev, t_wb[e_]])
                            t_h_parts.append(tmk)
                    t_h1_free = t_h_parts[-1]
                    for dc in range(8):
                        pb = dc % 2
                        gb = dc % 2
                        dgw = [t_w2g_free[gb]] if t_w2g_free[gb] else []
                        tw2 = R.dma("scalar", w2g[gb][:, :], io["m2g_s"][e_, dc], deps=dgw,
                                    sem=R.dma_sem(f"Ew2_{gb}"))
                        mm = None
                        for f in range(32):
                            d0 = [t_h_parts[f]]
                            if f == 0:
                                d0.append(tw2)
                                if t_ps_z_free[pb]:
                                    d0.append(t_ps_z_free[pb])
                            mm = R.op("tensor", lambda e, f=f, pb=pb, gb=gb: e.matmul(
                                ps_z[pb][:, :],
                                w2g[gb][:, f * 128:(f + 1) * 128],
                                h1[:, f * 512:(f + 1) * 512],
                                start=(f == 0), stop=False),
                                deps=d0, inc=False)
                        t_w2g_free[gb] = R.op("tensor", lambda e, e_=e_, dc=dc, pb=pb: e.matmul(
                            ps_z[pb][:, :],
                            b2r[0:1, dc * 128:(dc + 1) * 128],
                            wrow[0:1, e_ * 512:(e_ + 1) * 512],
                            start=False, stop=True),
                            deps=[t_b2] + t_wr, inc=True)
                        mm = t_w2g_free[gb]
                        if dc == 7:
                            t_b2_free = mm
                        if e_ == 0:
                            dacc = [mm] + ([t_acc_free] if t_acc_free else [])
                            tac = R.op("vector", lambda e, dc=dc, pb=pb: e.tensor_copy(
                                accT[:, dc * 512:(dc + 1) * 512], ps_z[pb][:, :]), deps=dacc)
                        else:
                            tac = R.op("vector", lambda e, dc=dc, pb=pb: e.tensor_tensor(
                                accT[:, dc * 512:(dc + 1) * 512], accT[:, dc * 512:(dc + 1) * 512],
                                ps_z[pb][:, :], ALU.add), deps=[mm, t_acc_parts[dc]])
                        t_ps_z_free[pb] = tac
                        t_acc_parts[dc] = tac
                    unit += 1
                dg1 = [t_g1f_free] if t_g1f_free else []
                tg1 = R.op("scalar", lambda e: e.activation(g1f[:, :], accT[:, :], AF.Silu),
                           deps=t_acc_parts + dg1)
                t_acc_free = tg1
                for c in range(4):
                    trs = []
                    for dc in range(8):
                        dtr = [tg1, g["t_ident"]]
                        if t_pstr_free:
                            dtr.append(t_pstr_free)
                        trs.append(R.op("tensor", lambda e, dc=dc, c=c: e.transpose(
                            ps_tr[:, dc * 128:(dc + 1) * 128],
                            g1f[:, dc * 512 + c * 128: dc * 512 + (c + 1) * 128],
                            g["ident_bf16"][:, :]), deps=dtr))
                    dgt = [t_g1tok_free] if (c == 0 and t_g1tok_free) else []
                    tev = R.op("vector", lambda e, c=c: e.tensor_copy(
                        g1tok[:, c * 1024:(c + 1) * 1024], ps_tr[:, :]), deps=trs + dgt)
                    t_pstr_free = tev
                    if c == 3:
                        t_g1f_free = trs[-1]
                    gch = ti * 4 + c
                    xb = gch % 2
                    dx1 = [t_x1_free[xb]] if t_x1_free[xb] else []
                    tx1 = R.dma("sync", x1t[xb][:, :], io["x1_s"][gch * 128:(gch + 1) * 128, :],
                                deps=dx1, sem=R.dma_sem(f"Ex1{xb}"))
                    ds1 = [t_s1_free[xb]] if t_s1_free[xb] else []
                    ts1 = R.dma("sync", s1t[xb][:, :], io["s1_s"][gch * 128:(gch + 1) * 128, :],
                                deps=ds1, sem=R.dma_sem(f"Es1{xb}"))
                    dsg = [t_out_free[xb]] if t_out_free[xb] else []
                    tsg = R.op("vector", lambda e, xb=xb, c=c: e.tensor_tensor(
                        outt[xb][:, :], s1t[xb][:, :], g1tok[:, c * 1024:(c + 1) * 1024], ALU.mult),
                        deps=[ts1, tev] + dsg)
                    t_s1_free[xb] = tsg
                    if c == 3:
                        t_g1tok_free = tsg
                    tout = R.op("vector", lambda e, xb=xb: e.tensor_tensor(
                        outt[xb][:, :], outt[xb][:, :], x1t[xb][:, :], ALU.add),
                        deps=[tsg, tx1])
                    t_x1_free[xb] = tout
                    t_out_free[xb] = R.dma("sync", io["out"][gch * 128:(gch + 1) * 128, :],
                                           outt[xb][:, :], deps=[tout], sem=R.dma_sem(f"Eout{xb}"))
                t_xnT_free = mm
                t_wbc_free = t_h_parts[-1]

            R.wait("sync", R.drain_ticks())


EMITTERS.update({"A": emit_phaseA, "B": emit_phaseB, "C": emit_phaseC, "D": emit_phaseD, "E": emit_phaseE})


# version 18
# speedup vs baseline: 1.0118x; 1.0118x over previous
import contextlib
import numpy as np
import concourse.bass as bass
import concourse.mybir as mybir
from concourse.bass_utils import run_bass_kernel_spmd

F32 = mybir.dt.float32
BF16 = mybir.dt.bfloat16
I32 = mybir.dt.int32
AF = mybir.ActivationFunctionType
ALU = mybir.AluOpType

T = 4096
D = 1024
NH = 16
HD = 64
FF = 4096
NE = 8
NC_ = 8
NCH = T // 128
EPS1 = 1e-5
EPS2 = 1e-6

ENGINES = ("sync", "scalar", "vector", "tensor", "gpsimd")
DEBUG_C = False
EXPOSE = []


class Tick:
    __slots__ = ("sem", "val")

    def __init__(self, sem, val):
        self.sem = sem
        self.val = val


class Rec:

    def __init__(self, nc):
        self.nc = nc
        self.es = contextlib.ExitStack()
        self.ops = None
        self.sem = {}
        self.semval = {}
        for eng in ENGINES:
            h = self.es.enter_context(nc.semaphore(f"prog_{eng}"))
            self.sem[eng] = h
            self.semval[eng] = 0
        self.dsem = {}
        self.dsemval = {}
        for q in ("sync", "scalar", "gpsimd", "vector"):
            h = self.es.enter_context(nc.semaphore(f"dma_{q}"))
            self.dsem[q] = h
            self.dsemval[q] = 0
        self.waited = {eng: {} for eng in ENGINES}
        self.named_dsems = {}

    def dma_sem(self, name):
        if name not in self.named_dsems:
            h = self.es.enter_context(self.nc.semaphore(f"dsem_{name}"))
            self.named_dsems[name] = [h, 0]
        return name

    def phase(self, name):
        return _Phase(self, name)

    def _add_waits(self, eng, deps):
        fns = []
        seen = self.waited[eng]
        best = {}
        for t in deps:
            if t is None:
                continue
            key = id(t.sem)
            if t.val > best.get(key, (0, None))[0]:
                best[key] = (t.val, t.sem)
        for key, (val, sem) in best.items():
            if seen.get(key, 0) >= val:
                continue
            seen[key] = val
            fns.append((sem, val))
        return fns

    def op(self, eng, fn, deps=(), inc=True):
        waits = self._add_waits(eng, deps)
        tick = None
        if inc:
            self.semval[eng] += 1
            tick = Tick(self.sem[eng], self.semval[eng])

        sem = self.sem[eng]

        def run(e, waits=waits, fn=fn, tick=tick, sem=sem):
            for (s, v) in waits:
                e.wait_ge(s, v)
            inst = fn(e)
            if tick is not None:
                inst.then_inc(sem, 1)

        self.ops[eng].append(run)
        return tick

    def raw(self, eng, fn, deps=()):
        return self.op(eng, fn, deps, inc=False)

    def dma(self, q, out, in_, deps=(), sem=None, **kw):
        waits = self._add_waits(q, deps)
        if sem is None:
            self.dsemval[q] += 16
            tick = Tick(self.dsem[q], self.dsemval[q])
            sem = self.dsem[q]
        else:
            rec = self.named_dsems[sem]
            rec[1] += 16
            tick = Tick(rec[0], rec[1])
            sem = rec[0]

        def run(e, waits=waits, out=out, in_=in_, kw=kw, sem=sem):
            for (s, v) in waits:
                e.wait_ge(s, v)
            e.dma_start(out, in_, **kw).then_inc(sem, 16)

        self.ops[q].append(run)
        return tick

    def drain_ticks(self):
        ticks = [Tick(self.dsem[q], self.dsemval[q]) for q in self.dsem]
        ticks += [Tick(h, v) for (h, v) in self.named_dsems.values()]
        return ticks

    def wait(self, eng, deps):
        waits = self._add_waits(eng, deps)
        if not waits:
            return

        def run(e, waits=waits):
            for (s, v) in waits:
                e.wait_ge(s, v)

        self.ops[eng].append(run)


class _Phase:
    def __init__(self, rec, name):
        self.rec = rec
        self.name = name

    def __enter__(self):
        self.rec.ops = {eng: [] for eng in ENGINES}
        return self.rec

    def __exit__(self, et, ev, tb):
        if et is not None:
            return False
        rec = self.rec
        nc = rec.nc
        with nc.Block(self.name) as blk:
            for eng in ENGINES:
                lst = rec.ops[eng]
                if not lst:
                    continue

                def body(e, lst=lst):
                    for fn in lst:
                        fn(e)

                getattr(blk, eng)(body)
        rec.ops = None
        return False



def bcast_row(R, sb_tile, dram_ap, n, deps=(), sem=None):
    src = bass.AP(dram_ap.tensor, dram_ap.offset, [[0, 128], [1, n]])
    return R.dma("sync", sb_tile[:, :n], src, deps=deps, sem=sem)


def col_tile_load(R, sb_tile, dram_vec, nchunk, deps=(), sem=None):
    ticks = []
    for c in range(nchunk):
        ticks.append(R.dma("sync", sb_tile[:, c:c + 1],
                           dram_vec[c * 128:(c + 1) * 128].rearrange("(p o) -> p o", o=1),
                           deps=deps, sem=sem))
    return ticks


def build_identity(R, ident_f32, ident_bf16, io_f, io_p):
    t1 = R.op("gpsimd", lambda e: e.iota(io_f[:, :], [[1, 128]], channel_multiplier=0,
                                         allow_small_or_imprecise_dtypes=True))
    t2 = R.op("gpsimd", lambda e: e.iota(io_p[:, :], [[0, 128]], channel_multiplier=1,
                                         allow_small_or_imprecise_dtypes=True))
    t3 = R.op("vector", lambda e: e.tensor_tensor(ident_f32[:, :], io_f[:, :], io_p[:, :],
                                                  ALU.is_equal), deps=[t1, t2])
    t4 = R.op("vector", lambda e: e.tensor_copy(ident_bf16[:, :], ident_f32[:, :]), deps=[t3])
    return t3, t4



SCRATCH = {
    "xn0_s": ([T, D], F32, "A"),
    "qT_s": ([D, T], BF16, "A"),
    "kT_s": ([D, T], BF16, "A"),
    "v2_s": ([NH, 128, NCH * 65], BF16, "A"),
    "woT_s": ([128, 8 * 1024], BF16, "A"),
    "w1g_s": ([8, 128, 32 * 128], BF16, "A"),
    "w2g_s": ([8, 128, 32 * 128], BF16, "A"),
    "wzbf_s": ([D, D], BF16, "A"),
    "whbf_s": ([D, D], BF16, "A"),
    "m1g_s": ([NE, 8, 128, 4 * 8 * 128], BF16, "A"),
    "m2g_s": ([NE, 8, 128, 32 * 128], BF16, "A"),
    "ln3_s": ([128, 64], F32, "A"),
    "xln3T_s": ([D, T], BF16, "A"),
    "z05_s": ([8, 128, 8 * 512], BF16, "B"),
    "rrow_s": ([2, 512], F32, "B"),
    "attnT_s": ([D, T], BF16, "B"),
    "x1_s": ([T, D], F32, "C"),
    "xn1T_s": ([D, T], BF16, "D"),
    "s1_s": ([T, D], BF16, "D"),
    "w_rows_s": ([NE, T], BF16, "D"),
    "y_s": ([T, D], F32, "D"),
    "dbg1_s": ([T, D], F32, "C"),
    "dbg2_s": ([T, D], F32, "C"),
    "dbgy_s": ([128, 32 * 512], BF16, "C"),
    "dbgz_s": ([128, 8 * 512], BF16, "C"),
    "dbgxl_s": ([128, 8 * 512], BF16, "C"),
    "dbgz2_s": ([128, 8 * 512], F32, "C"),
}

INPUTS = {
    "x": ([T, D], F32),
    "ln0_g": ([D], F32), "ln0_b": ([D], F32),
    "wq": ([D, D], F32), "bq": ([D], F32),
    "wk": ([D, D], F32), "bk": ([D], F32),
    "wv": ([D, D], F32), "bv": ([D], F32),
    "wo": ([D, D], F32), "bo": ([D], F32),
    "mu": ([1], F32),
    "lnf_g": ([D], F32), "lnf_b": ([D], F32),
    "fc1_q": ([D, FF], I32), "fc1_s": ([8, FF], F32), "fc1_b": ([FF], F32),
    "fc2_q": ([FF, D], I32), "fc2_s": ([32, D], F32), "fc2_b": ([D], F32),
    "wz": ([D, D], F32), "bz": ([D], F32),
    "wh": ([D, D], F32), "bh": ([D], F32),
    "sw_beta": ([1], F32),
    "lng_g": ([D], F32), "lng_b": ([D], F32),
    "mw1": ([NE, D, FF], F32), "mb1": ([NE, FF], F32),
    "mw2": ([NE, FF, D], F32), "mb2": ([NE, D], F32),
}

PHASE_INPUTS = {
    "A": ["x", "ln0_g", "ln0_b", "wq", "bq", "wk", "bk", "wv", "bv", "wo",
          "fc1_q", "fc1_s", "fc2_q", "fc2_s", "wz", "wh", "mw1", "mw2", "lnf_g", "lnf_b"],
    "B": ["mu", "fc1_b", "fc2_b"],
    "C": ["x", "bo"],
    "D": ["bz", "bh", "sw_beta", "lng_g", "lng_b"],
    "E": ["mb1", "mb2"],
}
PHASE_SCRATCH_R = {
    "A": [],
    "B": ["qT_s", "kT_s", "v2_s", "xln3T_s", "w1g_s", "w2g_s"],
    "C": ["xn0_s", "attnT_s", "woT_s", "z05_s"],
    "D": ["x1_s", "wzbf_s", "whbf_s"],
    "E": ["x1_s", "xn1T_s", "s1_s", "w_rows_s", "m1g_s", "m2g_s"],
}
OUTPUT = {"out": ([T, D], F32)}


def declare_io(nc, phases, fused):
    io = {}
    ins = set()
    for p in phases:
        ins.update(PHASE_INPUTS[p])
    for name in sorted(ins):
        shape, dt = INPUTS[name]
        io[name] = nc.dram_tensor(name, shape, dt, kind="ExternalInput").ap()
    for name, (shape, dt, wr) in SCRATCH.items():
        if name.startswith("dbg") and not DEBUG_C:
            continue
        if fused:
            kind = "ExternalOutput" if name in EXPOSE else "Internal"
            used = wr in phases or any(name in PHASE_SCRATCH_R[p] for p in phases)
            if not used:
                continue
        else:
            r = any(name in PHASE_SCRATCH_R[p] for p in phases)
            w = wr in phases
            if w:
                kind = "ExternalOutput"
            elif r:
                kind = "ExternalInput"
            else:
                continue
        io[name] = nc.dram_tensor(name, shape, dt, kind=kind).ap()
    if "E" in phases:
        io["out"] = nc.dram_tensor("out", [T, D], F32, kind="ExternalOutput").ap()
    return io



def const_col(R, g, val):
    key = ("ccol", float(val))
    if key in g:
        return g[key]
    tile = g["es"].enter_context(R.nc.sbuf_tensor(f"ccol_{len(g)}", [128, 1], F32, side="right"))
    t = R.op("vector", lambda e: e.memset(tile[:, :], float(val)))
    g[key] = (tile, t)
    return g[key]


def ln_stats(R, x_ap, x_tick, stat, eps, act_sq_scratch, n=D, deps=()):
    deps = list(deps)
    tsum = R.op("vector", lambda e: e.reduce_sum(stat[:, 0:1], x_ap, axis=mybir.AxisListType.X),
                deps=[x_tick] + deps)
    tssq = R.op("scalar", lambda e: e.activation(act_sq_scratch[:, :n], x_ap, AF.Square,
                                                 accum_out=stat[:, 1:2]), deps=[x_tick] + deps)
    tm = R.op("vector", lambda e: e.tensor_single_scalar(stat[:, 2:3], stat[:, 0:1], 1.0 / n, ALU.mult),
              deps=[tsum])
    te1 = R.op("vector", lambda e: e.tensor_single_scalar(stat[:, 3:4], stat[:, 1:2], 1.0 / n, ALU.mult),
               deps=[tssq])
    tmm = R.op("vector", lambda e: e.tensor_tensor(stat[:, 4:5], stat[:, 2:3], stat[:, 2:3], ALU.mult),
               deps=[tm])
    tvar = R.op("vector", lambda e: e.tensor_tensor(stat[:, 5:6], stat[:, 3:4], stat[:, 4:5], ALU.subtract),
                deps=[te1, tmm])
    eps_tile, eps_tick = eps
    tsd = R.op("scalar", lambda e: e.activation(stat[:, 6:7], stat[:, 5:6], AF.Sqrt,
                                                bias=eps_tile[:, 0:1]),
               deps=[tvar, eps_tick])
    tr = R.op("vector", lambda e: e.reciprocal(stat[:, 6:7], stat[:, 6:7]), deps=[tsd])
    tnmr = R.op("vector", lambda e: e.scalar_tensor_tensor(stat[:, 7:8], stat[:, 2:3], -1.0,
                                                           stat[:, 6:7], ALU.mult, ALU.mult),
                deps=[tm, tr])
    return tr, tnmr



def ensure_identity(R, g):
    if "ident_bf16" in g:
        return
    nc = R.nc
    ges = g["es"]
    g["ident_f32"] = ges.enter_context(nc.sbuf_tensor("ident_f32", [128, 128], F32, side="right"))
    g["ident_bf16"] = ges.enter_context(nc.sbuf_tensor("ident_bf16", [128, 128], BF16, side="right"))
    io_f = ges.enter_context(nc.sbuf_tensor("io_f", [128, 128], F32, side="right"))
    io_p = ges.enter_context(nc.sbuf_tensor("io_p", [128, 128], F32, side="right"))
    build_identity(R, g["ident_f32"], g["ident_bf16"], io_f, io_p)
    g["t_ident"] = R.op("vector", lambda e: e.engine_nop())


def emit_phaseA(R, io, g):
    nc = R.nc
    esA = contextlib.ExitStack()

    with esA:
        es0 = contextlib.ExitStack()
        wqT = esA.enter_context(nc.sbuf_tensor("wqT", [128, 8 * 1024], BF16))
        wkT = esA.enter_context(nc.sbuf_tensor("wkT", [128, 8 * 1024], BF16))
        wvT = esA.enter_context(nc.sbuf_tensor("wvT", [128, 8 * 1024], BF16))
        bqc = esA.enter_context(nc.sbuf_tensor("bqc", [128, 8], F32))
        bk8 = esA.enter_context(nc.sbuf_tensor("bk8", [128, 8], F32))
        bvb = esA.enter_context(nc.sbuf_tensor("bvb", [128, D], F32))
        g0b = esA.enter_context(nc.sbuf_tensor("g0b", [128, D], F32))
        b0b = esA.enter_context(nc.sbuf_tensor("b0b", [128, D], F32))
        g2b = esA.enter_context(nc.sbuf_tensor("g2bA", [128, D], F32))
        b2b = esA.enter_context(nc.sbuf_tensor("b2bA", [128, D], F32))

        with R.phase("A0"), es0:
            ensure_identity(R, g)

            if not g.get("defer_prep"):
                g["t_wzbf"] = R.dma("gpsimd", io["wzbf_s"], io["wz"])
                g["t_whbf"] = R.dma("gpsimd", io["whbf_s"], io["wh"])
                mg = es0.enter_context(nc.sbuf_tensor("m1cA0", [128, FF], BF16))
                pm = None
                for e_ in range(NE):
                    for kc in range(8):
                        tld = R.dma("gpsimd", mg[:, :], io["mw1"][e_, kc * 128:(kc + 1) * 128, :],
                                    deps=[pm] if pm else [], sem=R.dma_sem("Am1g"))
                        for g8 in range(8):
                            for fi in range(4):
                                pm = R.dma("gpsimd",
                                           io["m1g_s"][e_, g8][:, (fi * 8 + kc) * 128:(fi * 8 + kc + 1) * 128],
                                           mg[:, (4 * g8 + fi) * 128:(4 * g8 + fi + 1) * 128],
                                           deps=[tld], sem=R.dma_sem("Am1gst"))

            sset = R.dma_sem("Aset")
            t_g0 = bcast_row(R, g0b, io["ln0_g"], D, sem=sset)
            t_b0 = bcast_row(R, b0b, io["ln0_b"], D, sem=sset)
            t_bv = bcast_row(R, bvb, io["bv"], D, sem=sset)
            t_g2 = bcast_row(R, g2b, io["lnf_g"], D, sem=sset)
            t_b2f = bcast_row(R, b2b, io["lnf_b"], D, sem=sset)
            t_bq = col_tile_load(R, bqc, io["bq"], 8, sem=sset)
            t_bkc = col_tile_load(R, bk8, io["bk"], 8, sem=sset)
            t_bk8 = R.op("vector", lambda e: e.tensor_single_scalar(bk8[:, :], bk8[:, :], 0.125, ALU.mult),
                         deps=t_bkc)
            g["t_ln0"] = [t_g0, t_b0]
            g["t_ln3gb"] = [t_g2, t_b2f]
            g["t_bv"] = t_bv
            g["t_bq"] = t_bq
            g["t_bk8"] = t_bk8

            wstage = [es0.enter_context(nc.sbuf_tensor(f"wstage{i}", [128, 1024], BF16)) for i in range(2)]
            wo_stage = es0.enter_context(nc.sbuf_tensor("wo_stage", [128, 1024], BF16))
            ps_tr = es0.enter_context(nc.psum_tensor("ps_tr", [128, 1024], BF16))
            last_evac = [None, None]
            ps_free = None
            wo_dma = None
            for wi, (wname, dest) in enumerate([("wq", wqT), ("wk", wkT), ("wv", wvT), ("wo", None)]):
                for ic in range(8):
                    buf = (wi * 8 + ic) % 2
                    tld = R.dma("gpsimd", wstage[buf][:, :], io[wname][ic * 128:(ic + 1) * 128, :],
                                deps=[last_evac[buf]] if last_evac[buf] else [],
                                sem=R.dma_sem(f"Aws{buf}"))
                    trs = []
                    for kc in range(8):
                        trs.append(R.op("tensor", lambda e, buf=buf, kc=kc: e.transpose(
                            ps_tr[:, kc * 128:(kc + 1) * 128],
                            wstage[buf][:, kc * 128:(kc + 1) * 128], g["ident_bf16"][:, :]),
                            deps=[tld, g["t_ident"], ps_free] if ps_free else [tld, g["t_ident"]]))
                    last_evac[buf] = trs[-1]
                    if dest is not None:
                        tev = R.op("vector", lambda e, dest=dest, ic=ic: e.tensor_copy(
                            dest.ap().rearrange("p (kc i) -> p kc i", kc=8)[:, :, ic * 128:(ic + 1) * 128],
                            ps_tr.ap().rearrange("p (kc i) -> p kc i", kc=8)),
                            deps=trs)
                        ps_free = tev
                    else:
                        dwo = trs + ([wo_dma] if wo_dma else [])
                        tev = R.op("vector", lambda e, ic=ic: e.tensor_copy(
                            wo_stage[:, :], ps_tr[:, :]), deps=dwo)
                        ps_free = tev
                        wo_dma = R.dma("sync", io["woT_s"].rearrange("p (kc i) -> p kc i", kc=8)[:, :, ic * 128:(ic + 1) * 128],
                              wo_stage.ap().rearrange("p (kc i) -> p kc i", kc=8), deps=[tev],
                              sem=R.dma_sem("Awo"))
            g["t_wqkv"] = [last_evac[0], last_evac[1], ps_free]

            if not g.get("defer_prep"):
                qi1 = [es0.enter_context(nc.sbuf_tensor(f"qi1_{i}", [128, FF], I32)) for i in range(2)]
                sc1 = [es0.enter_context(nc.sbuf_tensor(f"sc1_{i}", [128, FF], F32)) for i in range(2)]
                wout1 = [es0.enter_context(nc.sbuf_tensor(f"wout1_{i}", [128, FF], BF16)) for i in range(2)]
                prev = [None, None]
                for kc in range(8):
                    buf = kc % 2
                    deps = [prev[buf]] if prev[buf] else []
                    tq = R.dma("sync", qi1[buf][:, :], io["fc1_q"][kc * 128:(kc + 1) * 128, :], deps=deps,
                               sem=R.dma_sem(f"Adq{buf}"))
                    tsc = bcast_row(R, sc1[buf], io["fc1_s"][kc], FF, deps=deps, sem=R.dma_sem(f"Adq{buf}"))
                    tw = R.op("vector", lambda e, buf=buf: e.scalar_tensor_tensor(
                        wout1[buf][:, :], qi1[buf][:, :], -128.0, sc1[buf][:, :], ALU.add, ALU.mult),
                        deps=[tq, tsc])
                    last = None
                    for g8 in range(8):
                        for fi in range(4):
                            last = R.dma("sync",
                                         io["w1g_s"][g8][:, (fi * 8 + kc) * 128:(fi * 8 + kc + 1) * 128],
                                         wout1[buf][:, (4 * g8 + fi) * 128:(4 * g8 + fi + 1) * 128],
                                         deps=[tw], sem=R.dma_sem(f"Aw1st{buf}"))
                    prev[buf] = last
                qi2 = [es0.enter_context(nc.sbuf_tensor(f"qi2_{i}", [128, D], I32)) for i in range(2)]
                sc2 = [es0.enter_context(nc.sbuf_tensor(f"sc2_{i}", [128, D], F32)) for i in range(2)]
                wout2 = [es0.enter_context(nc.sbuf_tensor(f"wout2_{i}", [128, D], BF16)) for i in range(2)]
                prev = [None, None]
                for fc in range(32):
                    buf = fc % 2
                    deps = [prev[buf]] if prev[buf] else []
                    tq = R.dma("sync", qi2[buf][:, :], io["fc2_q"][fc * 128:(fc + 1) * 128, :], deps=deps,
                               sem=R.dma_sem(f"Adq2{buf}"))
                    tsc = bcast_row(R, sc2[buf], io["fc2_s"][fc], D, deps=deps, sem=R.dma_sem(f"Adq2{buf}"))
                    tw = R.op("vector", lambda e, buf=buf: e.scalar_tensor_tensor(
                        wout2[buf][:, :], qi2[buf][:, :], -128.0, sc2[buf][:, :], ALU.add, ALU.mult),
                        deps=[tq, tsc])
                    last = None
                    for dc in range(8):
                        last = R.dma("sync", io["w2g_s"][dc][:, fc * 128:(fc + 1) * 128],
                                     wout2[buf][:, dc * 128:(dc + 1) * 128], deps=[tw],
                                     sem=R.dma_sem(f"Aw2st{buf}"))
                    prev[buf] = last

                m2st = [es0.enter_context(nc.sbuf_tensor(f"m2st{i}", [128, D], F32)) for i in range(2)]
                m2bf = [es0.enter_context(nc.sbuf_tensor(f"m2bf{i}", [128, D], BF16)) for i in range(2)]
                prev = [None, None]
                for e_ in range(NE):
                    for fc in range(32):
                        buf = (e_ * 32 + fc) % 2
                        deps = [prev[buf]] if prev[buf] else []
                        tld = R.dma("sync", m2st[buf][:, :], io["mw2"][e_, fc * 128:(fc + 1) * 128, :],
                                    deps=deps, sem=R.dma_sem(f"Am2{buf}"))
                        tcv = R.op("gpsimd", lambda e, buf=buf: e.tensor_copy(m2bf[buf][:, :], m2st[buf][:, :]),
                                   deps=[tld])
                        last = None
                        for dc in range(8):
                            last = R.dma("scalar", io["m2g_s"][e_, dc][:, fc * 128:(fc + 1) * 128],
                                         m2bf[buf][:, dc * 128:(dc + 1) * 128], deps=[tcv],
                                         sem=R.dma_sem(f"Am2st{buf}"))
                        prev[buf] = last


        with R.phase("A"):
            HQ = FF // 4
            qi1d = [esA.enter_context(nc.sbuf_tensor(f"qi1d{j}", [128, HQ], I32)) for j in range(2)]
            sc1d = [esA.enter_context(nc.sbuf_tensor(f"sc1d{j}", [128, HQ], F32)) for j in range(2)]
            wo1d = [esA.enter_context(nc.sbuf_tensor(f"wo1d{j}", [128, HQ], BF16)) for j in range(2)]
            qi2d = [esA.enter_context(nc.sbuf_tensor(f"qi2d{j}", [128, D], I32)) for j in range(2)]
            sc2d = [esA.enter_context(nc.sbuf_tensor(f"sc2d{j}", [128, D], F32)) for j in range(2)]
            wo2d = [esA.enter_context(nc.sbuf_tensor(f"wo2d{j}", [128, D], BF16)) for j in range(2)]
            dq_prev = [[None, None], [None, None]]

            def dequant1(idx):
                kc, hh = idx // 4, idx % 4
                j = idx % 2
                deps = [dq_prev[0][j]] if dq_prev[0][j] else []
                tq = R.dma("gpsimd", qi1d[j][:, :],
                           io["fc1_q"][kc * 128:(kc + 1) * 128, hh * HQ:(hh + 1) * HQ],
                           deps=deps, sem=R.dma_sem(f"Adq1_{j}"))
                sr = io["fc1_s"][kc][hh * HQ:(hh + 1) * HQ]
                tsc = R.dma("gpsimd", sc1d[j][:, :], bass.AP(sr.tensor, sr.offset, [[0, 128], [1, HQ]]),
                            deps=deps, sem=R.dma_sem(f"Adq1_{j}"))
                tw = R.op("vector", lambda e, j=j: e.scalar_tensor_tensor(
                    wo1d[j][:, :], qi1d[j][:, :], -128.0, sc1d[j][:, :], ALU.add, ALU.mult),
                    deps=[tq, tsc])
                last = None
                for g8 in range(2 * hh, 2 * hh + 2):
                    fl0 = 4 * g8 - 8 * hh
                    last = R.dma("scalar",
                                 io["w1g_s"][g8].rearrange("p (fi kq) -> p fi kq", fi=4)[:, :, kc * 128:(kc + 1) * 128],
                                 wo1d[j][:, fl0 * 128:(fl0 + 4) * 128].rearrange("p (fi q) -> p fi q", fi=4),
                                 deps=[tw], sem=R.dma_sem(f"Adq1st{j}"))
                dq_prev[0][j] = last

            def dequant2(fc):
                j = fc % 2
                deps = [dq_prev[1][j]] if dq_prev[1][j] else []
                tq = R.dma("gpsimd", qi2d[j][:, :], io["fc2_q"][fc * 128:(fc + 1) * 128, :], deps=deps,
                           sem=R.dma_sem(f"Adq2_{j}"))
                sr = io["fc2_s"][fc]
                tsc = R.dma("gpsimd", sc2d[j][:, :], bass.AP(sr.tensor, sr.offset, [[0, 128], [1, D]]),
                            deps=deps, sem=R.dma_sem(f"Adq2_{j}"))
                tw = R.op("vector", lambda e, j=j: e.scalar_tensor_tensor(
                    wo2d[j][:, :], qi2d[j][:, :], -128.0, sc2d[j][:, :], ALU.add, ALU.mult),
                    deps=[tq, tsc])
                last = R.dma("scalar",
                             io["w2g_s"].rearrange("dc p n -> p dc n")[:, :, fc * 128:(fc + 1) * 128],
                             wo2d[j][:, :].rearrange("p (dc q) -> p dc q", dc=8),
                             deps=[tw], sem=R.dma_sem(f"Adq2st{j}"))
                dq_prev[1][j] = last

            def dq_hook(i):
                for k in range(4):
                    idx = i * 4 + k
                    if idx < 32:
                        dequant1(idx)
                for k in range(4):
                    fc = i * 4 + k
                    if fc < 32:
                        dequant2(fc)

            ln3p = esA.enter_context(nc.sbuf_tensor("ln3p", [128, 64], F32))
            xb = [esA.enter_context(nc.sbuf_tensor(f"xA{i}", [128, D], F32)) for i in range(2)]
            sqs = esA.enter_context(nc.sbuf_tensor("sqsA", [128, D], F32))
            st1 = [esA.enter_context(nc.sbuf_tensor(f"st1A{i}", [128, 8], F32)) for i in range(2)]
            st2 = [esA.enter_context(nc.sbuf_tensor(f"st2A{i}", [128, 8], F32)) for i in range(2)]
            xn0b = [esA.enter_context(nc.sbuf_tensor(f"xn0A{i}", [128, D], F32)) for i in range(2)]
            tmp1 = [esA.enter_context(nc.sbuf_tensor(f"tmp1A{i}", [128, D], F32)) for i in range(2)]
            xlnb = [esA.enter_context(nc.sbuf_tensor(f"xlnA{i}", [128, D], BF16)) for i in range(2)]
            xlnT = [esA.enter_context(nc.sbuf_tensor(f"xlnTA{i}", [128, 8 * 512], BF16)) for i in range(2)]
            ps_tA = esA.enter_context(nc.psum_tensor("ps_tA", [128, 1024], BF16))
            ps_qk = [esA.enter_context(nc.psum_tensor(f"ps_qk{i}", [128, 512], F32)) for i in range(2)]
            ps_v = [esA.enter_context(nc.psum_tensor(f"ps_v{i}", [128, 1024], F32)) for i in range(2)]
            qs = [esA.enter_context(nc.sbuf_tensor(f"qsA{i}", [128, 512], BF16)) for i in range(2)]
            tmp2 = [esA.enter_context(nc.sbuf_tensor(f"tmp2A{i}", [128, D], F32)) for i in range(2)]
            xl3b = [esA.enter_context(nc.sbuf_tensor(f"xl3bA{i}", [128, D], BF16)) for i in range(2)]
            xl3st = [esA.enter_context(nc.sbuf_tensor(f"xl3stA{i}", [128, 8 * 512], BF16)) for i in range(2)]
            vst = [esA.enter_context(nc.sbuf_tensor(f"vstA{i}", [128, NH * 65], BF16)) for i in range(2)]

            t_vones = [R.op("vector", lambda e, b=b: e.memset(
                vst[b].ap().rearrange("p (h s) -> p h s", h=NH)[:, :, 64:65], 1.0)) for b in range(2)]

            t_x_free = [None, None]
            t_xn0_free = [[], []]
            t_st1_free = [[], []]
            t_st2_free = [[], []]
            t_tmp_free = [[], []]
            t_tmp2_free = [[], []]
            t_xl3b_free = [None, None]
            t_xl3st_free = [None, None]
            t_xln_free = [None, None]
            t_xlnT_free = [None, None]
            t_pstA_free = None
            t_psqk_free = [None, None]
            t_psv_free = [None, None]
            t_qs_free = [None, None]
            t_vst_free = [t_vones[0], t_vones[1]]

            ln0deps = g["t_ln0"]
            for i in range(8):
                dq_hook(i)
                tb = i % 2
                t_chunks = []
                for c in range(4):
                    gch = 4 * i + c
                    buf = gch % 2
                    dl = [t_x_free[buf]] if t_x_free[buf] else []
                    tx = R.dma("sync", xb[buf][:, :], io["x"][gch * 128:(gch + 1) * 128, :], deps=dl,
                               sem=R.dma_sem(f"Axb{buf}"))
                    r1, n1 = ln_stats(R, xb[buf][:, :], tx, st1[buf], const_col(R, g, EPS1), sqs,
                                      deps=t_st1_free[buf])
                    txn0 = R.op("scalar", lambda e, buf=buf: e.activation(
                        xn0b[buf][:, :], xb[buf][:, :], AF.Identity,
                        bias=st1[buf][:, 7:8], scale=st1[buf][:, 6:7]),
                        deps=[r1, n1, tx] + t_xn0_free[buf])
                    t_x_free[buf] = txn0
                    t_st1_free[buf] = [txn0]
                    txn0_st = R.dma("sync", io["xn0_s"][gch * 128:(gch + 1) * 128, :],
                                    xn0b[buf][:, :], deps=[txn0], sem=R.dma_sem(f"Axn0st{buf}"))
                    r2, n2 = ln_stats(R, xn0b[buf][:, :], txn0, st2[buf], const_col(R, g, EPS2), sqs,
                                      deps=t_st2_free[buf])
                    tt1 = R.op("scalar", lambda e, buf=buf: e.activation(
                        tmp1[buf][:, :], xn0b[buf][:, :], AF.Identity,
                        bias=st2[buf][:, 7:8], scale=st2[buf][:, 6:7]),
                        deps=[r2, n2] + t_tmp_free[buf])
                    eps1c = const_col(R, g, EPS1)
                    tsd3 = R.op("scalar", lambda e, buf=buf, gch=gch: e.activation(
                        ln3p[:, gch:gch + 1], st2[buf][:, 5:6], AF.Sqrt, bias=eps1c[0][:, 0:1]),
                        deps=[r2, eps1c[1]])
                    tr3 = R.op("vector", lambda e, gch=gch: e.reciprocal(
                        ln3p[:, gch:gch + 1], ln3p[:, gch:gch + 1]), deps=[tsd3])
                    tn3 = R.op("vector", lambda e, buf=buf, gch=gch: e.scalar_tensor_tensor(
                        ln3p[:, 32 + gch:33 + gch], st2[buf][:, 2:3], -1.0,
                        ln3p[:, gch:gch + 1], ALU.mult, ALU.mult), deps=[tr3])
                    t_st2_free[buf] = [tt1, tn3]
                    t_xn0_free[buf] = [tt1, txn0_st]
                    tt2 = R.op("vector", lambda e, buf=buf: e.tensor_tensor(
                        tmp1[buf][:, :], tmp1[buf][:, :], g0b[:, :], ALU.mult),
                        deps=[tt1] + ln0deps)
                    dxl = [t_xln_free[buf]] if t_xln_free[buf] else []
                    tt3 = R.op("vector", lambda e, buf=buf: e.tensor_tensor(
                        xlnb[buf][:, :], tmp1[buf][:, :], b0b[:, :], ALU.add),
                        deps=[tt2] + dxl)
                    t_tmp_free[buf] = [tt3]
                    trs = []
                    for kc in range(8):
                        dtr = [tt3, g["t_ident"]]
                        if t_pstA_free:
                            dtr.append(t_pstA_free)
                        trs.append(R.op("tensor", lambda e, buf=buf, kc=kc: e.transpose(
                            ps_tA[:, kc * 128:(kc + 1) * 128],
                            xlnb[buf][:, kc * 128:(kc + 1) * 128], g["ident_bf16"][:, :]),
                            deps=dtr))
                    t_xln_free[buf] = trs[-1]
                    dxT = [t_xlnT_free[tb]] if (c == 0 and t_xlnT_free[tb]) else []
                    tev = R.op("vector", lambda e, tb=tb, c=c: e.tensor_copy(
                        xlnT[tb].ap().rearrange("p (kc t) -> p kc t", kc=8)[:, :, c * 128:(c + 1) * 128],
                        ps_tA.ap().rearrange("p (kc t) -> p kc t", kc=8)),
                        deps=trs + dxT)
                    t_pstA_free = tev
                    t_chunks.append(tev)

                    tb1 = R.op("scalar", lambda e, buf=buf, gch=gch: e.activation(
                        tmp2[buf][:, :], xn0b[buf][:, :], AF.Identity,
                        bias=ln3p[:, 32 + gch:33 + gch], scale=ln3p[:, gch:gch + 1]),
                        deps=[tn3, txn0] + t_tmp2_free[buf])
                    t_xn0_free[buf] = t_xn0_free[buf] + [tb1]
                    tb2 = R.op("vector", lambda e, buf=buf: e.tensor_tensor(
                        tmp2[buf][:, :], tmp2[buf][:, :], g2b[:, :], ALU.mult),
                        deps=[tb1] + g["t_ln3gb"])
                    dx3 = [t_xl3b_free[buf]] if t_xl3b_free[buf] else []
                    tb3 = R.op("vector", lambda e, buf=buf: e.tensor_tensor(
                        xl3b[buf][:, :], tmp2[buf][:, :], b2b[:, :], ALU.add),
                        deps=[tb2] + dx3)
                    t_tmp2_free[buf] = [tb3]
                    trs3 = []
                    for kc in range(8):
                        dtr = [tb3, g["t_ident"], t_pstA_free]
                        trs3.append(R.op("tensor", lambda e, buf=buf, kc=kc: e.transpose(
                            ps_tA[:, kc * 128:(kc + 1) * 128],
                            xl3b[buf][:, kc * 128:(kc + 1) * 128], g["ident_bf16"][:, :]),
                            deps=dtr))
                    t_xl3b_free[buf] = trs3[-1]
                    sb2 = i % 2
                    d3s = [t_xl3st_free[sb2]] if (c == 0 and t_xl3st_free[sb2]) else []
                    tev3 = R.op("vector", lambda e, sb2=sb2, c=c: e.tensor_copy(
                        xl3st[sb2].ap().rearrange("p (kc t) -> p kc t", kc=8)[:, :, c * 128:(c + 1) * 128],
                        ps_tA.ap().rearrange("p (kc t) -> p kc t", kc=8)), deps=trs3 + d3s)
                    t_pstA_free = tev3
                    if c == 3:
                        t_xl3st_free[sb2] = R.dma(
                            "sync",
                            io["xln3T_s"].rearrange("(kc p) t -> p kc t", kc=8)[:, :, i * 512:(i + 1) * 512],
                            xl3st[sb2].ap().rearrange("p (kc t) -> p kc t", kc=8),
                            deps=[tev3], sem=R.dma_sem(f"Axl3{sb2}"))

                for (wT, bias_col, scale, dst) in [(wqT, bqc, 1.0, "qT_s"), (wkT, bk8, 0.125, "kT_s")]:
                    for ic in range(8):
                        pb = ic % 2
                        mm_last = None
                        for kc in range(8):
                            dmm = t_chunks + g["t_wqkv"] if kc == 0 else []
                            if kc == 0 and t_psqk_free[pb]:
                                dmm = dmm + [t_psqk_free[pb]]
                            mm_last = R.op("tensor", lambda e, wT=wT, ic=ic, kc=kc, pb=pb, tb=tb: e.matmul(
                                ps_qk[pb][:, :],
                                wT.ap().rearrange("p (kc i) -> p kc i", kc=8)[:, kc, ic * 128:(ic + 1) * 128],
                                xlnT[tb][:, kc * 512:(kc + 1) * 512],
                                start=(kc == 0), stop=(kc == 7)),
                                deps=dmm, inc=(kc == 7))
                        dq = [mm_last] + (g["t_bq"] if dst == "qT_s" else [g["t_bk8"]])
                        if t_qs_free[pb]:
                            dq.append(t_qs_free[pb])
                        tev = R.op("scalar", lambda e, pb=pb, ic=ic, bias_col=bias_col, scale=scale: e.activation(
                            qs[pb][:, :], ps_qk[pb][:, :], AF.Identity,
                            bias=bias_col[:, ic:ic + 1], scale=scale), deps=dq)
                        t_psqk_free[pb] = tev
                        t_qs_free[pb] = R.dma("sync", io[dst][ic * 128:(ic + 1) * 128, i * 512:(i + 1) * 512],
                                              qs[pb][:, :], deps=[tev], sem=R.dma_sem(f"Aqs{pb}"))

                for c in range(4):
                    gch = 4 * i + c
                    pb = c % 2
                    mm_last = None
                    for kc in range(8):
                        dmm = [t_chunks[c]] + g["t_wqkv"] if kc == 0 else []
                        if kc == 0 and t_psv_free[pb]:
                            dmm = dmm + [t_psv_free[pb]]
                        for nh_ in range(2):
                            mm_last = R.op("tensor", lambda e, kc=kc, c=c, pb=pb, nh_=nh_, tb=tb: e.matmul(
                                ps_v[pb][:, nh_ * 512:(nh_ + 1) * 512],
                                xlnT[tb].ap().rearrange("p (kc t) -> p kc t", kc=8)[:, kc, c * 128:(c + 1) * 128],
                                wvT.ap().rearrange("p (kc i) -> p kc i", kc=8)[:, kc, nh_ * 512:(nh_ + 1) * 512],
                                start=(kc == 0), stop=(kc == 7)),
                                deps=dmm, inc=(kc == 7 and nh_ == 1))
                    if c == 3:
                        t_xlnT_free[tb] = mm_last
                    dv = [mm_last, g["t_bv"], t_vst_free[pb]]
                    tev = R.op("vector", lambda e, pb=pb: e.scalar_tensor_tensor(
                        vst[pb].ap().rearrange("p (h s) -> p h s", h=NH)[:, :, 0:64],
                        ps_v[pb].ap().rearrange("p (h d) -> p h d", h=NH),
                        0.0, bvb.ap().rearrange("p (h d) -> p h d", h=NH),
                        ALU.bypass, ALU.add), deps=dv)
                    t_psv_free[pb] = tev
                    lastv = None
                    for hh in range(NH):
                        lastv = R.dma("sync", io["v2_s"][hh][:, gch * 65:(gch + 1) * 65],
                                      vst[pb][:, hh * 65:(hh + 1) * 65], deps=[tev],
                                      sem=R.dma_sem(f"Avst{pb}"))
                    t_vst_free[pb] = lastv

            R.dma("sync", io["ln3_s"][:, :], ln3p[:, :],
                  deps=[Tick(R.sem["vector"], R.semval["vector"]),
                        Tick(R.sem["scalar"], R.semval["scalar"])])
            R.wait("sync", R.drain_ticks())



EMITTERS = {}


def launch_io_names(phases):
    ins = set()
    outs = set()
    for p in phases:
        ins.update(PHASE_INPUTS[p])
    written = {n for n, (_, _, wr) in SCRATCH.items() if wr in phases}
    for p in phases:
        for n in PHASE_SCRATCH_R[p]:
            if n not in written:
                ins.add(n)
    outs.update(written)
    if "E" in phases:
        outs.add("out")
    return sorted(ins), sorted(outs)


def build_launch(phases, fused):
    nc = bass.Bass("TRN2", target_bir_lowering=False)
    io = declare_io(nc, phases, fused)
    g = {"es": contextlib.ExitStack(), "defer_prep": fused and "B" in phases}
    R = Rec(nc)
    with g["es"], R.es:
        for p in phases:
            EMITTERS[p](R, io, g)
    return nc


def map_inputs(inputs, core):
    d = {
        "x": np.ascontiguousarray(inputs["L0_attn_sequence"][core]),
        "ln0_g": inputs["L0_attn_ln_gamma"], "ln0_b": inputs["L0_attn_ln_beta"],
        "wq": inputs["L0_attn_Wq"], "bq": inputs["L0_attn_bq"],
        "wk": inputs["L0_attn_Wk"], "bk": inputs["L0_attn_bk"],
        "wv": inputs["L0_attn_Wv"], "bv": inputs["L0_attn_bv"],
        "wo": inputs["L0_attn_Wo"], "bo": inputs["L0_attn_bo"],
        "mu": np.asarray(inputs["L0_attn_mu"], dtype=np.float32).reshape(1),
        "lnf_g": inputs["L0_ffn_ln_gamma"], "lnf_b": inputs["L0_ffn_ln_beta"],
        "fc1_q": inputs["L0_ffn_fc1_q"], "fc1_s": inputs["L0_ffn_fc1_scales"],
        "fc1_b": inputs["L0_ffn_fc1_bias"],
        "fc2_q": inputs["L0_ffn_fc2_q"], "fc2_s": inputs["L0_ffn_fc2_scales"],
        "fc2_b": inputs["L0_ffn_fc2_bias"],
        "wz": inputs["L1_attn_Wz"], "bz": inputs["L1_attn_bz"],
        "wh": inputs["L1_attn_Wh"], "bh": inputs["L1_attn_bh"],
        "sw_beta": np.asarray(inputs["L1_attn_swish_beta"], dtype=np.float32).reshape(1),
        "lng_g": inputs["L1_attn_ln_gamma"], "lng_b": inputs["L1_attn_ln_beta"],
        "mw1": inputs["L1_ffn_W1"], "mb1": inputs["L1_ffn_b1"],
        "mw2": inputs["L1_ffn_W2"], "mb2": inputs["L1_ffn_b2"],
    }
    return {k: np.ascontiguousarray(np.asarray(v)) for k, v in d.items()}


LAUNCH_PLAN = [["A", "B", "C", "D", "E"]]


def kernel(**inputs):
    state = [map_inputs(inputs, c) for c in range(NC_)]
    for phases in LAUNCH_PLAN:
        fused = len(LAUNCH_PLAN) == 1
        nc = build_launch(phases, fused=fused)
        ins, outs = launch_io_names(phases)
        if fused:
            ins = sorted(set().union(*[PHASE_INPUTS[p] for p in phases]))
        in_maps = [{n: state[c][n] for n in ins} for c in range(NC_)]
        res = run_bass_kernel_spmd(nc, in_maps, core_ids=list(range(NC_)))
        for c in range(NC_):
            state[c].update(res.results[c])
    return np.stack([state[c]["out"] for c in range(NC_)], axis=0)



def make_prework(R, io, g, es):
    nc = R.nc
    m2bf = es.enter_context(nc.sbuf_tensor("m2bfP", [128, D], BF16))
    m1bf = es.enter_context(nc.sbuf_tensor("m1bfP", [128, FF // 2], BF16))
    prevm = [None]
    prevg = [None]

    def m2cast(e_, fc):
        deps = [prevm[0]] if prevm[0] else []
        tld = R.dma("gpsimd", m2bf[:, :], io["mw2"][e_, fc * 128:(fc + 1) * 128, :],
                    deps=deps, sem=R.dma_sem("Pm2"))
        last = None
        for dc in range(8):
            last = R.dma("gpsimd", io["m2g_s"][e_, dc][:, fc * 128:(fc + 1) * 128],
                         m2bf[:, dc * 128:(dc + 1) * 128], deps=[tld],
                         sem=R.dma_sem("Pm2st"))
        prevm[0] = last

    def m1cast(e_, kc, hh):
        deps = [prevg[0]] if prevg[0] else []
        tld = R.dma("gpsimd", m1bf[:, :],
                    io["mw1"][e_, kc * 128:(kc + 1) * 128, hh * (FF // 2):(hh + 1) * (FF // 2)],
                    deps=deps, sem=R.dma_sem("Pm1"))
        last = None
        for g8 in range(4 * hh, 4 * hh + 4):
            for fi in range(4):
                fl = (4 * g8 + fi) - 16 * hh
                last = R.dma("gpsimd",
                             io["m1g_s"][e_, g8][:, (fi * 8 + kc) * 128:(fi * 8 + kc + 1) * 128],
                             m1bf[:, fl * 128:(fl + 1) * 128],
                             deps=[tld], sem=R.dma_sem("Pm1st"))
        prevg[0] = last

    def slice_fn(si):
        def run():
            if si == 0:
                g["t_wzbf"] = R.dma("gpsimd", io["wzbf_s"], io["wz"])
                g["t_whbf"] = R.dma("gpsimd", io["whbf_s"], io["wh"])
            for k in range(8):
                idx = si * 8 + k
                m1cast(idx // 16, (idx // 2) % 8, idx % 2)
            for k in range(16):
                idx = si * 16 + k
                m2cast(idx // 32, idx % 32)
        return run

    return [slice_fn(si) for si in range(16)]


def emit_phaseB(R, io, g, prework=None):
    nc = R.nc
    QT = 512
    NQT = T // QT
    ND = 5
    with contextlib.ExitStack() as esB:
        with R.phase("B"):
            if g.get("defer_prep") and prework is None:
                prework = make_prework(R, io, g, esB)
            kTp = [esB.enter_context(nc.sbuf_tensor(f"kTpB{i}", [128, T], BF16)) for i in range(2)]
            vh = [esB.enter_context(nc.sbuf_tensor(f"vhB{i}", [128, NCH * 65], BF16)) for i in range(2)]
            qh = [esB.enter_context(nc.sbuf_tensor(f"qhB{i}", [128, T], BF16)) for i in range(2)]
            e_t = [esB.enter_context(nc.sbuf_tensor(f"e_tB{i}", [128, 2 * QT], BF16)) for i in range(3)]
            t_t = [esB.enter_context(nc.sbuf_tensor(f"t_tB{i}", [128, 2 * QT], BF16)) for i in range(3)]
            p_t = [esB.enter_context(nc.sbuf_tensor(f"p_tB{i}", [128, 2 * QT], BF16)) for i in range(3)]
            rrow = [esB.enter_context(nc.sbuf_tensor(f"rrowB{i}", [1, QT], F32)) for i in range(2)]
            o_un = [esB.enter_context(nc.sbuf_tensor(f"o_unB{i}", [64, QT], F32)) for i in range(2)]
            rb_bc = [esB.enter_context(nc.sbuf_tensor(f"rb_bcB{i}", [64, QT], F32)) for i in range(2)]
            att_st = [esB.enter_context(nc.sbuf_tensor(f"att_stB{i}", [64, QT], BF16)) for i in range(2)]

            t_th = esB.enter_context(nc.sbuf_tensor("t_thB", [128, 512], BF16))
            y_u = esB.enter_context(nc.sbuf_tensor("y_uB", [128, 512], BF16))
            hb1 = esB.enter_context(nc.sbuf_tensor("hb1B", [128, 32], F32))
            mu_c = esB.enter_context(nc.sbuf_tensor("mu_cB", [128, 1], F32))
            negmu = esB.enter_context(nc.sbuf_tensor("negmuB", [128, 1], F32))
            negmuh = esB.enter_context(nc.sbuf_tensor("negmuhB", [128, 1], F32))
            xl3 = esB.enter_context(nc.sbuf_tensor("xl3B", [128, 8 * 512], BF16))
            y_t = esB.enter_context(nc.sbuf_tensor("y_tB", [128, 32 * 512], BF16))
            wcf = [esB.enter_context(nc.sbuf_tensor(f"wcfB{i}", [128, 32 * 128], BF16)) for i in range(2)]
            z05s = [esB.enter_context(nc.sbuf_tensor(f"z05sB{i}", [128, 512], BF16)) for i in range(2)]
            fc1bc = esB.enter_context(nc.sbuf_tensor("fc1bcB", [128, 32], F32))
            h2bc = esB.enter_context(nc.sbuf_tensor("h2bcB", [128, 8], F32))
            ps_s = [esB.enter_context(nc.psum_tensor(f"ps_sB{i}", [128, 2 * QT], F32)) for i in range(3)]
            ps_o = esB.enter_context(nc.psum_tensor("ps_oB", [128, QT], F32))
            ps_c1 = [esB.enter_context(nc.psum_tensor("ps_c1B", [128, 512], F32))] * 2
            ps_c2 = ps_c1[0]

            skv = R.dma_sem("Bkv")
            t_mu = bcast_row(R, mu_c, io["mu"], 1, sem=skv)
            t_nm = R.op("vector", lambda e: e.tensor_single_scalar(negmu[:, :], mu_c[:, :], -1.0, ALU.mult),
                        deps=[t_mu])
            t_nmh = R.op("vector", lambda e: e.tensor_single_scalar(negmuh[:, :], mu_c[:, :], -0.5, ALU.mult),
                         deps=[t_mu])
            t_f1b = col_tile_load(R, fc1bc, io["fc1_b"], 32, sem=skv)
            t_f2b = col_tile_load(R, h2bc, io["fc2_b"], 8, sem=skv)
            t_h2b = R.op("vector", lambda e: e.tensor_single_scalar(h2bc[:, :], h2bc[:, :], 0.5, ALU.mult),
                         deps=t_f2b)
            t_hb1 = R.op("vector", lambda e: e.tensor_single_scalar(hb1[:, :], fc1bc[:, :], 0.5, ALU.mult),
                         deps=t_f1b)

            t_q = [None, None]
            t_q[0] = R.dma("sync", qh[0][0:64, :], io["qT_s"][0:64, :], sem=R.dma_sem("Bq0"))
            t_kTl = [None, None]
            t_kTl[0] = None
            tk0 = R.dma("sync", kTp[0][0:64, :], io["kT_s"][0:64, :], sem=R.dma_sem("BkT0"))
            tk1 = R.dma("sync", kTp[0][64:128, :], io["kT_s"][64:128, :], sem=R.dma_sem("BkT0"))
            t_kTl[0] = tk1
            t_vhl = [None, None]
            t_vhl[0] = R.dma("sync", vh[0][:, :], io["v2_s"][0], sem=R.dma_sem("Bv0"))
            q_free = [None, None]
            kT_free = [None, None]
            vh_free = [None, None]
            ps_s_free = [None] * 3
            ps_o_free = None
            ep_pend = [None]
            ep_rr_free = [None, None]
            ep_ou_free = [None, None]
            ep_rb_free = [None, None]
            ep_att_free = [None, None]
            et_free = [None] * 3
            tt_free = [None] * 3
            pt_free = [None] * 3
            att_free = None

            cf = {"xl3_free": None, "y_free": None, "wcf_free": [None, None],
                  "c1_free": [None, None], "c2_free": None, "z_free": [None, None],
                  "ylast": None, "xl3_ticks": [], "mm1_last": None,
                  "ev_free": None, "pend": []}

            def flush_ev():
                while cf["pend"]:
                    cf["pend"].pop(0)()

            def st_load_xl3(i):
                def run():
                    d = [cf["xl3_free"]] if cf["xl3_free"] else []
                    cf["xl3_ticks"] = [R.dma("sync", xl3[:, kc * 512:(kc + 1) * 512],
                                             io["xln3T_s"][kc * 128:(kc + 1) * 128, i * 512:(i + 1) * 512],
                                             deps=d, sem=R.dma_sem("BxlT")) for kc in range(8)]
                return run

            def st_mm1(i, g8, fi):
                def run():
                    flush_ev()
                    pg = g8 % 2
                    f = 4 * g8 + fi
                    if fi == 0:
                        dgw = [cf["wcf_free"][pg]] if cf["wcf_free"][pg] else []
                        cf["tw"] = R.dma("scalar", wcf[pg][:, :], io["w1g_s"][g8], deps=dgw,
                                         sem=R.dma_sem(f"BCw{pg}"))
                    tw = cf["tw"]

                    def make_ev(f, mmt):
                        def ev():
                            dv = [cf["ev_free"]] if cf["ev_free"] else []
                            tth = R.op("scalar", lambda e, f=f: e.activation(
                                t_th[:, :], ps_c1[0][:, :], AF.Tanh,
                                bias=hb1[:, f:f + 1], scale=0.5),
                                deps=[mmt, t_hb1] + dv)
                            tyu = R.op("scalar", lambda e, f=f: e.activation(
                                y_u[:, :], ps_c1[0][:, :], AF.Identity,
                                bias=fc1bc[:, f:f + 1]),
                                deps=[mmt] + dv)
                            cf["c1_free"][0] = tyu
                            s1 = R.op("vector", lambda e: e.scalar_tensor_tensor(
                                t_th[:, :], t_th[:, :], 0.5, y_u[:, :], ALU.mult, ALU.mult),
                                deps=[tth, tyu])
                            dyw = [cf["y_free"]] if (f < 2 and cf["y_free"]) else []
                            s2 = R.op("vector", lambda e, f=f: e.scalar_tensor_tensor(
                                y_t[:, f * 512:(f + 1) * 512], y_u[:, :], 0.5,
                                t_th[:, :], ALU.mult, ALU.add),
                                deps=[s1] + dyw)
                            cf["ev_free"] = s2
                            cf["ylast"] = s2
                        return ev

                    mm = None
                    for kc in range(8):
                        d0 = []
                        if kc == 0:
                            d0 = [tw, cf["c1_free"][0]] + cf["xl3_ticks"]
                            if fi == 0 and cf["c2_free"]:
                                d0.append(cf["c2_free"])
                        mm = R.op("tensor", lambda e, fi=fi, kc=kc, pg=pg: e.matmul(
                            ps_c1[0][:, :],
                            wcf[pg][:, (fi * 8 + kc) * 128:(fi * 8 + kc + 1) * 128],
                            xl3[:, kc * 512:(kc + 1) * 512],
                            start=(kc == 0), stop=(kc == 7)),
                            deps=d0, inc=(kc == 7))
                    cf["pend"].append(make_ev(f, mm))
                    if fi == 3:
                        cf["wcf_free"][pg] = mm
                        if g8 == 7:
                            cf["mm1_last"] = mm
                return run

            def st_mm2(i, dc, oct_):
                def run():
                    flush_ev()
                    pg = dc % 2
                    if oct_ == 0:
                        dgw = [cf["wcf_free"][pg]] if cf["wcf_free"][pg] else []
                        cf["tw"] = R.dma("scalar", wcf[pg][:, :], io["w2g_s"][dc], deps=dgw,
                                         sem=R.dma_sem(f"BCw{pg}"))
                    tw = cf["tw"]
                    mm = None
                    for fx in range(8):
                        f = oct_ * 8 + fx
                        d0 = []
                        if f == 0:
                            d0 = [cf["c2_free"], tw, cf["ylast"]]
                            if cf["c1_free"][0]:
                                d0.append(cf["c1_free"][0])
                        elif fx == 0:
                            d0 = [tw]
                        mm = R.op("tensor", lambda e, f=f, pg=pg: e.matmul(
                            ps_c2[:, :],
                            wcf[pg][:, f * 128:(f + 1) * 128],
                            y_t[:, f * 512:(f + 1) * 512],
                            start=(f == 0), stop=(f == 31)),
                            deps=d0, inc=(f == 31))
                    if oct_ < 3:
                        return
                    cf["wcf_free"][pg] = mm
                    if dc == 7:
                        cf["y_free"] = mm
                        cf["xl3_free"] = cf["mm1_last"]

                    def ev(mm=mm, dc=dc, i=i):
                        zb = dc % 2
                        dz = [cf["z_free"][zb]] if cf["z_free"][zb] else []
                        tev = R.op("scalar", lambda e, dc=dc, zb=zb: e.activation(
                            z05s[zb][:, :], ps_c2[:, :], AF.Identity,
                            bias=h2bc[:, dc:dc + 1], scale=0.5),
                            deps=[mm, t_h2b] + dz)
                        cf["c2_free"] = tev
                        cf["z_free"][zb] = R.dma("sync", io["z05_s"][i][:, dc * 512:(dc + 1) * 512],
                                                 z05s[zb][:, :], deps=[tev], sem=R.dma_sem(f"Bz{zb}"))
                    cf["pend"].append(ev)
                return run

            steps = []
            for i in range(8):
                steps.append(st_load_xl3(i))
                for g8 in range(8):
                    for fi in range(4):
                        steps.append(st_mm1(i, g8, fi))
                for dc in range(8):
                    for oc in range(4):
                        steps.append(st_mm2(i, dc, oc))
            step_i = [0]

            def run_steps(k):
                while k > 0 and step_i[0] < len(steps):
                    steps[step_i[0]]()
                    step_i[0] += 1
                    k -= 1

            NP = NCH // 2
            units = [(h, qt) for h in range(NH) for qt in range(NQT)]
            pend = {}

            def emit_S(u, kp):
                h, qt = units[u]
                qb = h % 2
                base = (h % 2) * 64
                pj = h // 2
                kb = pj % 2
                sb = (u * NP + kp) % 3
                if kp == 0 and qt == 0:
                    if prework is not None and h < len(prework):
                        prework[h]()
                    if h + 1 < NH:
                        nb = (h + 1) % 2
                        d = [q_free[nb]] if q_free[nb] else []
                        t_q[nb] = R.dma("sync", qh[nb][((h + 1) % 2) * 64:((h + 1) % 2) * 64 + 64, :],
                                        io["qT_s"][(h + 1) * 64:(h + 2) * 64, :], deps=d,
                                        sem=R.dma_sem(f"Bq{nb}"))
                        dv = [vh_free[nb]] if vh_free[nb] else []
                        t_vhl[nb] = R.dma("sync", vh[nb][:, :], io["v2_s"][h + 1], deps=dv,
                                          sem=R.dma_sem(f"Bv{nb}"))
                    if h % 2 == 0 and pj + 1 < 8:
                        nk = (pj + 1) % 2
                        dk = [kT_free[nk]] if kT_free[nk] else []
                        R.dma("sync", kTp[nk][0:64, :], io["kT_s"][(2 * pj + 2) * 64:(2 * pj + 3) * 64, :],
                              deps=dk, sem=R.dma_sem(f"BkT{nk}"))
                        t_kTl[nk] = R.dma("sync", kTp[nk][64:128, :],
                                          io["kT_s"][(2 * pj + 3) * 64:(2 * pj + 4) * 64, :],
                                          sem=R.dma_sem(f"BkT{nk}"))
                d0 = [t_q[qb], ps_s_free[sb], t_kTl[kb], t_vhl[h % 2]]
                tmm = None
                for half in range(2):
                    kc = 2 * kp + half
                    tmm = R.op("tensor", lambda e, base=base, kb=kb, kc=kc, qb=qb, qt=qt, sb=sb, half=half: e.matmul(
                        ps_s[sb][:, half * QT:(half + 1) * QT],
                        kTp[kb][base:base + 64, kc * 128:(kc + 1) * 128],
                        qh[qb][base:base + 64, qt * QT: (qt + 1) * QT],
                        start=True, stop=True),
                        deps=(d0 if half == 0 else []), inc=(half == 1))
                if qt == NQT - 1 and kp == NP - 1:
                    q_free[qb] = tmm
                    if h % 2 == 1:
                        kT_free[h // 2 % 2] = tmm
                pend[(u, kp)] = tmm

            emit_S(0, 0)
            for u in range(len(units)):
                h, qt = units[u]
                last_pv = None
                for kp in range(NP):
                    sb = (u * NP + kp) % 3
                    tb3 = (u * NP + kp) % 3
                    if kp + 1 < NP:
                        emit_S(u, kp + 1)
                    elif u + 1 < len(units):
                        emit_S(u + 1, 0)
                    tmm = pend.pop((u, kp))
                    t_tanh = R.op("scalar", lambda e, sb=sb, tb3=tb3: e.activation(
                        t_t[tb3][:, :], ps_s[sb][:, :], AF.Tanh, bias=negmuh[:, 0:1], scale=0.5),
                        deps=[tmm, t_nmh, tt_free[tb3]])
                    t_exp = R.op("scalar", lambda e, sb=sb, tb3=tb3: e.activation(
                        e_t[tb3][:, :], ps_s[sb][:, :], AF.Exp, bias=negmu[:, 0:1]),
                        deps=[tmm, t_nm, et_free[tb3]])
                    ps_s_free[sb] = t_exp
                    t_gg = R.op("vector", lambda e, tb3=tb3: e.tensor_scalar(
                        t_t[tb3][:, :], t_t[tb3][:, :], 0.5, 0.5, ALU.mult, ALU.add),
                        deps=[t_tanh])
                    t_p = R.op("vector", lambda e, tb3=tb3: e.tensor_tensor(
                        p_t[tb3][:, :], e_t[tb3][:, :], t_t[tb3][:, :], ALU.mult),
                        deps=[t_exp, t_gg, pt_free[tb3]])
                    et_free[tb3] = t_p
                    tt_free[tb3] = t_p
                    for half in range(2):
                        kc = 2 * kp + half
                        dpv = [t_p] if half == 0 else []
                        if kp == 0 and half == 0 and ps_o_free is not None:
                            dpv.append(ps_o_free)
                        last_pv = R.op("tensor", lambda e, kc=kc, h=h, tb3=tb3, half=half: e.matmul(
                            ps_o[0:65, :],
                            vh[h % 2][:, kc * 65:(kc + 1) * 65],
                            p_t[tb3][:, half * QT:(half + 1) * QT],
                            start=(kc == 0), stop=(kc == NCH - 1)),
                            deps=dpv, inc=(half == 1))
                    pt_free[tb3] = last_pv
                    if kp % 2 == 1:
                        run_steps(1)
                if qt == NQT - 1:
                    vh_free[h % 2] = last_pv
                if ep_pend[0] is not None:
                    ep_pend[0]()
                    ep_pend[0] = None
                eb = u % 2
                t_rec = R.op("vector", lambda e, eb=eb: e.reciprocal(rrow[eb][0:1, :], ps_o[64:65, :]),
                             deps=[last_pv] + ([ep_rr_free[eb]] if ep_rr_free[eb] else []))
                t_oev = R.op("vector", lambda e, eb=eb: e.tensor_copy(o_un[eb][:, :], ps_o[0:64, :]),
                             deps=[last_pv] + ([ep_ou_free[eb]] if ep_ou_free[eb] else []))
                ps_o_free = t_oev
                t_st1 = R.dma("sync", io["rrow_s"][eb], rrow[eb][0:1, :], deps=[t_rec],
                              sem=R.dma_sem(f"Brr{eb}"))
                rsl = io["rrow_s"][eb]
                t_bcd = R.dma("sync", rb_bc[eb][:, :],
                              bass.AP(rsl.tensor, rsl.offset, [[0, 64], [1, QT]]),
                              deps=[t_st1] + ([ep_rb_free[eb]] if ep_rb_free[eb] else []),
                              sem=R.dma_sem(f"Brb{eb}"))

                def stage2(eb=eb, h=h, qt=qt, t_bcd=t_bcd, t_oev=t_oev):
                    da = [ep_att_free[eb]] if ep_att_free[eb] else []
                    t_mul = R.op("vector", lambda e, eb=eb: e.tensor_tensor(
                        att_st[eb][:, :], o_un[eb][:, :], rb_bc[eb][:, :], ALU.mult),
                        deps=[t_oev, t_bcd] + da)
                    ep_ou_free[eb] = t_mul
                    ep_rb_free[eb] = t_mul
                    ep_rr_free[eb] = t_mul
                    ep_att_free[eb] = R.dma("sync",
                                            io["attnT_s"][h * 64:(h + 1) * 64, qt * QT:(qt + 1) * QT],
                                            att_st[eb][:, :], deps=[t_mul], sem=R.dma_sem(f"Batt{eb}"))
                ep_pend[0] = stage2
                run_steps(1)
            if ep_pend[0] is not None:
                ep_pend[0]()
                ep_pend[0] = None
            run_steps(len(steps))
            flush_ev()

            R.wait("sync", R.drain_ticks())


def emit_phaseC(R, io, g):
    nc = R.nc
    with contextlib.ExitStack() as esC:
        with R.phase("C"):
            ensure_identity(R, g)
            woT = esC.enter_context(nc.sbuf_tensor("woTC", [128, 8 * 1024], BF16))
            bob = esC.enter_context(nc.sbuf_tensor("bobC", [128, D], F32))
            xn0t = esC.enter_context(nc.sbuf_tensor("xn0tC", [128, 4 * D], F32))
            xt = [esC.enter_context(nc.sbuf_tensor(f"xtC{i}", [128, D], F32)) for i in range(2)]
            z05 = esC.enter_context(nc.sbuf_tensor("z05C", [128, 8 * 512], BF16))
            attnt = esC.enter_context(nc.sbuf_tensor("attntC", [128, 8 * 512], BF16))
            ztok = esC.enter_context(nc.sbuf_tensor("ztokC", [128, D], F32))
            fg = esC.enter_context(nc.sbuf_tensor("fgC", [128, D], F32))
            ug = esC.enter_context(nc.sbuf_tensor("ugC", [128, D], F32))
            sg = esC.enter_context(nc.sbuf_tensor("sgC", [128, D], F32))
            x1b = [esC.enter_context(nc.sbuf_tensor(f"x1bC{i}", [128, D], F32)) for i in range(4)]
            stg = [esC.enter_context(nc.sbuf_tensor(f"stgC{i}", [128, 8 * 512], BF16)) for i in range(2)]
            xnb = [esC.enter_context(nc.sbuf_tensor(f"xnbC{i}", [128, D], BF16)) for i in range(2)]
            stf = esC.enter_context(nc.sbuf_tensor("stfC", [128, 40], F32))
            sqsC = esC.enter_context(nc.sbuf_tensor("sqsC", [128, D], F32))
            scol = esC.enter_context(nc.sbuf_tensor("scolC", [128, 32], F32))
            t1c = esC.enter_context(nc.sbuf_tensor("t1cC", [128, 4], F32))
            srow = esC.enter_context(nc.sbuf_tensor("srowC", [32, 128], F32))
            r_sgn = esC.enter_context(nc.sbuf_tensor("r_sgnC", [32, 128], F32))
            r_adj = esC.enter_context(nc.sbuf_tensor("r_adjC", [32, 128], F32))
            r_i = esC.enter_context(nc.sbuf_tensor("r_iC", [32, 128], I32))
            r_f = esC.enter_context(nc.sbuf_tensor("r_fC", [32, 128], F32))
            r_u = esC.enter_context(nc.sbuf_tensor("r_uC", [32, 128], F32))
            r_fli = esC.enter_context(nc.sbuf_tensor("r_fliC", [32, 128], I32))
            r_flf = esC.enter_context(nc.sbuf_tensor("r_flfC", [32, 128], F32))
            r_m = esC.enter_context(nc.sbuf_tensor("r_mC", [32, 128], F32))
            r_e1 = esC.enter_context(nc.sbuf_tensor("r_e1C", [32, 128], F32))
            r_e2 = esC.enter_context(nc.sbuf_tensor("r_e2C", [32, 128], F32))
            r_w = esC.enter_context(nc.sbuf_tensor("r_wC", [32, 128], BF16))
            ps_tr = esC.enter_context(nc.psum_tensor("ps_trC", [128, 1024], BF16))
            ps_s0 = esC.enter_context(nc.psum_tensor("ps_s0C", [128, 1024], F32))
            ps_rt = esC.enter_context(nc.psum_tensor("ps_rtC", [128, 128], F32))

            scs = R.dma_sem("Cset")
            t_wo = R.dma("sync", woT[:, :], io["woT_s"][:, :], sem=scs)
            t_bo = bcast_row(R, bob, io["bo"], D, sem=scs)
            epsC = const_col(R, g, EPS1)

            t_xn0_free = None
            t_xt_free = [None, None]
            t_pstr_free = None
            t_ps0_free = None
            t_z05_free = None
            t_att_free = None
            t_ztok_free = None
            t_fg_free = None
            t_ug_free = None
            t_sg_free = None
            t_x1_free = [[], [], [], []]
            x1w = [None] * 4
            app_free = [[], [], [], []]
            t_sqs_free = None
            xnb_free = [None, None]
            t_stg_free = [None, None]
            t_scol = []
            vt4 = [None] * 4
            mt4 = [None] * 4

            for i in range(8):
                dz05 = [t_z05_free] if t_z05_free else []
                t_z = R.dma("sync", z05[:, :], io["z05_s"][i], deps=dz05, sem=R.dma_sem("Cz05"))
                txn_l = []
                for c in range(4):
                    gch = 4 * i + c
                    dl = [t_xn0_free] if (c == 0 and t_xn0_free) else []
                    txn_l.append(R.dma("sync", xn0t[:, c * D:(c + 1) * D],
                                       io["xn0_s"][gch * 128:(gch + 1) * 128, :], deps=dl,
                                       sem=R.dma_sem(f"Cxn{c}")))
                datt = [t_att_free] if t_att_free else []
                t_att = [R.dma("sync", attnt[:, jc * 512:(jc + 1) * 512],
                               io["attnT_s"][jc * 128:(jc + 1) * 128, i * 512:(i + 1) * 512],
                               deps=datt, sem=R.dma_sem("Catt")) for jc in range(8)]

                last_trs = None
                for c in range(4):
                    gch = 4 * i + c
                    xbuf = gch % 2
                    dxl2 = [t_xt_free[xbuf]] if t_xt_free[xbuf] else []
                    tx = R.dma("sync", xt[xbuf][:, :], io["x"][gch * 128:(gch + 1) * 128, :],
                               deps=dxl2, sem=R.dma_sem(f"Cx{xbuf}"))
                    trs = []
                    for dc in range(8):
                        dtr = [t_z, g["t_ident"]]
                        if t_pstr_free:
                            dtr.append(t_pstr_free)
                        trs.append(R.op("tensor", lambda e, dc=dc, c=c: e.transpose(
                            ps_tr[:, dc * 128:(dc + 1) * 128],
                            z05[:, dc * 512 + c * 128: dc * 512 + (c + 1) * 128],
                            g["ident_bf16"][:, :]), deps=dtr))
                    last_trs = trs[-1]
                    dzt = [t_ztok_free] if t_ztok_free else []
                    tzt = R.op("vector", lambda e: e.tensor_copy(ztok[:, :], ps_tr[:, :]),
                               deps=trs + dzt)
                    t_pstr_free = tzt
                    dfg = [t_fg_free] if t_fg_free else []
                    tf = R.op("vector", lambda e, c=c: e.tensor_tensor(
                        fg[:, :], ztok[:, :], xn0t[:, c * D:(c + 1) * D], ALU.add),
                        deps=[tzt, txn_l[c]] + dfg)
                    t_ztok_free = tf
                    tg0 = R.op("scalar", lambda e: e.activation(fg[:, :], fg[:, :], AF.Silu),
                               deps=[tf])
                    mm = None
                    for jc in range(8):
                        d0 = []
                        if jc == 0:
                            d0 = t_att + ([t_wo] if i == 0 else [])
                            if t_ps0_free:
                                d0.append(t_ps0_free)
                        for nh_ in range(2):
                            mm = R.op("tensor", lambda e, jc=jc, c=c, nh_=nh_: e.matmul(
                                ps_s0[:, nh_ * 512:(nh_ + 1) * 512],
                                attnt[:, jc * 512 + c * 128: jc * 512 + (c + 1) * 128],
                                woT[:, jc * 1024 + nh_ * 512: jc * 1024 + (nh_ + 1) * 512],
                                start=(jc == 0), stop=(jc == 7)),
                                deps=d0, inc=(jc == 7 and nh_ == 1))
                    dug = [t_ug_free] if t_ug_free else []
                    tu = R.op("vector", lambda e: e.scalar_tensor_tensor(
                        ug[:, :], ps_s0[:, :], 0.0, bob[:, :], ALU.bypass, ALU.add),
                        deps=[mm, t_bo] + dug)
                    t_ps0_free = tu
                    dsg = [t_sg_free] if t_sg_free else []
                    tsg = R.op("vector", lambda e: e.tensor_tensor(
                        sg[:, :], ug[:, :], fg[:, :], ALU.mult),
                        deps=[tu, tg0] + dsg)
                    t_ug_free = tsg
                    t_fg_free = tsg
                    dx1 = list(t_x1_free[c])
                    tx1 = R.op("vector", lambda e, c=c, xbuf=xbuf: e.tensor_tensor(
                        x1b[c][:, :], sg[:, :], xt[xbuf][:, :], ALU.add),
                        deps=[tsg, tx] + dx1)
                    t_sg_free = tx1
                    t_xt_free[xbuf] = tx1
                    x1w[c] = tx1
                    tst = R.dma("sync", io["x1_s"][gch * 128:(gch + 1) * 128, :],
                                x1b[c][:, :], deps=[tx1], sem=R.dma_sem(f"Cx1{c}"))
                    t_x1_free[c] = [tst]
                    j = c
                    dst = list(app_free[j])
                    tsum = R.op("vector", lambda e, j=j: e.reduce_sum(
                        stf[:, j:j + 1], x1b[j][:, :], axis=mybir.AxisListType.X),
                        deps=[tx1] + dst)
                    dsq = [t_sqs_free] if t_sqs_free else []
                    tssq = R.op("scalar", lambda e, j=j: e.activation(
                        sqsC[:, :], x1b[j][:, :], AF.Square, accum_out=stf[:, 4 + j:5 + j]),
                        deps=[tx1] + dst + dsq)
                    t_sqs_free = tssq
                    tlm = R.op("vector", lambda e, j=j: e.tensor_single_scalar(
                        stf[:, 8 + j:9 + j], stf[:, j:j + 1], 1.0 / D, ALU.mult), deps=[tsum])
                    te1 = R.op("vector", lambda e, j=j: e.tensor_single_scalar(
                        stf[:, 12 + j:13 + j], stf[:, 4 + j:5 + j], 1.0 / D, ALU.mult), deps=[tssq])
                    tmm2 = R.op("vector", lambda e, j=j: e.tensor_tensor(
                        stf[:, 16 + j:17 + j], stf[:, 8 + j:9 + j], stf[:, 8 + j:9 + j], ALU.mult),
                        deps=[tlm])
                    tvar = R.op("vector", lambda e, j=j: e.tensor_tensor(
                        stf[:, 20 + j:21 + j], stf[:, 12 + j:13 + j], stf[:, 16 + j:17 + j],
                        ALU.subtract), deps=[te1, tmm2])
                    vt4[j] = tvar
                    mt4[j] = tlm
                    if c < 3:
                        continue
                    eps_tile, eps_tick = epsC
                    tsd = R.op("scalar", lambda e: e.activation(
                        stf[:, 24:28], stf[:, 20:24], AF.Sqrt, bias=eps_tile[:, 0:1]),
                        deps=vt4 + [eps_tick])
                    trc = R.op("vector", lambda e: e.reciprocal(stf[:, 24:28], stf[:, 24:28]),
                               deps=[tsd])
                    tnm = R.op("vector", lambda e: e.scalar_tensor_tensor(
                        stf[:, 28:32], stf[:, 8:12], -1.0, stf[:, 24:28], ALU.mult, ALU.mult),
                        deps=[trc] + mt4)
                    tnm2 = R.op("vector", lambda e: e.tensor_single_scalar(
                        stf[:, 36:40], stf[:, 28:32], 2.0, ALU.mult), deps=[tnm])
                    for jj in range(4):
                        xb2 = jj % 2
                        dxn = [xnb_free[xb2]] if xnb_free[xb2] else []
                        tapp = R.op("scalar", lambda e, jj=jj, xb2=xb2: e.activation(
                            xnb[xb2][:, :], x1b[jj][:, :], AF.Identity,
                            bias=stf[:, 28 + jj:29 + jj], scale=stf[:, 24 + jj:25 + jj]),
                            deps=[tnm, trc] + dxn)
                        tt1 = R.op("vector", lambda e, jj=jj: e.tensor_tensor(
                            t1c[:, jj:jj + 1], x1b[jj][:, 0:1], x1b[jj][:, 1:2], ALU.add),
                            deps=[x1w[jj]])
                        tscol = R.op("vector", lambda e, jj=jj, i=i: e.scalar_tensor_tensor(
                            scol[:, 4 * i + jj:4 * i + jj + 1], t1c[:, jj:jj + 1],
                            stf[:, 24 + jj:25 + jj], stf[:, 36 + jj:37 + jj],
                            ALU.mult, ALU.add), deps=[tt1, tnm2])
                        t_scol.append(tscol)
                        app_free[jj] = [tapp, tscol]
                        t_x1_free[jj] = t_x1_free[jj] + [tapp, tscol]
                        trs2 = []
                        for kc in range(8):
                            dtr = [tapp, g["t_ident"]]
                            if t_pstr_free:
                                dtr.append(t_pstr_free)
                            trs2.append(R.op("tensor", lambda e, kc=kc, xb2=xb2: e.transpose(
                                ps_tr[:, kc * 128:(kc + 1) * 128],
                                xnb[xb2][:, kc * 128:(kc + 1) * 128], g["ident_bf16"][:, :]),
                                deps=dtr))
                        xnb_free[xb2] = trs2[-1]
                        sbuf2 = i % 2
                        dstg = [t_stg_free[sbuf2]] if (jj == 0 and t_stg_free[sbuf2]) else []
                        tev2 = R.op("vector", lambda e, sbuf2=sbuf2, jj=jj: e.tensor_copy(
                            stg[sbuf2].ap().rearrange("p (kc t) -> p kc t", kc=8)[:, :, jj * 128:(jj + 1) * 128],
                            ps_tr.ap().rearrange("p (kc t) -> p kc t", kc=8)), deps=trs2 + dstg)
                        t_pstr_free = tev2
                        if jj == 3:
                            t_stg_free[sbuf2] = R.dma(
                                "sync",
                                io["xn1T_s"].rearrange("(kc p) t -> p kc t", kc=8)[:, :, i * 512:(i + 1) * 512],
                                stg[sbuf2].ap().rearrange("p (kc t) -> p kc t", kc=8),
                                deps=[tev2], sem=R.dma_sem(f"Cxst{sbuf2}"))
                t_xn0_free = tf
                t_z05_free = last_trs
                t_att_free = mm

            ttr = R.op("tensor", lambda e: e.transpose(ps_rt[0:32, :], scol[:, :], g["ident_f32"][:, :]),
                       deps=t_scol + [g["t_ident"], t_pstr_free])
            tsr = R.op("vector", lambda e: e.tensor_copy(srow[:, :], ps_rt[0:32, :]), deps=[ttr])
            tg1 = R.op("scalar", lambda e: e.activation(r_sgn[:, :], srow[:, :], AF.Sign), deps=[tsr])
            tg2 = R.op("vector", lambda e: e.scalar_tensor_tensor(
                r_adj[:, :], r_sgn[:, :], -0.5, srow[:, :], ALU.mult, ALU.add), deps=[tg1, tsr])
            tg3 = R.op("vector", lambda e: e.tensor_copy(r_i[:, :], r_adj[:, :]), deps=[tg2])
            tg4 = R.op("vector", lambda e: e.tensor_copy(r_f[:, :], r_i[:, :]), deps=[tg3])
            tg5 = R.op("vector", lambda e: e.tensor_scalar(
                r_u[:, :], r_f[:, :], 0.125, -0.4375, ALU.mult, ALU.add), deps=[tg4])
            tg6 = R.op("vector", lambda e: e.tensor_copy(r_fli[:, :], r_u[:, :]), deps=[tg5])
            tg7 = R.op("vector", lambda e: e.tensor_copy(r_flf[:, :], r_fli[:, :]), deps=[tg6])
            tg8 = R.op("vector", lambda e: e.scalar_tensor_tensor(
                r_m[:, :], r_flf[:, :], -8.0, r_f[:, :], ALU.mult, ALU.add), deps=[tg7, tg4])
            prev_w = None
            for e_ in range(NE):
                dp = [tg8] + ([prev_w] if prev_w else [])
                te1_ = R.op("vector", lambda e, e_=e_: e.tensor_single_scalar(
                    r_e1[:, :], r_m[:, :], float(e_), ALU.is_equal), deps=dp)
                te2_ = R.op("vector", lambda e, e_=e_: e.tensor_single_scalar(
                    r_e2[:, :], r_m[:, :], float((e_ + 7) % 8), ALU.is_equal), deps=dp)
                te3_ = R.op("vector", lambda e: e.tensor_tensor(
                    r_e1[:, :], r_e1[:, :], r_e2[:, :], ALU.add), deps=[te1_, te2_])
                te4_ = R.op("vector", lambda e: e.tensor_single_scalar(
                    r_w[:, :], r_e1[:, :], 0.5, ALU.mult), deps=[te3_])
                prev_w = R.dma("sync", io["w_rows_s"][e_].rearrange("(c j) -> c j", c=32), r_w[:, :],
                               deps=[te4_], sem=R.dma_sem("Cwr"))

            R.wait("sync", R.drain_ticks())



def emit_phaseD(R, io, g):
    nc = R.nc
    with contextlib.ExitStack() as esD:
        es1 = contextlib.ExitStack()
        with R.phase("D1"), es1:
            ensure_identity(R, g)
            wz_t = es1.enter_context(nc.sbuf_tensor("wz_tD", [128, 8 * D], BF16))
            wh_t = es1.enter_context(nc.sbuf_tensor("wh_tD", [128, 8 * D], BF16))
            xnT = es1.enter_context(nc.sbuf_tensor("xnTD", [128, 8 * 1024], BF16))
            zf = es1.enter_context(nc.sbuf_tensor("zfD", [128, 8 * 1024], F32))
            hf = es1.enter_context(nc.sbuf_tensor("hfD", [128, 8 * 1024], F32))
            hout = es1.enter_context(nc.sbuf_tensor("houtD", [128, 8 * 1024], F32))
            sgt = [es1.enter_context(nc.sbuf_tensor(f"sgtD{i}", [128, 1024], F32)) for i in range(2)]
            ytok = [es1.enter_context(nc.sbuf_tensor(f"ytokD{i}", [128, 1024], F32)) for i in range(4)]
            stf = es1.enter_context(nc.sbuf_tensor("stfD", [128, 32], F32))
            sqsD = es1.enter_context(nc.sbuf_tensor("sqsD1", [128, 1024], F32))
            ggb = es1.enter_context(nc.sbuf_tensor("ggbD1", [128, D], F32))
            gbb = es1.enter_context(nc.sbuf_tensor("gbbD1", [128, D], F32))
            s1b = [es1.enter_context(nc.sbuf_tensor(f"s1bD1{i}", [128, D], BF16)) for i in range(2)]
            carry = es1.enter_context(nc.sbuf_tensor("carryD", [128, 8], F32))
            bzc = es1.enter_context(nc.sbuf_tensor("bzcD", [128, 8], F32))
            bhc = es1.enter_context(nc.sbuf_tensor("bhcD", [128, 8], F32))
            betac = es1.enter_context(nc.sbuf_tensor("betacD", [128, 1], F32))
            ps_pz = [es1.enter_context(nc.psum_tensor(f"ps_pzD{i}", [128, 1024], F32)) for i in range(2)]
            ps_trD = es1.enter_context(nc.psum_tensor("ps_trD1", [128, 1024], F32))

            sdw = R.dma_sem("Dw")
            t_wz = [R.dma("sync", wz_t[:, kc * D:(kc + 1) * D],
                          io["wzbf_s"][kc * 128:(kc + 1) * 128, :], sem=sdw) for kc in range(8)]
            t_wh = [R.dma("sync", wh_t[:, kc * D:(kc + 1) * D],
                          io["whbf_s"][kc * 128:(kc + 1) * 128, :], sem=sdw) for kc in range(8)]
            sds = R.dma_sem("Dset")
            t_bz = col_tile_load(R, bzc, io["bz"], 8, sem=sds)
            t_bh = col_tile_load(R, bhc, io["bh"], 8, sem=sds)
            t_beta = bcast_row(R, betac, io["sw_beta"], 1, sem=sds)
            t_ggb = bcast_row(R, ggb, io["lng_g"], D, sem=sds)
            t_gbb = bcast_row(R, gbb, io["lng_b"], D, sem=sds)
            epsD = const_col(R, g, EPS1)
            t_carry0 = R.op("vector", lambda e: e.memset(carry[:, :], 0.0))

            t_xnT_free = None
            t_pz_free = [None, None]
            t_zf_free = None
            t_hf_free = None
            t_hout_free = None
            t_sgt_free = [None, None]
            t_sqs_free = None
            app_free = [None] * 4
            s1b_free = [None, None]
            vt4 = [None] * 4
            mt4 = [None] * 4
            s1ctr = [0]
            t_pstr_free = None
            t_carry = {dc: t_carry0 for dc in range(8)}
            for tc in range(4):
                dxn = [t_xnT_free] if t_xnT_free else []
                t_xn = [R.dma("sync", xnT[:, kc * 1024:(kc + 1) * 1024],
                              io["xn1T_s"][kc * 128:(kc + 1) * 128, tc * 1024:(tc + 1) * 1024],
                              deps=dxn, sem=R.dma_sem("DxnT")) for kc in range(8)]
                t_z_evac = []
                t_h_evac = []
                for (wt, tws, bias_t, tbias, func, dest, evlist) in (
                        (wz_t, t_wz, bzc, t_bz, AF.Sigmoid, zf, t_z_evac),
                        (wh_t, t_wh, bhc, t_bh, AF.Identity, hf, t_h_evac)):
                    for dc in range(8):
                        pb = dc % 2
                        mm = None
                        for kc in range(8):
                            d0 = []
                            if kc == 0:
                                d0 = t_xn + tws + [t_pz_free[pb]]
                                if dest is zf and t_zf_free:
                                    d0.append(t_zf_free)
                                if dest is hf and t_hf_free:
                                    d0.append(t_hf_free)
                            for n in range(2):
                                mm = R.op("tensor", lambda e, wt=wt, dc=dc, kc=kc, pb=pb, n=n: e.matmul(
                                    ps_pz[pb][:, n * 512:(n + 1) * 512],
                                    wt[:, kc * D + dc * 128: kc * D + (dc + 1) * 128],
                                    xnT[:, kc * 1024 + n * 512: kc * 1024 + (n + 1) * 512],
                                    start=(kc == 0), stop=(kc == 7)),
                                    deps=d0 if (kc == 0 and n == 0) else [], inc=(kc == 7 and n == 1))
                        tev = R.op("scalar", lambda e, wt=wt, dc=dc, pb=pb, bias_t=bias_t, func=func, dest=dest: e.activation(
                            dest[:, dc * 1024:(dc + 1) * 1024], ps_pz[pb][:, :], func,
                            bias=bias_t[:, dc:dc + 1]),
                            deps=[mm] + tbias)
                        t_pz_free[pb] = tev
                        evlist.append(tev)
                t_xnT_free = mm
                tb = R.op("vector", lambda e: e.tensor_tensor(hf[:, :], zf[:, :], hf[:, :], ALU.mult),
                          deps=t_z_evac + t_h_evac)
                ta = R.op("vector", lambda e: e.tensor_scalar(zf[:, :], zf[:, :], -1.0, 1.0,
                                                              ALU.mult, ALU.add), deps=[tb])
                t_scans = []
                for dc in range(8):
                    dsc = [ta, tb, t_carry[dc]]
                    if t_hout_free:
                        dsc.append(t_hout_free)
                    tsc = R.op("vector", lambda e, dc=dc: e.tensor_tensor_scan(
                        hout[:, dc * 1024:(dc + 1) * 1024],
                        zf[:, dc * 1024:(dc + 1) * 1024],
                        hf[:, dc * 1024:(dc + 1) * 1024],
                        carry[:, dc:dc + 1] if tc > 0 else 0.0,
                        ALU.mult, ALU.add), deps=dsc)
                    t_scans.append(tsc)
                    if tc < 3:
                        t_carry[dc] = R.op("vector", lambda e, dc=dc: e.tensor_copy(
                            carry[:, dc:dc + 1],
                            hout[:, dc * 1024 + 1023: dc * 1024 + 1024]), deps=[tsc])
                t_zf_free = t_scans[-1]
                t_hf_free = t_scans[-1]
                ty_dc = []
                for dc in range(8):
                    sb_ = dc % 2
                    dsg = [t_sgt_free[sb_]] if t_sgt_free[sb_] else []
                    tsig = R.op("scalar", lambda e, dc=dc, sb_=sb_: e.activation(
                        sgt[sb_][:, :], hout[:, dc * 1024:(dc + 1) * 1024], AF.Sigmoid,
                        scale=betac[:, 0:1]),
                        deps=[t_scans[dc], t_beta] + dsg)
                    tyd = R.op("vector", lambda e, dc=dc, sb_=sb_: e.scalar_tensor_tensor(
                        hf[:, dc * 1024:(dc + 1) * 1024], hout[:, dc * 1024:(dc + 1) * 1024],
                        betac[:, 0:1], sgt[sb_][:, :], ALU.mult, ALU.mult),
                        deps=[tsig])
                    t_sgt_free[sb_] = tyd
                    ty_dc.append(tyd)
                ty = ty_dc[-1]
                t_hout_free = ty
                t_hf_free = None
                for ts_ in range(8):
                    yb = ts_ % 4
                    trs = []
                    for dc in range(8):
                        dtr = [ty_dc[dc], g["t_ident"]]
                        if t_pstr_free:
                            dtr.append(t_pstr_free)
                        trs.append(R.op("tensor", lambda e, dc=dc, ts_=ts_: e.transpose(
                            ps_trD[:, dc * 128:(dc + 1) * 128],
                            hf[:, dc * 1024 + ts_ * 128: dc * 1024 + (ts_ + 1) * 128],
                            g["ident_f32"][:, :]), deps=dtr))
                    dyt = [app_free[yb]] if app_free[yb] else []
                    tev = R.op("vector", lambda e, yb=yb: e.tensor_copy(ytok[yb][:, :], ps_trD[:, :]),
                               deps=trs + dyt)
                    t_pstr_free = tev
                    j = yb
                    dst = [app_free[j]] if app_free[j] else []
                    tsum = R.op("vector", lambda e, j=j, yb=yb: e.reduce_sum(
                        stf[:, j:j + 1], ytok[yb][:, :], axis=mybir.AxisListType.X),
                        deps=[tev] + dst)
                    dsq = [t_sqs_free] if t_sqs_free else []
                    tssq = R.op("scalar", lambda e, j=j, yb=yb: e.activation(
                        sqsD[:, :], ytok[yb][:, :], AF.Square, accum_out=stf[:, 4 + j:5 + j]),
                        deps=[tev] + dst + dsq)
                    t_sqs_free = tssq
                    tm = R.op("vector", lambda e, j=j: e.tensor_single_scalar(
                        stf[:, 8 + j:9 + j], stf[:, j:j + 1], 1.0 / D, ALU.mult), deps=[tsum])
                    te1 = R.op("vector", lambda e, j=j: e.tensor_single_scalar(
                        stf[:, 12 + j:13 + j], stf[:, 4 + j:5 + j], 1.0 / D, ALU.mult), deps=[tssq])
                    tmm = R.op("vector", lambda e, j=j: e.tensor_tensor(
                        stf[:, 16 + j:17 + j], stf[:, 8 + j:9 + j], stf[:, 8 + j:9 + j], ALU.mult),
                        deps=[tm])
                    tvar = R.op("vector", lambda e, j=j: e.tensor_tensor(
                        stf[:, 20 + j:21 + j], stf[:, 12 + j:13 + j], stf[:, 16 + j:17 + j],
                        ALU.subtract), deps=[te1, tmm])
                    vt4[j] = tvar
                    mt4[j] = tm
                    if ts_ == 7:
                        t_hf_free = trs[-1]
                    if j < 3:
                        continue
                    eps_tile, eps_tick = epsD
                    tsd = R.op("scalar", lambda e: e.activation(
                        stf[:, 24:28], stf[:, 20:24], AF.Sqrt, bias=eps_tile[:, 0:1]),
                        deps=vt4 + [eps_tick])
                    trc = R.op("vector", lambda e: e.reciprocal(stf[:, 24:28], stf[:, 24:28]),
                               deps=[tsd])
                    tnm = R.op("vector", lambda e: e.scalar_tensor_tensor(
                        stf[:, 28:32], stf[:, 8:12], -1.0, stf[:, 24:28], ALU.mult, ALU.mult),
                        deps=[trc] + mt4)
                    for jj in range(4):
                        tapp = R.op("scalar", lambda e, jj=jj: e.activation(
                            ytok[jj][:, :], ytok[jj][:, :], AF.Identity,
                            bias=stf[:, 28 + jj:29 + jj], scale=stf[:, 24 + jj:25 + jj]),
                            deps=[tnm, trc])
                        tgm = R.op("vector", lambda e, jj=jj: e.tensor_tensor(
                            ytok[jj][:, :], ytok[jj][:, :], ggb[:, :], ALU.mult),
                            deps=[tapp, t_ggb])
                        sb2 = s1ctr[0] % 2
                        s1ctr[0] += 1
                        ds1 = [s1b_free[sb2]] if s1b_free[sb2] else []
                        tbb = R.op("vector", lambda e, jj=jj, sb2=sb2: e.tensor_tensor(
                            s1b[sb2][:, :], ytok[jj][:, :], gbb[:, :], ALU.add),
                            deps=[tgm, t_gbb] + ds1)
                        app_free[jj] = tbb
                        gch = tc * 8 + (ts_ - 3) + jj
                        s1b_free[sb2] = R.dma("sync", io["s1_s"][gch * 128:(gch + 1) * 128, :],
                                              s1b[sb2][:, :], deps=[tbb], sem=R.dma_sem(f"Ds1{sb2}"))
            R.wait("sync", R.drain_ticks())


def emit_phaseE(R, io, g):
    nc = R.nc
    NRING = 9
    with contextlib.ExitStack() as esE:
        with R.phase("E"):
            ensure_identity(R, g)
            m1g = [esE.enter_context(nc.sbuf_tensor(f"m1gE{i}", [128, 4096], BF16)) for i in range(2)]
            w2g = [esE.enter_context(nc.sbuf_tensor(f"w2gE{i}", [128, 32 * 128], BF16)) for i in range(2)]
            h1 = esE.enter_context(nc.sbuf_tensor("h1E", [128, 32 * 512], BF16))
            accT = esE.enter_context(nc.sbuf_tensor("accTE", [128, 8 * 512], F32))
            xnT = esE.enter_context(nc.sbuf_tensor("xnTE", [128, 8 * 512], BF16))
            wbc = esE.enter_context(nc.sbuf_tensor("wbcE", [128, 8 * 512], BF16))
            g1f = esE.enter_context(nc.sbuf_tensor("g1fE", [128, 8 * 512], BF16))
            g1tok = esE.enter_context(nc.sbuf_tensor("g1tokE", [128, 4 * 1024], BF16))
            b1c = esE.enter_context(nc.sbuf_tensor("b1cE", [128, 8 * 32], F32))
            b2r = esE.enter_context(nc.sbuf_tensor("b2rE", [1, D], BF16))
            wrow = esE.enter_context(nc.sbuf_tensor("wrowE", [1, NE * 512], BF16))
            x1t = [esE.enter_context(nc.sbuf_tensor(f"x1tE{i}", [128, D], F32)) for i in range(2)]
            s1t = [esE.enter_context(nc.sbuf_tensor(f"s1tE{i}", [128, D], BF16)) for i in range(2)]
            outt = [esE.enter_context(nc.sbuf_tensor(f"outtE{i}", [128, D], F32)) for i in range(2)]
            ps_h = [esE.enter_context(nc.psum_tensor(f"ps_hE{i}", [128, 512], F32)) for i in range(2)]
            ps_z = [esE.enter_context(nc.psum_tensor(f"ps_zE{i}", [128, 512], F32)) for i in range(2)]
            ps_tr = esE.enter_context(nc.psum_tensor("ps_trE", [128, 1024], BF16))

            ses = R.dma_sem("Eset")
            t_b1 = []
            for e_ in range(NE):
                t_b1 += col_tile_load(R, b1c[:, e_ * 32:(e_ + 1) * 32], io["mb1"][e_], 32, sem=ses)
            t_b2_free = None

            t_m1g_free = [None, None]
            t_w2g_free = [None, None]
            t_xnT_free = None
            t_wbc_free = None
            t_h1_free = None
            t_ps_h_free = [None, None]
            t_ps_z_free = [None, None]
            t_acc_free = None
            t_g1f_free = None
            t_pstr_free = None
            t_g1tok_free = None
            t_x1_free = [None, None]
            t_s1_free = [None, None]
            t_out_free = [None, None]
            unit = 0
            for ti in range(8):
                dxn = [t_xnT_free] if t_xnT_free else []
                t_xn = [R.dma("gpsimd", xnT[:, kc * 512:(kc + 1) * 512],
                              io["xn1T_s"][kc * 128:(kc + 1) * 128, ti * 512:(ti + 1) * 512],
                              deps=dxn, sem=R.dma_sem("ExnT")) for kc in range(8)]
                dwb = [t_wbc_free] if t_wbc_free else []
                t_wb = []
                t_wr = []
                for e_ in range(NE):
                    wslice = io["w_rows_s"][e_][ti * 512:(ti + 1) * 512]
                    srcap = bass.AP(wslice.tensor, wslice.offset, [[0, 128], [1, 512]])
                    t_wb.append(R.dma("sync", wbc[:, e_ * 512:(e_ + 1) * 512], srcap,
                                      deps=dwb, sem=R.dma_sem("Ewbc")))
                    t_wr.append(R.dma("sync", wrow[0:1, e_ * 512:(e_ + 1) * 512],
                                      wslice.rearrange("(o n) -> o n", o=1), deps=dwb,
                                      sem=R.dma_sem("Ewbc")))
                t_acc_parts = [None] * 8
                for e_ in range(NE):
                    db2 = [t_b2_free] if t_b2_free else []
                    t_b2 = R.dma("gpsimd", b2r[0:1, :], io["mb2"][e_].rearrange("(o n) -> o n", o=1),
                                 deps=db2, sem=R.dma_sem("Eb2"))
                    t_h_parts = []
                    tgl = [None] * 8
                    for g8 in range(2):
                        gb = g8 % 2
                        dg = [t_m1g_free[gb]] if t_m1g_free[gb] else []
                        tgl[g8] = R.dma("gpsimd", m1g[gb][:, :], io["m1g_s"][e_, g8], deps=dg,
                                        sem=R.dma_sem(f"Em1g{gb}"))
                    for g8 in range(8):
                        gb = g8 % 2
                        for fi in range(4):
                            f = 4 * g8 + fi
                            pb = f % 2
                            mm = None
                            for kc in range(8):
                                d0 = []
                                if kc == 0:
                                    d0 = [tgl[g8]] + t_xn
                                    if t_ps_h_free[pb]:
                                        d0.append(t_ps_h_free[pb])
                                    if f < 2 and t_h1_free:
                                        d0.append(t_h1_free)
                                mm = R.op("tensor", lambda e, fi=fi, kc=kc, pb=pb, gb=gb: e.matmul(
                                    ps_h[pb][:, :],
                                    m1g[gb][:, (fi * 8 + kc) * 128:(fi * 8 + kc + 1) * 128],
                                    xnT[:, kc * 512:(kc + 1) * 512],
                                    start=(kc == 0), stop=(kc == 7)),
                                    deps=d0, inc=(kc == 7))
                            if fi == 3:
                                t_m1g_free[gb] = mm
                                if g8 + 2 < 8:
                                    dg = [mm]
                                    tgl[g8 + 2] = R.dma("gpsimd", m1g[gb][:, :],
                                                        io["m1g_s"][e_, g8 + 2], deps=dg,
                                                        sem=R.dma_sem(f"Em1g{gb}"))
                            tev = R.op("scalar", lambda e, f=f, pb=pb, e_=e_: e.activation(
                                h1[:, f * 512:(f + 1) * 512], ps_h[pb][:, :], AF.Relu,
                                bias=b1c[:, e_ * 32 + f:e_ * 32 + f + 1]),
                                deps=[mm] + (t_b1 if unit == 0 else []))
                            t_ps_h_free[pb] = tev
                            tmk = R.op("vector", lambda e, f=f, e_=e_: e.tensor_tensor(
                                h1[:, f * 512:(f + 1) * 512], h1[:, f * 512:(f + 1) * 512],
                                wbc[:, e_ * 512:(e_ + 1) * 512], ALU.mult),
                                deps=[tev, t_wb[e_]])
                            t_h_parts.append(tmk)
                    t_h1_free = t_h_parts[-1]
                    for dc in range(8):
                        pb = dc % 2
                        gb = dc % 2
                        dgw = [t_w2g_free[gb]] if t_w2g_free[gb] else []
                        tw2 = R.dma("scalar", w2g[gb][:, :], io["m2g_s"][e_, dc], deps=dgw,
                                    sem=R.dma_sem(f"Ew2_{gb}"))
                        mm = None
                        for f in range(32):
                            d0 = [t_h_parts[f]]
                            if f == 0:
                                d0.append(tw2)
                                if t_ps_z_free[pb]:
                                    d0.append(t_ps_z_free[pb])
                            mm = R.op("tensor", lambda e, f=f, pb=pb, gb=gb: e.matmul(
                                ps_z[pb][:, :],
                                w2g[gb][:, f * 128:(f + 1) * 128],
                                h1[:, f * 512:(f + 1) * 512],
                                start=(f == 0), stop=False),
                                deps=d0, inc=False)
                        t_w2g_free[gb] = R.op("tensor", lambda e, e_=e_, dc=dc, pb=pb: e.matmul(
                            ps_z[pb][:, :],
                            b2r[0:1, dc * 128:(dc + 1) * 128],
                            wrow[0:1, e_ * 512:(e_ + 1) * 512],
                            start=False, stop=True),
                            deps=[t_b2] + t_wr, inc=True)
                        mm = t_w2g_free[gb]
                        if dc == 7:
                            t_b2_free = mm
                        if e_ == 0:
                            dacc = [mm] + ([t_acc_free] if t_acc_free else [])
                            tac = R.op("vector", lambda e, dc=dc, pb=pb: e.tensor_copy(
                                accT[:, dc * 512:(dc + 1) * 512], ps_z[pb][:, :]), deps=dacc)
                        else:
                            tac = R.op("vector", lambda e, dc=dc, pb=pb: e.tensor_tensor(
                                accT[:, dc * 512:(dc + 1) * 512], accT[:, dc * 512:(dc + 1) * 512],
                                ps_z[pb][:, :], ALU.add), deps=[mm, t_acc_parts[dc]])
                        t_ps_z_free[pb] = tac
                        t_acc_parts[dc] = tac
                    unit += 1
                dg1 = [t_g1f_free] if t_g1f_free else []
                tg1 = R.op("scalar", lambda e: e.activation(g1f[:, :], accT[:, :], AF.Silu),
                           deps=t_acc_parts + dg1)
                t_acc_free = tg1
                for c in range(4):
                    trs = []
                    for dc in range(8):
                        dtr = [tg1, g["t_ident"]]
                        if t_pstr_free:
                            dtr.append(t_pstr_free)
                        trs.append(R.op("tensor", lambda e, dc=dc, c=c: e.transpose(
                            ps_tr[:, dc * 128:(dc + 1) * 128],
                            g1f[:, dc * 512 + c * 128: dc * 512 + (c + 1) * 128],
                            g["ident_bf16"][:, :]), deps=dtr))
                    dgt = [t_g1tok_free] if (c == 0 and t_g1tok_free) else []
                    tev = R.op("vector", lambda e, c=c: e.tensor_copy(
                        g1tok[:, c * 1024:(c + 1) * 1024], ps_tr[:, :]), deps=trs + dgt)
                    t_pstr_free = tev
                    if c == 3:
                        t_g1f_free = trs[-1]
                    gch = ti * 4 + c
                    xb = gch % 2
                    dx1 = [t_x1_free[xb]] if t_x1_free[xb] else []
                    tx1 = R.dma("sync", x1t[xb][:, :], io["x1_s"][gch * 128:(gch + 1) * 128, :],
                                deps=dx1, sem=R.dma_sem(f"Ex1{xb}"))
                    ds1 = [t_s1_free[xb]] if t_s1_free[xb] else []
                    ts1 = R.dma("sync", s1t[xb][:, :], io["s1_s"][gch * 128:(gch + 1) * 128, :],
                                deps=ds1, sem=R.dma_sem(f"Es1{xb}"))
                    dsg = [t_out_free[xb]] if t_out_free[xb] else []
                    tsg = R.op("vector", lambda e, xb=xb, c=c: e.tensor_tensor(
                        outt[xb][:, :], s1t[xb][:, :], g1tok[:, c * 1024:(c + 1) * 1024], ALU.mult),
                        deps=[ts1, tev] + dsg)
                    t_s1_free[xb] = tsg
                    if c == 3:
                        t_g1tok_free = tsg
                    tout = R.op("vector", lambda e, xb=xb: e.tensor_tensor(
                        outt[xb][:, :], outt[xb][:, :], x1t[xb][:, :], ALU.add),
                        deps=[tsg, tx1])
                    t_x1_free[xb] = tout
                    t_out_free[xb] = R.dma("sync", io["out"][gch * 128:(gch + 1) * 128, :],
                                           outt[xb][:, :], deps=[tout], sem=R.dma_sem(f"Eout{xb}"))
                t_xnT_free = mm
                t_wbc_free = t_h_parts[-1]

            R.wait("sync", R.drain_ticks())


EMITTERS.update({"A": emit_phaseA, "B": emit_phaseB, "C": emit_phaseC, "D": emit_phaseD, "E": emit_phaseE})
